# version 11
# speedup vs baseline: 1.7711x; 1.0527x over previous
import numpy as np
import ml_dtypes
from concourse import library_config as _libcfg
from concourse import library_overlay as _libov
from contextlib import ExitStack
import concourse.bass as bass
import concourse.mybir as mybir
from concourse.bass_utils import run_bass_kernel_spmd

F32 = mybir.dt.float32
BF16 = mybir.dt.bfloat16
AF = mybir.ActivationFunctionType
ALU = mybir.AluOpType
AX = mybir.AxisListType
I32 = mybir.dt.int32
I16 = mybir.dt.int16
bf16 = ml_dtypes.bfloat16

D = 1024
TC = 2048
NT = TC // 128
ND = D // 128
H = 16
DH = 64
NE = 16
EH = 512
EPS_LN = 1e-5
EPS_A = 1e-6
SPARSE_F1 = False
SPARSE_F2 = False
CAP = 640
NBLK = 5
LN0125 = float(np.log(0.125))

ENGS = ("sy", "gp", "te", "ve", "sc")


class Sem:
    def __init__(self, stack, nc, name):
        self.h = stack.enter_context(nc.semaphore(name))
        self.n = 0


class B:

    def __init__(self, nc, stack):
        self.nc = nc
        self.stack = stack
        self.q = {e: [] for e in ENGS}
        self.sems = {}
        self.selfsem = {e: self.sem("fl_" + e) for e in ENGS}

    def sem(self, name):
        if name not in self.sems:
            self.sems[name] = Sem(self.stack, self.nc, name)
        return self.sems[name]

    def sbuf(self, name, shape, dt, stack=None):
        return (stack or self.stack).enter_context(
            self.nc.sbuf_tensor(name, shape, dt))

    def psum(self, name, shape, dt=F32, stack=None):
        return (stack or self.stack).enter_context(
            self.nc.psum_tensor(name, shape, dt))

    def op(self, eng, fn, inc=None, dinc=None):
        incs = []
        if inc is not None:
            s, k = inc if isinstance(inc, tuple) else (inc, 1)
            s.n += k
            incs.append((s.h, k))
        if dinc is not None:
            dinc.n += 16
            incs.append((dinc.h, 16))

        def play(e, fn=fn, incs=incs):
            ins = fn(e)
            for h, k in incs:
                ins.then_inc(h, k)
        self.q[eng].append(play)

    def wait(self, eng, sem, val=None):
        v = sem.n if val is None else val
        if v > 0:
            self.q[eng].append(lambda e, h=sem.h, v=v: e.wait_ge(h, v))

    def opf(self, eng, fn):
        s = self.selfsem[eng]
        self.op(eng, fn, inc=s)
        self.wait(eng, s)

    def op_branch(self, eng, reg, fn_slot0, fn_slot1, inc=None, dinc=None):
        incs = []
        if inc is not None:
            s, k = inc if isinstance(inc, tuple) else (inc, 1)
            s.n += k
            incs.append((s.h, k))
        if dinc is not None:
            dinc.n += 16
            incs.append((dinc.h, 16))

        def play(e, fn0=fn_slot0, fn1=fn_slot1, incs=incs, reg=reg):
            with e.If_eq(reg, 0):
                ins = fn0(e)
                for h, k in incs:
                    ins.then_inc(h, k)
            with e.Else():
                ins = fn1(e)
                for h, k in incs:
                    ins.then_inc(h, k)
        self.q[eng].append(play)

    def play(self, block):
        for eng, meth in (("sy", block.sync), ("gp", block.gpsimd),
                          ("te", block.tensor), ("ve", block.vector),
                          ("sc", block.scalar)):
            ops = self.q[eng]
            if ops:
                meth(lambda e, ops=ops: [f(e) for f in ops])



def _recip16(b, v, out, in_):
    with b.nc.allow_low_precision(reason="bf16 rden for fast replicate MM"):
        return v.reciprocal(out, in_)


def _host_consts():
    eye = np.eye(128, dtype=bf16)
    onesf = np.ones((128, 64), dtype=np.float32)
    ones_bf = np.ones((1, 512), dtype=bf16)
    erep = np.zeros((16, 16 * 128), dtype=bf16)
    for e in range(16):
        erep[e, e * 128:(e + 1) * 128] = 1
    hm2 = np.zeros((128, 2), dtype=np.float32)
    hm2[0:64, 0] = 1.0
    hm2[64:128, 1] = 1.0
    misc = np.zeros((128, 4), dtype=np.float32)
    misc[:, 0] = EPS_LN
    misc[:, 1] = LN0125
    misc[:, 2] = EPS_A
    triu = np.triu(np.ones((128, 128), dtype=np.float32)).astype(bf16)
    o128 = np.ones((128, 128), dtype=bf16)
    sloti = np.broadcast_to(np.arange(CAP, dtype=np.float32)[None, :],
                            (128, CAP)).copy()
    hl3 = np.zeros((128, 48), dtype=np.float32)
    for k in range(16):
        hl3[:, 3 * k + 1] = 128.0 * k
        hl3[:, 3 * k + 2] = np.arange(128) + 1.0
    return dict(c_eye=eye, c_onesf=onesf, c_ones=ones_bf, c_erep=erep,
                c_hm2=hm2, c_misc=misc, c_triu=triu, c_o128=o128,
                c_sloti=sloti, c_hl3=hl3.astype(bf16))


CONST_SHAPES = {"c_eye": ([128, 128], BF16), "c_onesf": ([128, 64], F32),
                "c_ones": ([1, 512], BF16), "c_erep": ([16, 2048], BF16),
                "c_hm2": ([128, 2], F32), "c_misc": ([128, 4], F32),
                "c_triu": ([128, 128], BF16), "c_o128": ([128, 128], BF16),
                "c_sloti": ([128, CAP], F32), "c_hl3": ([128, 48], BF16)}


def decl_consts(b, names):
    out = {}
    for n in names:
        shape, dt = CONST_SHAPES[n]
        d = b.nc.dram_tensor(n, shape, dt, kind="ExternalInput").ap()
        sb = b.sbuf(n + "_sb", shape, dt)
        out[n] = sb
        b.op("sy", lambda e, sb=sb, d=d: e.dma_start(out=sb[:, :], in_=d),
             dinc=b.sem("s_ld"))
    return out


def load_x(b, x_d, x_sb):
    b.op("sy", lambda e: e.dma_start(
        out=x_sb[:, :].rearrange("p (t d) -> p t d", t=NT),
        in_=x_d.rearrange("(t p) d -> p t d", p=128)), dinc=b.sem("s_ld"))


def _ap(x):
    return x if isinstance(x, bass.AP) else x.ap()


def load_weight(b, d_ap, sb, nblk, sem):
    b.op("gp", lambda e: e.dma_start(
        out=_ap(sb).rearrange("p (k c) -> p k c", k=nblk),
        in_=d_ap.rearrange("(k p) c -> p k c", p=128)), dinc=sem)



def emit_ln(b, tag, x_sb, xt, eye_sb, cb, x_ready, mask_sb=None,
            lnH=None, stage_tok=None):
    sv = b.sem(tag + "v")
    ss = b.sem(tag + "s")
    sp = b.sem(tag + "p")
    se = b.sem(tag + "e")
    sd = b.sem(tag + "d")
    with ExitStack() as ps:
        stats = b.sbuf(tag + "_st", [128, 20], F32, stack=ps)
        lnst = b.sbuf(tag + "_ln", [128, 2 * 1024], BF16, stack=ps)
        lnf = (b.sbuf(tag + "_lf", [128, 2 * 1024], F32, stack=ps)
               if lnH is not None else None)
        ptrs = [b.psum(tag + f"_ptr{i}", [128, 128], BF16, stack=ps)
                for i in range(2)]

        b.wait("ve", *x_ready)
        b.wait("sc", *x_ready)
        b.wait("te", *x_ready)
        tr_done = {}
        ev_done = {}
        lnf_war = {}
        for t in range(NT):
            lo = (t % 2) * 1024
            xt_ap = x_sb[:, t * 1024:(t + 1) * 1024]
            b.op("ve", lambda v, a=xt_ap: v.bn_stats(stats[:, 0:6],
                                                     a[:, 0:512]))
            b.opf("ve", lambda v, a=xt_ap: v.bn_stats(stats[:, 6:12],
                                                      a[:, 512:1024]))
            b.op("ve", lambda v: v.bn_aggr(stats[:, 12:14], stats[:, 0:12]),
                 inc=sv)
            b.wait("sc", sv)
            b.opf("sc", lambda s: s.activation(
                stats[:, 14:15], stats[:, 13:14], AF.Ln, bias=cb[:, 0:1]))
            b.op("sc", lambda s: s.activation(
                stats[:, 15:16], stats[:, 14:15], AF.Exp, scale=-0.5), inc=ss)
            b.wait("ve", ss)
            b.op("ve", lambda v: v.scalar_tensor_tensor(
                stats[:, 16:17], stats[:, 12:13], -1.0, stats[:, 15:16],
                ALU.mult, ALU.mult), inc=sv)
            if (t % 2) in tr_done:
                b.wait("sc", sp, tr_done[t % 2])
            b.wait("sc", sv)
            if lnH is None:
                b.op("sc", lambda s, a=xt_ap, lo=lo: s.activation(
                    lnst[:, lo:lo + 1024], a, AF.Identity,
                    bias=stats[:, 16:17], scale=stats[:, 15:16]), inc=ss)
            else:
                if (t % 2) in lnf_war:
                    b.wait("sc", sd, lnf_war[t % 2])
                b.op("sc", lambda s, a=xt_ap, lo=lo: s.activation(
                    lnf[:, lo:lo + 1024], a, AF.Identity,
                    bias=stats[:, 16:17], scale=stats[:, 15:16]), inc=ss)
                b.wait("ve", ss)
                if (t % 2) in tr_done:
                    b.wait("ve", sp, tr_done[t % 2])
                b.op("ve", lambda v, lo=lo: v.tensor_copy(
                    lnst[:, lo:lo + 1024], lnf[:, lo:lo + 1024]), inc=sv)
                b.wait("sy", ss)
                b.op("sy", lambda e, t=t, lo=lo: e.dma_start(
                    out=lnH[t * 128:(t + 1) * 128, :],
                    in_=lnf[:, lo:lo + 1024]), dinc=sd)
                lnf_war[t % 2] = sd.n
                b.wait("te", sv)
            if mask_sb is not None:
                b.wait("ve", ss)
                b.op("ve", lambda v, t=t, lo=lo: v.tensor_single_scalar(
                    mask_sb[:, t:t + 1], lnst[:, lo:lo + 1], 0.0,
                    ALU.not_equal), inc=sv)
            b.wait("te", ss)
            for dc in range(ND):
                par = dc % 2
                if par in ev_done:
                    b.wait("te", se, ev_done[par])
                b.op("te", lambda te, par=par, lo=lo, dc=dc: te.transpose(
                    ptrs[par][:, :], lnst[:, lo + dc * 128:lo + (dc + 1) * 128],
                    eye_sb[:, :]), inc=sp)
                b.wait("ve", sp)
                b.op("ve", lambda v, par=par, dc=dc, t=t: v.tensor_copy(
                    xt[:, dc * 2048 + t * 128:dc * 2048 + (t + 1) * 128],
                    ptrs[par][:, :]), inc=se)
                ev_done[par] = se.n
            tr_done[t % 2] = sp.n
        if stage_tok is not None:
            stage_tok.append((sd, sd.n))
    return se, se.n



def emit_attn_a_kv(b, xt, wk_sb, wv_sb, xt_ready, w_ready, kvp_d):
    sp = b.sem("akv_p")
    sv = b.sem("akv_v")
    ss = b.sem("akv_s")
    skv = b.sem("akv_kv")
    sst = b.sem("akv_st")
    so = b.sem("akv_ones")

    kf = b.sbuf("kf", [128, 2 * 1024], BF16)
    v65 = b.sbuf("v65", [128, 2 * 1040], BF16)
    phs = b.sbuf("phs", [128, 2 * 1024], F32)
    kvs = b.sbuf("kvs", [64, 1040], F32)
    pk = [b.psum(f"pk{i}", [128, 512]) for i in range(2)]
    pv = [b.psum(f"pvv{i}", [128, 512]) for i in range(2)]
    pkv = [b.psum("pkv0", [64, 455]), b.psum("pkv1", [64, 455]),
           b.psum("pkv2", [64, 130])]

    for sl in range(2):
        b.op("gp", lambda e, sl=sl: e.memset(
            bass.AP(v65, sl * 1040 + 64, [[2 * 1040, 128], [65, 16]]), 1.0),
            inc=so)

    b.wait("te", *xt_ready)
    b.wait("te", *w_ready)
    b.wait("ve", so)
    last_k_use = [0, 0]
    last_v_use = [0, 0]
    last_kf_use = 0
    for t in range(NT):
        sl = t % 2
        if t >= 2:
            b.wait("ve", skv, t - 1)
        for dn in range(2):
            b.wait("te", sv, last_k_use[dn])
            for kdc in range(ND):
                b.op("te", lambda te, dn=dn, kdc=kdc, t=t: te.matmul(
                    pk[dn][:, :],
                    xt[:, kdc * 2048 + t * 128:kdc * 2048 + (t + 1) * 128],
                    wk_sb[:, kdc * 1024 + dn * 512:kdc * 1024 + (dn + 1) * 512],
                    start=(kdc == 0), stop=(kdc == ND - 1)),
                    inc=(sp if kdc == ND - 1 else None))
            b.wait("ve", sp)
            b.op("ve", lambda v, dn=dn, sl=sl: v.tensor_single_scalar(
                phs[:, sl * 1024 + dn * 512:sl * 1024 + (dn + 1) * 512],
                pk[dn][:, :], 0.0, ALU.min), inc=sv)
            b.wait("sc", sv)
            b.op("sc", lambda s, dn=dn, sl=sl: s.activation(
                phs[:, sl * 1024 + dn * 512:sl * 1024 + (dn + 1) * 512],
                phs[:, sl * 1024 + dn * 512:sl * 1024 + (dn + 1) * 512],
                AF.Exp), inc=ss)
            b.wait("ve", ss)
            b.op("ve", lambda v, dn=dn, sl=sl: v.scalar_tensor_tensor(
                kf[:, sl * 1024 + dn * 512:sl * 1024 + (dn + 1) * 512],
                pk[dn][:, :], 0.0,
                phs[:, sl * 1024 + dn * 512:sl * 1024 + (dn + 1) * 512],
                ALU.max, ALU.add), inc=sv)
            last_k_use[dn] = sv.n
        for dn in range(2):
            b.wait("te", sv, last_v_use[dn])
            for kdc in range(ND):
                b.op("te", lambda te, dn=dn, kdc=kdc, t=t: te.matmul(
                    pv[dn][:, :],
                    xt[:, kdc * 2048 + t * 128:kdc * 2048 + (t + 1) * 128],
                    wv_sb[:, kdc * 1024 + dn * 512:kdc * 1024 + (dn + 1) * 512],
                    start=(kdc == 0), stop=(kdc == ND - 1)),
                    inc=(sp if kdc == ND - 1 else None))
            b.wait("ve", sp)
            b.op("ve", lambda v, dn=dn, sl=sl: v.tensor_copy(
                bass.AP(v65, sl * 1040 + dn * 8 * 65,
                        [[2 * 1040, 128], [65, 8], [1, 64]]),
                bass.AP(pv[dn], 0, [[512, 128], [64, 8], [1, 64]])), inc=sv)
            last_v_use[dn] = sv.n
        b.wait("te", sv)
        for h in range(H):
            bank, slot = divmod(h, 7)
            b.op("te", lambda te, h=h, sl=sl, bank=bank, slot=slot, t=t:
                 te.matmul(
                     pkv[bank][:, slot * 65:(slot + 1) * 65],
                     kf[:, sl * 1024 + h * 64:sl * 1024 + (h + 1) * 64],
                     v65[:, sl * 1040 + h * 65:sl * 1040 + (h + 1) * 65],
                     start=(t == 0 and slot == 0), stop=(t == NT - 1)),
                 inc=(skv if h == H - 1 else None))
    b.wait("ve", skv)
    b.op("ve", lambda v: v.tensor_copy(kvs[:, 0:455], pkv[0][:, :]))
    b.op("ve", lambda v: v.tensor_copy(kvs[:, 455:910], pkv[1][:, :]))
    b.op("ve", lambda v: v.tensor_copy(kvs[:, 910:1040], pkv[2][:, :]),
         inc=sv)
    b.wait("sy", sv)
    b.op("sy", lambda e: e.dma_start(out=kvp_d, in_=kvs[:, :]), dinc=sst)
    b.wait("sy", sst)


def build_launch_a():
    nc = bass.Bass("TRN2", target_bir_lowering=False)
    with ExitStack() as stack:
        b = B(nc, stack)
        x_d = nc.dram_tensor("x", [TC, D], F32, kind="ExternalInput").ap()
        wk_d = nc.dram_tensor("wk", [D, D], F32, kind="ExternalInput").ap()
        wv_d = nc.dram_tensor("wv", [D, D], F32, kind="ExternalInput").ap()
        kvp_d = nc.dram_tensor("kvp", [64, 1040], F32,
                               kind="ExternalOutput").ap()

        consts = decl_consts(b, ["c_eye", "c_misc"])
        x_sb = b.sbuf("x_sbuf", [128, NT * 1024], F32)
        xt = b.sbuf("xt_sbuf", [128, ND * 2048], BF16)
        wk_sb = b.sbuf("wk_sb", [128, 8192], BF16)
        wv_sb = b.sbuf("wv_sb", [128, 8192], BF16)
        sw = b.sem("s_w")
        load_weight(b, wk_d, wk_sb, 8, sw)
        load_weight(b, wv_d, wv_sb, 8, sw)
        load_x(b, x_d, x_sb)
        sld = b.sem("s_ld")

        xt_ready = emit_ln(b, "ln1", x_sb, xt, consts["c_eye"],
                           consts["c_misc"], (sld, sld.n))
        emit_attn_a_kv(b, xt, wk_sb, wv_sb, xt_ready, (sw, sw.n), kvp_d)

        block = stack.enter_context(nc.Block())
        b.play(block)
    return nc



def emit_kv_sum(b, kvp_all_d, kvb2):
    sa = b.sem("kvs_a")
    sv = b.sem("kvs_v")
    sd = b.sem("kvs_d")
    kvs = b.sbuf("kvs_f", [64, 1040], F32)
    for r in range(4):
        if r > 0:
            b.wait("gp", sa)
        b.op("gp", lambda e, r=r: e.dma_start(
            out=kvs[:, :], in_=kvp_all_d[r * 64:(r + 1) * 64, :],
            accum_op=(ALU.add if r > 0 else ALU.bypass)), dinc=sa)
    b.wait("ve", sa)
    b.op("ve", lambda v: v.tensor_copy(kvb2[0:64, :], kvs[:, :]), inc=sv)
    b.wait("sy", sv)
    b.op("sy", lambda e: e.dma_start(out=kvb2[64:128, :], in_=kvb2[0:64, :]),
         dinc=sd)
    return sd, sd.n


def emit_attn_a_main(b, xt, x_sb, wq_sb, wo_sb, kvb2, onesf, onesb, xt_ready,
                     kv_ready, wq_ready, wo_ready, tag="am"):
    sp = b.sem(tag + "_p")
    sv = b.sem(tag + "_v")
    ss = b.sem(tag + "_s")
    sd = b.sem(tag + "_d")
    with ExitStack() as ps:
        at = b.sbuf(tag + "_at_sb", [128, 16384], BF16, stack=ps)
        qf = b.sbuf(tag + "_qf", [128, 2 * 512], BF16, stack=ps)
        phs = b.sbuf(tag + "_am_phs", [128, 2 * 512], F32, stack=ps)
        asm = b.sbuf(tag + "_am_asm", [128, 2048], F32, stack=ps)
        rrec = b.sbuf(tag + "_am_rrec", [1, 1024], BF16, stack=ps)
        atst = b.sbuf(tag + "_am_atst", [64, 2 * 512], BF16, stack=ps)
        pq = [b.psum(tag + f"_pq{i}", [128, 512], stack=ps) for i in range(2)]
        pnd = [b.psum(tag + f"_pnd{i}", [65, 512], stack=ps) for i in range(2)]
        prep = [b.psum(tag + f"_prep{i}", [64, 512], stack=ps) for i in range(2)]
        po = [b.psum(tag + f"_po{i}", [128, 512], stack=ps) for i in range(2)]

        b.wait("te", *xt_ready)
        b.wait("te", *kv_ready)
        b.wait("te", *wq_ready)
        tok_pq = [0, 0]
        tok_qf = [0, 0]
        tok_pnd = [0, 0]
        tok_prep = [0, 0]
        tok_atst = [0, 0]
        tok_po = [0, 0]
        tok_asm = [0, 0]
        hidx = 0
        for qc in range(4):
            for dc in range(ND):
                par = (qc * ND + dc) % 2
                b.wait("te", sv, tok_pq[par])
                for kdc in range(ND):
                    b.op("te", lambda te, kdc=kdc, dc=dc, qc=qc, par=par:
                         te.matmul(
                             pq[par][:, :],
                             wq_sb[:, kdc * 1024 + dc * 128:kdc * 1024 + (dc + 1) * 128],
                             xt[:, kdc * 2048 + qc * 512:kdc * 2048 + (qc + 1) * 512],
                             start=(kdc == 0), stop=(kdc == ND - 1)),
                         inc=(sp if kdc == ND - 1 else None))
                b.wait("ve", sp)
                b.wait("ve", sp, tok_qf[par])
                b.op("ve", lambda v, par=par: v.tensor_single_scalar(
                    phs[:, par * 512:(par + 1) * 512], pq[par][:, :], 0.0,
                    ALU.min), inc=sv)
                b.wait("sc", sv)
                b.op("sc", lambda s, par=par: s.activation(
                    phs[:, par * 512:(par + 1) * 512],
                    phs[:, par * 512:(par + 1) * 512], AF.Exp), inc=ss)
                b.wait("ve", ss)
                b.op("ve", lambda v, par=par: v.scalar_tensor_tensor(
                    qf[:, par * 512:(par + 1) * 512], pq[par][:, :], 0.0,
                    phs[:, par * 512:(par + 1) * 512], ALU.max, ALU.add),
                    inc=sv)
                tok_pq[par] = sv.n
                qf_sv = sv.n
                for j in range(2):
                    h = 2 * dc + j
                    hp = hidx % 2
                    hidx += 1
                    b.wait("te", sv, tok_pnd[hp])
                    b.wait("te", sv, qf_sv)
                    b.op("te", lambda te, h=h, j=j, par=par, hp=hp: te.matmul(
                        pnd[hp][:, :],
                        kvb2[j * 64:(j + 1) * 64, h * 65:(h + 1) * 65],
                        qf[j * 64:(j + 1) * 64, par * 512:(par + 1) * 512],
                        start=True, stop=True), inc=sp)
                    tok_qf[par] = sp.n
                    b.wait("ve", sp)
                    b.wait("ve", sp, tok_asm[hp])
                    b.opf("ve", lambda v, hp=hp: v.tensor_single_scalar(
                        asm[64:65, hp * 1024:hp * 1024 + 512],
                        pnd[hp][64:65, :], EPS_A, ALU.add))
                    b.op("ve", lambda v, hp=hp: _recip16(b, v,
                        rrec[0:1, hp * 512:(hp + 1) * 512],
                        asm[64:65, hp * 1024:hp * 1024 + 512]), inc=sv)
                    b.wait("te", sv)
                    b.wait("te", ss, tok_prep[hp])
                    b.op("te", lambda te, hp=hp: te.matmul(
                        prep[hp][:, :], onesb[0:1, 0:64],
                        rrec[0:1, hp * 512:(hp + 1) * 512],
                        start=True, stop=True), inc=sp)
                    tok_asm[hp] = sp.n
                    b.wait("sc", sp)
                    b.wait("sc", sv, tok_pnd[hp])
                    b.op("sc", lambda s, hp=hp: s.copy(
                        asm[0:64, hp * 1024 + 512:hp * 1024 + 1024],
                        prep[hp][:, :]), inc=ss)
                    tok_prep[hp] = ss.n
                    b.wait("ve", ss)
                    b.wait("ve", sd, tok_atst[hp])
                    b.op("ve", lambda v, hp=hp: v.tensor_tensor(
                        atst[:, hp * 512:(hp + 1) * 512], pnd[hp][0:64, :],
                        asm[0:64, hp * 1024 + 512:hp * 1024 + 1024],
                        ALU.mult), inc=sv)
                    tok_pnd[hp] = sv.n
                    b.wait("sy", sv)
                    b.op("sy", lambda e, hp=hp, j=j, dc=dc, qc=qc: e.dma_start(
                        out=at[j * 64:(j + 1) * 64,
                               dc * 2048 + qc * 512:dc * 2048 + (qc + 1) * 512],
                        in_=atst[:, hp * 512:(hp + 1) * 512]), dinc=sd)
                    tok_atst[hp] = sd.n
            b.wait("te", sd)
            b.wait("te", *wo_ready)
            for t in range(4 * qc, 4 * qc + 4):
                for dn in range(2):
                    pr = (2 * t + dn) % 2
                    b.wait("te", sv, tok_po[pr])
                    for kdc in range(ND):
                        b.op("te", lambda te, kdc=kdc, t=t, dn=dn, pr=pr:
                             te.matmul(
                                 po[pr][:, :],
                                 at[:, kdc * 2048 + t * 128:kdc * 2048 + (t + 1) * 128],
                                 wo_sb[:, kdc * 1024 + dn * 512:kdc * 1024 + (dn + 1) * 512],
                                 start=(kdc == 0), stop=(kdc == ND - 1)),
                             inc=(sp if kdc == ND - 1 else None))
                    b.wait("ve", sp)
                    b.op("ve", lambda v, t=t, dn=dn, pr=pr: v.tensor_tensor(
                        x_sb[:, t * 1024 + dn * 512:t * 1024 + (dn + 1) * 512],
                        po[pr][:, :],
                        x_sb[:, t * 1024 + dn * 512:t * 1024 + (dn + 1) * 512],
                        ALU.add), inc=sv)
                    tok_po[pr] = sv.n
    return sv, sv.n


def emit_ffn(b, tag, xt, x_sb, wb, gwt_sb, dram_w, consts, xt_ready,
             gwt_ready, wb_free):
    sp = b.sem(tag + "p")
    sv = b.sem(tag + "v")
    ss = b.sem(tag + "s")
    swl = b.sem(tag + "w")
    eye = consts["c_eye"]
    erep = consts["c_erep"]
    with ExitStack() as ps:
        cmb = b.sbuf(tag + "_cmb", [16, 2048], BF16, stack=ps)
        crep2 = b.sbuf(tag + "_crep", [128, 4096], BF16, stack=ps)
        zb = b.sbuf(tag + "_z", [128, 2048], BF16, stack=ps)
        vw = b.sbuf(tag + "_vw", [128, 2048], BF16, stack=ps)
        hsl = b.sbuf(tag + "_h", [128, 16384], BF16, stack=ps)
        gt = b.sbuf(tag + "_gt", [128, 64], F32, stack=ps)
        gcb = b.sbuf(tag + "_gcb", [128, 16], BF16, stack=ps)

        with ExitStack() as gs:
            pg = b.psum(tag + "_pg", [128, 16], stack=gs)
            ptc = b.psum(tag + "_ptc", [16, 128], BF16, stack=gs)
            b.wait("te", *xt_ready)
            b.wait("te", *gwt_ready)
            tok_pg = 0
            tok_ptc = 0
            tok_gcb = 0
            for t in range(NT):
                b.wait("te", ss, tok_pg)
                for kdc in range(ND):
                    b.op("te", lambda te, kdc=kdc, t=t: te.matmul(
                        pg[:, :],
                        xt[:, kdc * 2048 + t * 128:kdc * 2048 + (t + 1) * 128],
                        gwt_sb[:, kdc * 16:(kdc + 1) * 16],
                        start=(kdc == 0), stop=(kdc == ND - 1)),
                        inc=(sp if kdc == ND - 1 else None))
                b.wait("sc", sp)
                b.op("sc", lambda s: s.activation(gt[:, 0:16], pg[:, :],
                                                  AF.Sigmoid), inc=ss)
                tok_pg = ss.n
                b.wait("ve", ss)
                b.opf("ve", lambda v: v.max(out=gt[:, 16:24], in_=gt[:, 0:16]))
                b.opf("ve", lambda v: v.tensor_single_scalar(
                    gt[:, 24:40], gt[:, 0:16], gt[:, 19:20], ALU.is_ge))
                b.opf("ve", lambda v: v.tensor_mul(gt[:, 40:56], gt[:, 0:16],
                                                   gt[:, 24:40]))
                b.opf("ve", lambda v: v.reduce_sum(gt[:, 56:57], gt[:, 40:56],
                                                   AX.X))
                b.opf("ve", lambda v: v.reciprocal(gt[:, 57:58], gt[:, 56:57]))
                b.wait("ve", sp, tok_gcb)
                b.op("ve", lambda v: v.tensor_single_scalar(
                    gcb[:, :], gt[:, 40:56], gt[:, 57:58], ALU.mult), inc=sv)
                b.wait("te", sv)
                b.wait("te", sv, tok_ptc)
                b.op("te", lambda te: te.transpose(ptc[:, :], gcb[:, :],
                                                   eye[:, :]), inc=sp)
                tok_gcb = sp.n
                b.wait("ve", sp)
                b.op("ve", lambda v, t=t: v.tensor_copy(
                    cmb[:, t * 128:(t + 1) * 128], ptc[:, :]), inc=sv)
                tok_ptc = sv.n

        ph = b.psum(tag + "_ph", [128, 2048], stack=ps)
        pvf = b.psum(tag + "_pv", [128, 2048], stack=ps)
        W1 = wb[:, 0:4096]
        W2 = wb[:, 4096:8192]
        O_ = [wb[:, 8192:12288], wb[:, 12288:16384]]

        def loadw(dst, src, nblk):
            b.op("gp", lambda e, dst=dst, src=src, nblk=nblk: e.dma_start(
                out=dst.rearrange("p (k c) -> p k c", k=nblk),
                in_=src.rearrange("(k p) c -> p k c", p=128)), dinc=swl)

        tok_o = [0, 0]
        tok_h = [0, 0]
        tok_ph = 0
        tok_pvf = 0
        tok_zb = 0
        tok_po = [0, 0]
        crep_last = [0, 0]
        wl12 = {}
        wlo = {}
        b.wait("gp", *wb_free)
        loadw(W1, dram_w["e_win"][0], 8)
        loadw(W2, dram_w["e_vin"][0], 8)
        wl12[0] = swl.n
        loadw(O_[0], dram_w["e_wout"][0], 4)
        wlo[0] = swl.n
        loadw(O_[1], dram_w["e_wout"][1], 4)
        wlo[1] = swl.n

        def emit_crep(e):
            for tc in range(4):
                b.op("te", lambda te, e=e, tc=tc: te.matmul(
                    ph[:, tc * 512:(tc + 1) * 512],
                    erep[:, e * 128:(e + 1) * 128],
                    cmb[:, tc * 512:(tc + 1) * 512],
                    start=True, stop=True), inc=(sp if tc == 3 else None))
            b.wait("sc", sp)
            b.wait("sc", sv, crep_last[e % 2])
            b.op("sc", lambda s, e=e: s.copy(
                crep2[:, (e % 2) * 2048:(e % 2) * 2048 + 2048], ph[:, :]),
                inc=ss)
            return ss.n

        b.wait("te", sv)
        b.wait("te", ss, tok_ph)
        tok_ph = emit_crep(0)
        for e in range(NE):
            sl = e % 2
            crep = crep2[:, sl * 2048:sl * 2048 + 2048]
            b.wait("te", swl, wl12[e])
            for hc in range(4):
                b.wait("te", ss, tok_ph)
                b.wait("te", sv, tok_pvf)
                for tc in range(4):
                    for kdc in range(ND):
                        last = (tc == 3 and kdc == ND - 1)
                        b.op("te", lambda te, hc=hc, tc=tc, kdc=kdc:
                             te.matmul(
                                 ph[:, tc * 512:(tc + 1) * 512],
                                 W1[:, kdc * 512 + hc * 128:kdc * 512 + (hc + 1) * 128],
                                 xt[:, kdc * 2048 + tc * 512:kdc * 2048 + (tc + 1) * 512],
                                 start=(kdc == 0), stop=(kdc == ND - 1)),
                             inc=(sp if last else None))
                w1_tok = sp.n
                for tc in range(4):
                    for kdc in range(ND):
                        last = (tc == 3 and kdc == ND - 1)
                        b.op("te", lambda te, hc=hc, tc=tc, kdc=kdc:
                             te.matmul(
                                 pvf[:, tc * 512:(tc + 1) * 512],
                                 W2[:, kdc * 512 + hc * 128:kdc * 512 + (hc + 1) * 128],
                                 xt[:, kdc * 2048 + tc * 512:kdc * 2048 + (tc + 1) * 512],
                                 start=(kdc == 0), stop=(kdc == ND - 1)),
                             inc=(sp if last else None))
                w2_tok = sp.n
                b.wait("sc", sp, w1_tok)
                b.wait("sc", sv, tok_zb)
                b.op("sc", lambda s: s.activation(zb[:, :], ph[:, :],
                                                  AF.Silu), inc=ss)
                tok_ph = ss.n
                b.wait("ve", sp, w2_tok)
                b.wait("ve", ss)
                b.op("ve", lambda v, crep=crep: v.tensor_tensor(
                    vw[:, :], pvf[:, :], crep, ALU.mult), inc=sv)
                tok_pvf = sv.n
                b.wait("ve", sp, tok_h[sl])
                b.op("ve", lambda v, sl=sl, hc=hc: v.tensor_tensor(
                    hsl[:, sl * 8192 + hc * 2048:sl * 8192 + (hc + 1) * 2048],
                    zb[:, :], vw[:, :], ALU.mult), inc=sv)
                tok_zb = sv.n
            crep_last[sl] = sv.n
            tok_w12 = sp.n
            if e + 1 < NE:
                b.wait("te", ss, tok_ph)
                tok_ph = emit_crep(e + 1)
                b.wait("gp", sp, tok_w12)
                loadw(W1, dram_w["e_win"][e + 1], 8)
                loadw(W2, dram_w["e_vin"][e + 1], 8)
                wl12[e + 1] = swl.n
            if sl == 1:
                b.wait("te", sv)
                b.wait("te", swl, wlo[e])
                for t in range(NT):
                    for dn in range(2):
                        pr = (2 * t + dn) % 2
                        b.wait("te", sv, tok_po[pr])
                        n = 0
                        for s2 in range(2):
                            for hkc in range(4):
                                n += 1
                                b.op("te", lambda te, s2=s2, hkc=hkc, t=t,
                                     dn=dn, pr=pr, first=(n == 1),
                                     last=(n == 8): te.matmul(
                                         pvf[:, pr * 512:(pr + 1) * 512],
                                         hsl[:, s2 * 8192 + hkc * 2048 + t * 128:
                                             s2 * 8192 + hkc * 2048 + (t + 1) * 128],
                                         O_[s2][:, hkc * 1024 + dn * 512:
                                                hkc * 1024 + (dn + 1) * 512],
                                         start=first, stop=last),
                                     inc=(sp if last else None))
                        b.wait("ve", sp)
                        b.op("ve", lambda v, t=t, dn=dn, pr=pr:
                             v.tensor_tensor(
                                 x_sb[:, t * 1024 + dn * 512:t * 1024 + (dn + 1) * 512],
                                 pvf[:, pr * 512:(pr + 1) * 512],
                                 x_sb[:, t * 1024 + dn * 512:t * 1024 + (dn + 1) * 512],
                                 ALU.add), inc=sv)
                        tok_po[pr] = sv.n
                tok_h = [sp.n, sp.n]
                tok_o = [sp.n, sp.n]
                tok_pvf = sv.n
            if e + 1 < NE and e >= 1:
                slot_next = (e + 1) % 2
                b.wait("gp", sp, tok_o[slot_next])
                loadw(O_[slot_next], dram_w["e_wout"][e + 1], 4)
                wlo[e + 1] = swl.n

        b.wait("gp", sp)
        loadw(wb[:, 0:8192], dram_w["ws_in"], 8)
        loadw(wb[:, 8192:16384], dram_w["vs_in"], 8)
        wsl1 = swl.n
        b.wait("te", swl, wsl1)
        for hsc in range(ND):
            b.wait("te", ss, tok_ph)
            b.wait("te", sv, tok_pvf)
            b.wait("te", sp, tok_h[hsc // 4])
            for tc in range(4):
                for kdc in range(ND):
                    b.op("te", lambda te, hsc=hsc, tc=tc, kdc=kdc: te.matmul(
                        ph[:, tc * 512:(tc + 1) * 512],
                        wb[:, kdc * 1024 + hsc * 128:kdc * 1024 + (hsc + 1) * 128],
                        xt[:, kdc * 2048 + tc * 512:kdc * 2048 + (tc + 1) * 512],
                        start=(kdc == 0), stop=(kdc == ND - 1)))
            for tc in range(4):
                for kdc in range(ND):
                    last = (tc == 3 and kdc == ND - 1)
                    b.op("te", lambda te, hsc=hsc, tc=tc, kdc=kdc: te.matmul(
                        pvf[:, tc * 512:(tc + 1) * 512],
                        wb[:, 8192 + kdc * 1024 + hsc * 128:
                           8192 + kdc * 1024 + (hsc + 1) * 128],
                        xt[:, kdc * 2048 + tc * 512:kdc * 2048 + (tc + 1) * 512],
                        start=(kdc == 0), stop=(kdc == ND - 1)),
                        inc=(sp if last else None))
            b.wait("sc", sp)
            b.wait("sc", sv, tok_zb)
            b.op("sc", lambda s: s.activation(zb[:, :], ph[:, :], AF.Silu),
                 inc=ss)
            tok_ph = ss.n
            b.wait("ve", ss)
            b.op("ve", lambda v, hsc=hsc: v.scalar_tensor_tensor(
                hsl[:, hsc * 2048:(hsc + 1) * 2048], pvf[:, :], 0.0,
                zb[:, :], ALU.bypass, ALU.mult), inc=sv)
            tok_pvf = sv.n
            tok_zb = sv.n
        hs_done = sv.n
        b.wait("gp", sp)
        loadw(wb[:, 0:8192], dram_w["ws_out"], 8)
        wsl2 = swl.n
        b.wait("te", swl, wsl2)
        b.wait("te", sv, hs_done)
        for t in range(NT):
            for dn in range(2):
                pr = (2 * t + dn) % 2
                b.wait("te", sv, tok_po[pr])
                for hkc in range(ND):
                    b.op("te", lambda te, hkc=hkc, t=t, dn=dn, pr=pr:
                         te.matmul(
                             pvf[:, pr * 512:(pr + 1) * 512],
                             hsl[:, hkc * 2048 + t * 128:hkc * 2048 + (t + 1) * 128],
                             wb[:, hkc * 1024 + dn * 512:hkc * 1024 + (dn + 1) * 512],
                             start=(hkc == 0), stop=(hkc == ND - 1)),
                         inc=(sp if hkc == ND - 1 else None))
                b.wait("ve", sp)
                b.op("ve", lambda v, t=t, dn=dn, pr=pr: v.tensor_tensor(
                    x_sb[:, t * 1024 + dn * 512:t * 1024 + (dn + 1) * 512],
                    pvf[:, pr * 512:(pr + 1) * 512],
                    x_sb[:, t * 1024 + dn * 512:t * 1024 + (dn + 1) * 512],
                    ALU.add), inc=sv)
                tok_po[pr] = sv.n
    return sv, sv.n


def emit_ffn_sparse(b, tag, xt, x_sb, wb, gwt_sb, dram_w, consts, xt_ready,
                    gwt_ready, wb_free, lnH, ln_ok, ybuf, idxD, cmbD, regs):
    sp = b.sem(tag + "p")
    sv = b.sem(tag + "v")
    ss = b.sem(tag + "s")
    swl = b.sem(tag + "w")
    sdd = b.sem(tag + "d")
    sgt = b.sem(tag + "g")
    se2 = b.sem(tag + "e2")
    eye = consts["c_eye"]
    triu = consts["c_triu"]
    o128 = consts["c_o128"]
    sloti = consts["c_sloti"]
    hl3 = consts["c_hl3"]
    onesb = consts["c_ones"]
    nc = b.nc
    CW = CAP // 16

    def ncdma(e, out, in_):
        with nc.allow_non_contiguous_dma(reason="tiny idx wrap"):
            return e.dma_start(out=out, in_=in_)

    with ExitStack() as ps:
        gcball = b.sbuf(tag + "_gcb", [128, 256], BF16, stack=ps)
        maskall = b.sbuf(tag + "_msk", [128, 256], BF16, stack=ps)
        slotT = b.sbuf(tag + "_slt", [128, 256], F32, stack=ps)
        cntf = b.sbuf(tag + "_cnf", [1, 16], F32, stack=ps)
        cnti = b.sbuf(tag + "_cni", [1, 16], I32, stack=ps)
        cmbrow = b.sbuf(tag + "_crw", [1, 2 * CAP], BF16, stack=ps)
        idxw = b.sbuf(tag + "_ix", [32, 16 * CW], I16, stack=ps)

        xg = b.sbuf(tag + "_xg", [128, ND * CAP], BF16, stack=ps)

        with ExitStack() as gs:
            gt = b.sbuf(tag + "_gt", [128, 64], F32, stack=gs)
            gcb = b.sbuf(tag + "_gc", [128, 16], BF16, stack=gs)
            pg = b.psum(tag + "_pg", [128, 16], stack=gs)
            b.wait("te", *xt_ready)
            b.wait("te", *gwt_ready)
            tok_pg = 0
            b.op("ve", lambda v: v.memset(xg[:, 0:1024], 0.0), inc=sv)
            zt_ok = sv.n
            for t in range(NT):
                b.wait("te", ss, tok_pg)
                for kdc in range(ND):
                    b.op("te", lambda te, kdc=kdc, t=t: te.matmul(
                        pg[:, :],
                        xt[:, kdc * 2048 + t * 128:kdc * 2048 + (t + 1) * 128],
                        gwt_sb[:, kdc * 16:(kdc + 1) * 16],
                        start=(kdc == 0), stop=(kdc == ND - 1)),
                        inc=(sp if kdc == ND - 1 else None))
                b.wait("sc", sp)
                b.op("sc", lambda s: s.activation(gt[:, 0:16], pg[:, :],
                                                  AF.Sigmoid), inc=ss)
                tok_pg = ss.n
                b.wait("ve", ss)
                b.opf("ve", lambda v: v.max(out=gt[:, 16:24], in_=gt[:, 0:16]))
                b.opf("ve", lambda v: v.tensor_single_scalar(
                    gt[:, 24:40], gt[:, 0:16], gt[:, 19:20], ALU.is_ge))
                b.opf("ve", lambda v: v.tensor_mul(gt[:, 40:56], gt[:, 0:16],
                                                   gt[:, 24:40]))
                b.opf("ve", lambda v: v.reduce_sum(gt[:, 56:57], gt[:, 40:56],
                                                   AX.X))
                b.opf("ve", lambda v: v.reciprocal(gt[:, 57:58], gt[:, 56:57]))
                b.op("ve", lambda v: v.tensor_single_scalar(
                    gcb[:, :], gt[:, 40:56], gt[:, 57:58], ALU.mult), inc=sv)
                b.opf("ve", lambda v, t=t: v.tensor_copy(
                    gcball[:, t * 16:(t + 1) * 16], gcb[:, :]))
                b.op("ve", lambda v, t=t: v.tensor_copy(
                    maskall[:, t * 16:(t + 1) * 16], gt[:, 24:40]), inc=sv)

        b.wait("sy", sv, zt_ok)
        for t in range(NT):
            b.op("sy", lambda e, t=t: e.dma_start(
                out=ybuf[t * 128:(t + 1) * 128, :], in_=xg[:, 0:1024]),
                dinc=sdd)
        ybz_ok = sdd.n

        with ExitStack() as cs:
            pcs = b.psum(tag + "_pc", [128, 16], stack=cs)
            csv = b.sbuf(tag + "_cs", [128, 16], F32, stack=cs)
            tmp1 = b.sbuf(tag + "_t1", [128, 16], F32, stack=cs)
            tok_pcs = 0
            b.wait("te", sv)
            for g in range(NT):
                b.wait("te", sv, tok_pcs)
                for k in range(g + 1):
                    b.op("te", lambda te, k=k, g=g: te.matmul(
                        pcs[:, :],
                        (triu if k == g else o128)[:, :],
                        maskall[:, k * 16:(k + 1) * 16],
                        start=(k == 0), stop=(k == g)),
                        inc=(sp if k == g else None))
                b.wait("ve", sp)
                if g == NT - 1:
                    b.wait("sc", sp)
                    b.op("sc", lambda s: s.copy(csv[:, :], pcs[:, :]),
                         inc=ss)
                b.op("ve", lambda v, g=g: v.scalar_tensor_tensor(
                    tmp1[:, :], pcs[:, :], CAP + 0.5,
                    maskall[:, g * 16:(g + 1) * 16], ALU.is_lt, ALU.mult))
                b.opf("ve", lambda v: v.tensor_tensor(
                    tmp1[:, :], tmp1[:, :], pcs[:, :], ALU.mult))
                b.op("ve", lambda v, g=g: v.tensor_single_scalar(
                    slotT[:, g * 16:(g + 1) * 16], tmp1[:, :], 1.0,
                    ALU.subtract), inc=sv)
                tok_pcs = sv.n
            b.wait("sy", ss)
            b.op("sy", lambda e: e.dma_start(out=cntf[:, :],
                                             in_=csv[127:128, :]), dinc=sdd)
            b.wait("ve", sdd)
            b.op("ve", lambda v: v.tensor_single_scalar(
                cntf[:, :], cntf[:, :], float(CAP), ALU.min))
            b.op("ve", lambda v: v.tensor_copy(cnti[:, :], cntf[:, :]),
                 inc=sv)
            cnt_ok = sv.n

        idx_ok = {}
        with ExitStack() as isx:
            pidx = b.psum(tag + "_pi", [3, CAP], stack=isx)
            ilhs = b.sbuf(tag + "_il", [128, 48], BF16, stack=isx)
            Sbuf = b.sbuf(tag + "_S", [128, 2 * CAP], BF16, stack=isx)
            prow = b.sbuf(tag + "_pr", [3, CAP], F32, stack=isx)
            tkb = b.sbuf(tag + "_tk", [1, CAP], F32, stack=isx)
            tki = b.sbuf(tag + "_ti", [1, CAP], I16, stack=isx)
            b.op("ve", lambda v: v.tensor_copy(ilhs[:, :], hl3[:, :]), inc=sv)
            tok_S = [0, 0]
            tok_pidx = 0
            tok_prow = 0
            tok_tkb = 0
            cmb_cp = 0
            for e in range(NE):
                b.wait("ve", sp)
                for k in range(NT):
                    b.op("ve", lambda v, k=k, e=e: v.tensor_copy(
                        ilhs[:, 3 * k:3 * k + 1],
                        gcball[:, k * 16 + e:k * 16 + e + 1]))
                for k in range(NT):
                    b.wait("ve", sp, tok_S[k % 2])
                    b.op("ve", lambda v, k=k, e=e: v.tensor_single_scalar(
                        Sbuf[:, (k % 2) * CAP:(k % 2 + 1) * CAP],
                        sloti[:, :], slotT[:, k * 16 + e:k * 16 + e + 1],
                        ALU.is_equal), inc=sv)
                    b.wait("te", sv)
                    if e == 0 and k == 0:
                        b.wait("te", ss)
                    if k == 0:
                        b.wait("te", ss, tok_pidx)
                    for (c0, cw) in ((0, 512), (512, CAP - 512)):
                        b.op("te", lambda te, k=k, c0=c0, cw=cw: te.matmul(
                            pidx[:, c0:c0 + cw],
                            ilhs[:, 3 * k:3 * (k + 1)],
                            Sbuf[:, (k % 2) * CAP + c0:(k % 2) * CAP + c0 + cw],
                            start=(k == 0), stop=(k == NT - 1)),
                            inc=(sp if c0 else None))
                    tok_S[k % 2] = sp.n
                b.wait("sc", sp)
                b.wait("sc", sdd, tok_prow)
                b.wait("sc", sv, cmb_cp)
                b.op("sc", lambda s: s.copy(prow[:, :], pidx[:, :]), inc=ss)
                tok_pidx = ss.n
                b.wait("ve", ss)
                b.op("ve", lambda v: v.tensor_copy(cmbrow[0:1, 0:CAP],
                                                   prow[0:1, :]), inc=sv)
                cmb_cp = sv.n
                b.wait("sy", sv)
                b.op("sy", lambda e2, e=e: e2.dma_start(
                    out=cmbD[e:e + 1, :], in_=cmbrow[0:1, 0:CAP]), dinc=sdd)
                b.wait("gp", ss, tok_pidx)
                b.wait("gp", sv, tok_tkb)
                b.op("gp", lambda e2: e2.dma_start(
                    out=tkb[:, :], in_=prow[1:2, :]), dinc=sdd)
                b.wait("gp", sdd)
                b.op("gp", lambda e2: e2.dma_start(
                    out=tkb[:, :], in_=prow[2:3, :],
                    accum_op=ALU.add), dinc=sdd)
                tok_prow = sdd.n
                b.wait("ve", sdd)
                b.op("ve", lambda v: v.tensor_single_scalar(
                    tkb[:, :], tkb[:, :], 1.0, ALU.subtract))
                b.op("ve", lambda v: v.tensor_copy(tki[:, :], tkb[:, :]),
                     inc=sv)
                tok_tkb = sv.n
                b.wait("sy", sv)
                b.op("sy", lambda e2, e=e: e2.dma_start(
                    out=idxD[e:e + 1, :], in_=tki[:, :]), dinc=sdd)
                wtok = sdd.n
                b.wait("sy", sdd, wtok)
                for hh in range(2):
                    b.op("sy", lambda e2, e=e, hh=hh: ncdma(
                        e2, idxw[hh * 16:(hh + 1) * 16, e * CW:(e + 1) * CW],
                        idxD[e:e + 1, :].rearrange("o (j q) -> q (o j)",
                                                   q=16)), dinc=sdd)
                idx_ok[e] = sdd.n
            idx_sc_done = ss.n

        with ExitStack() as shs:
            ph = b.psum(tag + "_ph", [128, 2048], stack=shs)
            pvf = b.psum(tag + "_pv", [128, 2048], stack=shs)
            zb = b.sbuf(tag + "_zs", [128, 2048], BF16, stack=shs)
            hsl = b.sbuf(tag + "_hs", [128, 16384], BF16, stack=shs)

            def loadw(dst, src, nblk):
                b.op("gp", lambda e, dst=dst, src=src, nblk=nblk:
                     e.dma_start(
                         out=dst.rearrange("p (k c) -> p k c", k=nblk),
                         in_=src.rearrange("(k p) c -> p k c", p=128)),
                     dinc=swl)

            b.wait("gp", *wb_free)
            loadw(wb[:, 0:8192], dram_w["ws_in"], 8)
            loadw(wb[:, 8192:16384], dram_w["vs_in"], 8)
            wsl1 = swl.n
            b.wait("te", swl, wsl1)
            b.wait("te", ss, idx_sc_done)
            tok_ph = 0
            tok_pvf = 0
            tok_zb = 0
            tok_po = [0, 0]
            for hsc in range(ND):
                b.wait("te", ss, tok_ph)
                b.wait("te", sv, tok_pvf)
                for tc in range(4):
                    for kdc in range(ND):
                        last = (tc == 3 and kdc == ND - 1)
                        b.op("te", lambda te, hsc=hsc, tc=tc, kdc=kdc:
                             te.matmul(
                                 ph[:, tc * 512:(tc + 1) * 512],
                                 wb[:, kdc * 1024 + hsc * 128:
                                    kdc * 1024 + (hsc + 1) * 128],
                                 xt[:, kdc * 2048 + tc * 512:
                                    kdc * 2048 + (tc + 1) * 512],
                                 start=(kdc == 0), stop=(kdc == ND - 1)),
                             inc=(sp if last else None))
                w1tok = sp.n
                for tc in range(4):
                    for kdc in range(ND):
                        last = (tc == 3 and kdc == ND - 1)
                        b.op("te", lambda te, hsc=hsc, tc=tc, kdc=kdc:
                             te.matmul(
                                 pvf[:, tc * 512:(tc + 1) * 512],
                                 wb[:, 8192 + kdc * 1024 + hsc * 128:
                                    8192 + kdc * 1024 + (hsc + 1) * 128],
                                 xt[:, kdc * 2048 + tc * 512:
                                    kdc * 2048 + (tc + 1) * 512],
                                 start=(kdc == 0), stop=(kdc == ND - 1)),
                             inc=(sp if last else None))
                w2tok = sp.n
                b.wait("sc", sp, w1tok)
                b.wait("sc", sv, tok_zb)
                b.op("sc", lambda s: s.activation(zb[:, :], ph[:, :],
                                                  AF.Silu), inc=ss)
                tok_ph = ss.n
                b.wait("ve", sp, w2tok)
                b.wait("ve", ss)
                b.op("ve", lambda v, hsc=hsc: v.scalar_tensor_tensor(
                    hsl[:, hsc * 2048:(hsc + 1) * 2048], pvf[:, :], 0.0,
                    zb[:, :], ALU.bypass, ALU.mult), inc=sv)
                tok_pvf = sv.n
                tok_zb = sv.n
            hs_done = sv.n
            b.wait("gp", sp)
            loadw(wb[:, 0:8192], dram_w["ws_out"], 8)
            wsl2 = swl.n
            b.wait("te", swl, wsl2)
            b.wait("te", sv, hs_done)
            for t in range(NT):
                for dn in range(2):
                    pr = (2 * t + dn) % 2
                    b.wait("te", sv, tok_po[pr])
                    for hkc in range(ND):
                        b.op("te", lambda te, hkc=hkc, t=t, dn=dn, pr=pr:
                             te.matmul(
                                 pvf[:, pr * 512:(pr + 1) * 512],
                                 hsl[:, hkc * 2048 + t * 128:
                                     hkc * 2048 + (t + 1) * 128],
                                 wb[:, hkc * 1024 + dn * 512:
                                    hkc * 1024 + (dn + 1) * 512],
                                 start=(hkc == 0), stop=(hkc == ND - 1)),
                             inc=(sp if hkc == ND - 1 else None))
                    b.wait("ve", sp)
                    b.op("ve", lambda v, t=t, dn=dn, pr=pr: v.tensor_tensor(
                        x_sb[:, t * 1024 + dn * 512:t * 1024 + (dn + 1) * 512],
                        pvf[:, pr * 512:(pr + 1) * 512],
                        x_sb[:, t * 1024 + dn * 512:t * 1024 + (dn + 1) * 512],
                        ALU.add), inc=sv)
                    tok_po[pr] = sv.n
            shared_done_te = sp.n
            shared_done = sv.n

        gxs = b.sbuf(tag + "_gx", [128, NBLK * 1024], F32, stack=ps)
        gxb = b.sbuf(tag + "_gb", [128, 2 * 1024], BF16, stack=ps)
        hsl2 = b.sbuf(tag + "_h2", [128, 4 * CAP], BF16, stack=ps)
        zb2 = b.sbuf(tag + "_z2", [128, CAP], BF16, stack=ps)
        vw2 = b.sbuf(tag + "_v2", [128, CAP], BF16, stack=ps)
        crepb = b.sbuf(tag + "_cp", [128, 2 * CAP], BF16, stack=ps)
        scat = b.sbuf(tag + "_sct", [128, NBLK * 1024], BF16, stack=ps)
        ph2 = b.psum(tag + "_p2", [128, CAP], stack=ps)
        pv2 = b.psum(tag + "_q2", [128, CAP], stack=ps)
        po2 = b.psum(tag + "_o2", [128, 1024], stack=ps)
        ptr2 = [b.psum(tag + f"_tp{i}", [128, 128], BF16, stack=ps)
                for i in range(2)]
        W1 = wb[:, 0:4096]
        W2 = wb[:, 4096:8192]
        O_ = [wb[:, 8192:12288], wb[:, 12288:16384]]

        def loadw2(dst, src, nblk):
            b.op("gp", lambda e, dst=dst, src=src, nblk=nblk: e.dma_start(
                out=dst.rearrange("p (k c) -> p k c", k=nblk),
                in_=src.rearrange("(k p) c -> p k c", p=128)), dinc=swl)

        b.wait("gp", sp, shared_done_te)
        loadw2(W1, dram_w["e_win"][0], 8)
        loadw2(W2, dram_w["e_vin"][0], 8)
        wl12 = {0: swl.n}
        loadw2(O_[0], dram_w["e_wout"][0], 4)
        wlo = {0: swl.n}
        loadw2(O_[1], dram_w["e_wout"][1], 4)
        wlo[1] = swl.n

        def emit_gather(e):
            rg = regs["rca"] if e % 2 == 0 else regs["rcb"]
            b.wait("gp", sdd, idx_ok[e])
            b.wait("gp", sv, cnt_ok)
            b.wait("gp", *ln_ok)
            b.op("gp", lambda g, rg=rg, e=e: g.reg_load(
                rg, cnti[0:1, e:e + 1]))
            b.op("gp", lambda g, rg=rg, e=e: g.dma_gather(
                gxs[:, :].rearrange("p (k c) -> p k c", k=NBLK),
                lnH, idxw[:, e * CW:(e + 1) * CW],
                CAP, rg, 1024, transpose=False), dinc=sgt)
            return sgt.n

        cmb_w = [0, 0]
        crep_war = [0, 0]

        def emit_crep(e, ph2_war):
            sl = e % 2
            b.wait("sy", sdd, idx_ok[e])
            b.wait("sy", sp, cmb_w[sl])
            b.op("sy", lambda e2, e=e, sl=sl: e2.dma_start(
                out=cmbrow[0:1, sl * CAP:(sl + 1) * CAP],
                in_=cmbD[e:e + 1, :]), dinc=sdd)
            cr_ld = sdd.n
            b.wait("te", sdd, cr_ld)
            b.wait("te", ss, ph2_war)
            for (c0, cw) in ((0, 512), (512, CAP - 512)):
                b.op("te", lambda te, sl=sl, c0=c0, cw=cw: te.matmul(
                    ph2[:, c0:c0 + cw], onesb[0:1, 0:128],
                    cmbrow[0:1, sl * CAP + c0:sl * CAP + c0 + cw],
                    start=True, stop=True), inc=(sp if c0 else None))
            cmb_w[sl] = sp.n
            b.wait("sc", sp)
            b.wait("sc", sv, crep_war[sl])
            b.op("sc", lambda s, sl=sl: s.copy(
                crepb[:, sl * CAP:(sl + 1) * CAP], ph2[:, :]), inc=ss)
            return ss.n

        gat_tok = {}
        b.wait("gp", sv, shared_done)
        gat_tok[0] = emit_gather(0)
        tok_ph2 = emit_crep(0, 0)
        tok_pv2 = shared_done
        tok_zb2 = 0
        tok_h2 = 0
        tok_gxb = [0, 0]
        tok_po2 = 0
        tok_scat_dma = 0
        proj_last = 0
        tr_ev = {}
        b.wait("te", sv, shared_done)
        for e in range(NE):
            sl = e % 2
            last_cast = 0
            for c in range(NBLK):
                cp = c % 2
                b.wait("ve", sgt, gat_tok[e])
                b.wait("ve", sp, tok_gxb[cp])
                b.op("ve", lambda v, c=c, cp=cp: v.tensor_copy(
                    gxb[:, cp * 1024:(cp + 1) * 1024],
                    gxs[:, c * 1024:(c + 1) * 1024]), inc=sv)
                if c == NBLK - 1:
                    last_cast = sv.n
                b.wait("te", sv)
                if e > 0 and c == 0:
                    b.wait("te", sp, proj_last)
                for k in range(ND):
                    par = k % 2
                    if par in tr_ev:
                        b.wait("te", se2, tr_ev[par])
                    b.op("te", lambda te, k=k, cp=cp, par=par: te.transpose(
                        ptr2[par][:, :],
                        gxb[:, cp * 1024 + k * 128:cp * 1024 + (k + 1) * 128],
                        eye[:, :]), inc=sp)
                    b.wait("ve", sp)
                    if e == 0 and c == 0 and k == 0:
                        b.wait("ve", sdd, ybz_ok)
                    b.op("ve", lambda v, k=k, c=c, par=par: v.tensor_copy(
                        xg[:, k * CAP + c * 128:k * CAP + (c + 1) * 128],
                        ptr2[par][:, :]), inc=se2)
                    tr_ev[par] = se2.n
                tok_gxb[cp] = sp.n
            xg_ready = se2.n
            if e + 1 < NE:
                b.wait("gp", sv, last_cast)
                gat_tok[e + 1] = emit_gather(e + 1)
            b.wait("te", swl, wl12[e])
            b.wait("te", se2, xg_ready)
            for hc in range(4):
                b.wait("te", ss, tok_ph2)
                b.wait("te", sv, tok_pv2)
                for (c0, cw) in ((0, 512), (512, CAP - 512)):
                    for kdc in range(ND):
                        last = (c0 == 512 and kdc == ND - 1)
                        b.op("te", lambda te, hc=hc, c0=c0, cw=cw, kdc=kdc:
                             te.matmul(
                                 ph2[:, c0:c0 + cw],
                                 W1[:, kdc * 512 + hc * 128:
                                    kdc * 512 + (hc + 1) * 128],
                                 xg[:, kdc * CAP + c0:kdc * CAP + c0 + cw],
                                 start=(kdc == 0), stop=(kdc == ND - 1)),
                             inc=(sp if last else None))
                w1tok = sp.n
                for (c0, cw) in ((0, 512), (512, CAP - 512)):
                    for kdc in range(ND):
                        last = (c0 == 512 and kdc == ND - 1)
                        b.op("te", lambda te, hc=hc, c0=c0, cw=cw, kdc=kdc:
                             te.matmul(
                                 pv2[:, c0:c0 + cw],
                                 W2[:, kdc * 512 + hc * 128:
                                    kdc * 512 + (hc + 1) * 128],
                                 xg[:, kdc * CAP + c0:kdc * CAP + c0 + cw],
                                 start=(kdc == 0), stop=(kdc == ND - 1)),
                             inc=(sp if last else None))
                w2tok = sp.n
                proj_last = sp.n
                b.wait("sc", sp, w1tok)
                b.wait("sc", sv, tok_zb2)
                b.op("sc", lambda s: s.activation(zb2[:, :], ph2[:, :],
                                                  AF.Silu), inc=ss)
                tok_ph2 = ss.n
                b.wait("ve", sp, w2tok)
                b.wait("ve", ss)
                b.op("ve", lambda v, sl=sl: v.tensor_tensor(
                    vw2[:, :], pv2[:, :],
                    crepb[:, sl * CAP:(sl + 1) * CAP], ALU.mult), inc=sv)
                tok_pv2 = sv.n
                b.wait("ve", sp, tok_h2)
                b.op("ve", lambda v, hc=hc: v.tensor_tensor(
                    hsl2[:, hc * CAP:(hc + 1) * CAP], zb2[:, :], vw2[:, :],
                    ALU.mult), inc=sv)
                tok_zb2 = sv.n
            h_done = sv.n
            crep_war[sl] = sv.n
            if e + 1 < NE:
                b.wait("gp", sp, proj_last)
                loadw2(W1, dram_w["e_win"][e + 1], 8)
                loadw2(W2, dram_w["e_vin"][e + 1], 8)
                wl12[e + 1] = swl.n
                tok_ph2 = emit_crep(e + 1, tok_ph2)
            b.wait("te", swl, wlo[e])
            b.wait("te", sv, h_done)
            for c in range(NBLK):
                b.wait("te", ss, tok_po2)
                for dn in range(2):
                    for hkc in range(4):
                        b.op("te", lambda te, c=c, dn=dn, hkc=hkc, sl=sl:
                             te.matmul(
                                 po2[:, dn * 512:(dn + 1) * 512],
                                 hsl2[:, hkc * CAP + c * 128:
                                      hkc * CAP + (c + 1) * 128],
                                 O_[sl][:, hkc * 1024 + dn * 512:
                                        hkc * 1024 + (dn + 1) * 512],
                                 start=(hkc == 0), stop=(hkc == 3)),
                             inc=(sp if (hkc == 3 and dn == 1) else None))
                b.wait("sc", sp)
                b.wait("sc", sgt, tok_scat_dma)
                b.op("sc", lambda s, c=c: s.copy(
                    scat[:, c * 1024:(c + 1) * 1024], po2[:, :]), inc=ss)
                tok_po2 = ss.n
            tok_h2 = sp.n
            scat_done = ss.n
            if e + 2 < NE:
                b.wait("gp", sp, tok_h2)
                loadw2(O_[sl], dram_w["e_wout"][e + 2], 4)
                wlo[e + 2] = swl.n
            rg = regs["rca"] if e % 2 == 0 else regs["rcb"]
            b.wait("gp", ss, scat_done)
            b.wait("gp", sdd, ybz_ok)
            b.op("gp", lambda g, rg=rg, e=e: g.reg_load(
                rg, cnti[0:1, e:e + 1]))
            b.op("gp", lambda g, rg=rg, e=e: g.dma_scatter_add(
                ybuf, scat[:, :].rearrange("p (k c) -> p k c", k=NBLK),
                idxw[:, e * CW:(e + 1) * CW],
                CAP, rg, 1024, read_from_swizzled=False), dinc=sgt)
            tok_scat_dma = sgt.n

        rb = gxb
        rb_tok = [tok_gxb[0], tok_gxb[1]]
        b.wait("sy", sgt, tok_scat_dma)
        b.wait("sy", sp, max(tok_gxb))
        for t in range(NT):
            par = t % 2
            b.wait("sy", sv, rb_tok[par])
            b.op("sy", lambda e2, t=t, par=par: e2.dma_start(
                out=rb[:, par * 1024:(par + 1) * 1024],
                in_=ybuf[t * 128:(t + 1) * 128, :]), dinc=sdd)
            b.wait("ve", sdd)
            b.op("ve", lambda v, t=t, par=par: v.tensor_tensor(
                x_sb[:, t * 1024:(t + 1) * 1024],
                rb[:, par * 1024:(par + 1) * 1024],
                x_sb[:, t * 1024:(t + 1) * 1024], ALU.add), inc=sv)
            rb_tok[par] = sv.n
        x_done = sv.n
    return sv, x_done


def emit_kvpayload(b, xt, wkT_sb, wvT_sb, bkt_sb, bvrow_sb, mask_sb, consts,
                   kt_d, v65_d, xt_ready, w_ready, mask_ready):
    sp = b.sem("kp_p")
    sv = b.sem("kp_v")
    ss = b.sem("kp_s")
    sd = b.sem("kp_d")
    eye = consts["c_eye"]
    with ExitStack() as ps:
        ktile = b.sbuf("kp_kt", [128, 2 * 512], BF16, stack=ps)
        ksq = b.sbuf("kp_ksq", [128, 512], F32, stack=ps)
        kn2 = b.sbuf("kp_kn2", [128, 32], F32, stack=ps)
        v65t = b.sbuf("kp_v65", [128, 2 * 1040], BF16, stack=ps)
        pkb = [b.psum(f"kp_pk{i}", [128, 512], stack=ps) for i in range(2)]
        pn = b.psum("kp_pn", [2, 512], stack=ps)
        ptr8 = b.psum("kp_ptr8", [16, 128], BF16, stack=ps)

        b.wait("te", *xt_ready)
        b.wait("te", *w_ready)
        tok_pkb = [0, 0]
        tok_dma = [0, 0]
        tok_ksq = 0
        tok_pn = 0
        for oc in range(ND):
            for tc in range(4):
                par = (oc * 4 + tc) % 2
                b.wait("te", ss, tok_pkb[par])
                for kic in range(ND):
                    b.op("te", lambda te, oc=oc, tc=tc, kic=kic, par=par:
                         te.matmul(
                             pkb[par][:, :],
                             wkT_sb[:, kic * 1024 + oc * 128:kic * 1024 + (oc + 1) * 128],
                             xt[:, kic * 2048 + tc * 512:kic * 2048 + (tc + 1) * 512],
                             start=(kic == 0), stop=(kic == ND - 1)),
                         inc=(sp if kic == ND - 1 else None))
                b.wait("sc", sp)
                b.wait("sc", sd, tok_dma[par])
                b.op("sc", lambda s, oc=oc, par=par: s.activation(
                    ktile[:, par * 512:(par + 1) * 512], pkb[par][:, :],
                    AF.Identity, bias=bkt_sb[:, oc:oc + 1]), inc=ss)
                tok_pkb[par] = ss.n
                b.wait("sy", ss)
                b.op("sy", lambda e, oc=oc, tc=tc, par=par: e.dma_start(
                    out=kt_d[oc * 128:(oc + 1) * 128, tc * 512:(tc + 1) * 512],
                    in_=ktile[:, par * 512:(par + 1) * 512]), dinc=sd)
                tok_dma[par] = sd.n
                b.wait("sc", sp, tok_ksq)
                b.opf("sc", lambda s, par=par: s.activation(
                    ksq[:, :], ktile[:, par * 512:(par + 1) * 512],
                    AF.Square))
                b.wait("te", b.selfsem["sc"])
                b.wait("te", b.selfsem["ve"], tok_pn)
                b.op("te", lambda te: te.matmul(
                    pn[:, :], consts["c_hm2"][:, :], ksq[:, :],
                    start=True, stop=True), inc=sp)
                tok_ksq = sp.n
                b.wait("ve", sp)
                if tc == 0:
                    b.opf("ve", lambda v, oc=oc: v.reduce_max(
                        kn2[0:2, oc:oc + 1], pn[:, :], AX.X))
                else:
                    b.opf("ve", lambda v: v.reduce_max(kn2[0:2, 16:17],
                                                       pn[:, :], AX.X))
                    b.opf("ve", lambda v, oc=oc: v.tensor_max(
                        kn2[0:2, oc:oc + 1], kn2[0:2, oc:oc + 1],
                        kn2[0:2, 16:17]))
                tok_pn = b.selfsem["ve"].n
        kn2b = b.sbuf("kp_kn2b", [16, 16], BF16, stack=ps)
        b.opf("ve", lambda v: v.tensor_copy(kn2b[0:2, 0:8], kn2[0:2, 0:8]))
        b.wait("te", b.selfsem["ve"])
        b.op("te", lambda te: te.transpose(ptr8[0:8, 0:2], kn2b[0:2, 0:8],
                                           eye[0:2, 0:2]), inc=sp)
        b.wait("ve", sp)
        b.opf("ve", lambda v: v.tensor_copy(kn2[0:8, 20:22], ptr8[0:8, 0:2]))
        b.opf("ve", lambda v: v.reduce_max(kn2[0:8, 22:23], kn2[0:8, 20:22],
                                           AX.X))
        b.opf("ve", lambda v: v.tensor_copy(kn2b[0:8, 8:9],
                                            kn2[0:8, 22:23]))
        b.wait("sy", b.selfsem["ve"])
        b.op("sy", lambda e: e.dma_start(
            out=kt_d[1024:1025, 0:8].rearrange("a (c u) -> (a c) u", u=1),
            in_=kn2b[0:8, 8:9]), dinc=sd)
        mrow = b.sbuf("kp_mrow", [16, 128], BF16, stack=ps)
        mbf = b.sbuf("kp_mbf", [128, 16], BF16, stack=ps)
        b.wait("ve", *mask_ready)
        b.opf("ve", lambda v: v.tensor_copy(mbf[:, :], mask_sb[:, :]))
        b.wait("te", b.selfsem["ve"])
        b.op("te", lambda te: te.transpose(ptr8[:, :], mbf[:, :], eye[:, :]),
             inc=sp)
        b.wait("ve", sp)
        b.op("ve", lambda v: v.tensor_copy(mrow[:, :], ptr8[:, :]), inc=sv)
        b.wait("sy", sv)
        b.op("sy", lambda e: e.dma_start(
            out=kt_d[1025:1026, :].rearrange("a (t p) -> (a t) p", p=128),
            in_=mrow[:, :]), dinc=sd)
        tok_pv = [0, 0]
        tok_v65dma = [0, 0]
        b.wait("te", ss)
        for t in range(NT):
            for dn in range(2):
                par = dn
                b.wait("te", sv, tok_pv[par])
                for kic in range(ND):
                    b.op("te", lambda te, t=t, dn=dn, kic=kic, par=par:
                         te.matmul(
                             pkb[par][:, :],
                             xt[:, kic * 2048 + t * 128:kic * 2048 + (t + 1) * 128],
                             wvT_sb[:, kic * 1024 + dn * 512:kic * 1024 + (dn + 1) * 512],
                             start=(kic == 0), stop=False))
                b.op("te", lambda te, dn=dn, par=par: te.matmul(
                    pkb[par][:, :], consts["c_ones"][0:1, 0:128],
                    bvrow_sb[0:1, dn * 512:(dn + 1) * 512],
                    start=False, stop=True), inc=sp)
                b.wait("ve", sp)
                sl = t % 2
                b.wait("ve", sd, tok_v65dma[sl])
                b.op("ve", lambda v, t=t, dn=dn, par=par, sl=sl:
                     v.tensor_single_scalar(
                         bass.AP(v65t, sl * 1040 + dn * 8 * 65,
                                 [[2 * 1040, 128], [65, 8], [1, 64]]),
                         bass.AP(pkb[par], 0, [[512, 128], [64, 8], [1, 64]]),
                         mask_sb[:, t:t + 1], ALU.mult), inc=sv)
                tok_pv[par] = sv.n
                b.op("ve", lambda v, t=t, dn=dn, sl=sl: v.tensor_copy(
                    bass.AP(v65t, sl * 1040 + dn * 8 * 65 + 64,
                            [[2 * 1040, 128], [65, 8]]),
                    mask_sb[:, t:t + 1].to_broadcast([128, 8])), inc=sv)
            b.wait("sy", sv)
            b.op("sy", lambda e, t=t, sl=t % 2: e.dma_start(
                out=v65_d[t * 128:(t + 1) * 128, :],
                in_=v65t[:, sl * 1040:sl * 1040 + 1040]), dinc=sd)
            tok_v65dma[t % 2] = sd.n
        b.wait("sy", sd)
    return sd, sd.n


def build_launch_b(stage="full"):
    nc = bass.Bass("TRN2", target_bir_lowering=False)
    with ExitStack() as stack:
        b = B(nc, stack)
        dram = {}
        for nm, shape in [("x", [TC, D]), ("kvp_all", [256, 1040]),
                          ("wq", [D, D]), ("wo", [D, D]),
                          ("gwt", [D, 16]),
                          ("ws_in", [D, D]), ("vs_in", [D, D]),
                          ("ws_out", [D, D]),
                          ("e_win", [NE, D, EH]), ("e_vin", [NE, D, EH]),
                          ("e_wout", [NE, EH, D]),
                          ("wkT", [D, D]), ("wvT", [D, D]),
                          ("bk_t", [128, 8]), ("bv_row", [1, D])]:
            dram[nm] = nc.dram_tensor(nm, shape, F32,
                                      kind="ExternalInput").ap()
        x2_d = nc.dram_tensor("x2", [TC, D], F32, kind="ExternalOutput").ap()
        kt_d = nc.dram_tensor("kt", [1026, 2048], BF16,
                              kind="ExternalOutput").ap()
        v65_d = nc.dram_tensor("v65", [TC, 1040], BF16,
                               kind="ExternalOutput").ap()

        consts = decl_consts(b, ["c_eye", "c_misc", "c_onesf", "c_erep",
                                 "c_hm2", "c_ones"])
        x_sb = b.sbuf("x_sbuf", [128, NT * 1024], F32)
        xt = b.sbuf("xt_sbuf", [128, ND * 2048], BF16)
        wb = b.sbuf("wb_sbuf", [128, 16384], BF16)
        kvb2 = b.sbuf("kvb2_sb", [128, 1040], BF16)
        mask_sb = b.sbuf("mask_sb", [128, 16], F32)
        gwt_sb = b.sbuf("gwt_sb", [128, 128], BF16)
        bkt_sb = b.sbuf("bkt_sb", [128, 8], F32)
        bvrow_sb = b.sbuf("bvrow_sb", [1, 1024], BF16)

        sw = b.sem("s_w")
        load_weight(b, dram["wq"], wb[:, 0:8192], 8, sw)
        load_weight(b, dram["wo"], wb[:, 8192:16384], 8, sw)
        wqwo_ld = sw.n
        load_x(b, dram["x"], x_sb)
        sld = b.sem("s_ld")
        kv_ready = emit_kv_sum(b, dram["kvp_all"], kvb2)

        xt_ready = emit_ln(b, "ln1", x_sb, xt, consts["c_eye"],
                           consts["c_misc"], (sld, sld.n))
        x1_done = emit_attn_a_main(b, xt, x_sb, wb[:, 0:8192],
                                   wb[:, 8192:16384], kvb2,
                                   consts["c_onesf"], consts["c_ones"],
                                   xt_ready, kv_ready,
                                   (sw, wqwo_ld), (sw, wqwo_ld))
        load_weight(b, dram["gwt"], gwt_sb, 8, sw)
        gwt_ld = sw.n
        xt2_ready = emit_ln(b, "ln2", x_sb, xt, consts["c_eye"],
                            consts["c_misc"], x1_done)
        dram_w = {k: dram[k] for k in
                  ("e_win", "e_vin", "e_wout", "ws_in", "vs_in", "ws_out")}
        dram_w = dict(dram_w)
        dram_w["e_win"] = [dram["e_win"][e] for e in range(NE)]
        dram_w["e_vin"] = [dram["e_vin"][e] for e in range(NE)]
        dram_w["e_wout"] = [dram["e_wout"][e] for e in range(NE)]
        if stage == "x1":
            sst = b.sem("s_x2")
            b.wait("sy", *x1_done)
            b.op("sy", lambda e: e.dma_start(
                out=x2_d.rearrange("(t p) d -> p t d", p=128),
                in_=x_sb[:, :].rearrange("p (t d) -> p t d", t=NT)), dinc=sst)
            b.wait("sy", sst)
            block = stack.enter_context(nc.Block())
            b.play(block)
            return nc
        x2_done = emit_ffn(b, "f1", xt, x_sb, wb, gwt_sb, dram_w, consts,
                           xt2_ready, (sw, gwt_ld), x1_done)
        b.wait("gp", *x2_done)
        load_weight(b, dram["wkT"], wb[:, 0:8192], 8, sw)
        load_weight(b, dram["wvT"], wb[:, 8192:16384], 8, sw)
        b.op("gp", lambda e: e.dma_start(out=bkt_sb[:, :],
                                         in_=dram["bk_t"]), dinc=sw)
        b.op("gp", lambda e: e.dma_start(out=bvrow_sb[:, :],
                                         in_=dram["bv_row"]), dinc=sw)
        kw_ld = sw.n
        xt3_ready = emit_ln(b, "ln3", x_sb, xt, consts["c_eye"],
                            consts["c_misc"], x2_done, mask_sb=mask_sb)
        mask_ready = (b.sems["ln3v"], b.sems["ln3v"].n)
        kp_done = emit_kvpayload(b, xt, wb[:, 0:8192], wb[:, 8192:16384],
                                 bkt_sb, bvrow_sb, mask_sb, consts, kt_d,
                                 v65_d, xt3_ready, (sw, kw_ld), mask_ready)
        sst = b.sem("s_x2")
        b.wait("sy", *x2_done)
        b.op("sy", lambda e: e.dma_start(
            out=x2_d.rearrange("(t p) d -> p t d", p=128),
            in_=x_sb[:, :].rearrange("p (t d) -> p t d", t=NT)), dinc=sst)
        b.wait("sy", sst)
        b.wait("gp", sst)
        b.wait("gp", *kp_done)

        block = stack.enter_context(nc.Block())
        b.play(block)
    return nc



def _shard_inputs(inputs):
    x = np.ascontiguousarray(
        np.asarray(inputs["L0_attn_a_x"], np.float32).reshape(2 * 8192, D))
    return [np.ascontiguousarray(x[c * TC:(c + 1) * TC]) for c in range(8)]


def run_launch_a(inputs):
    xs = _shard_inputs(inputs)
    consts = _host_consts()
    nc = build_launch_a()
    in_maps = [{"x": xs[c],
                "wk": np.asarray(inputs["L0_attn_a_Wk"], np.float32),
                "wv": np.asarray(inputs["L0_attn_a_Wv"], np.float32),
                "c_eye": consts["c_eye"], "c_misc": consts["c_misc"]}
               for c in range(8)]
    res = run_bass_kernel_spmd(nc, in_maps, core_ids=list(range(8)))
    return [np.asarray(r["kvp"], np.float32) for r in res.results]


def _b_inmaps(inputs, xs, kvps):
    consts = _host_consts()
    f32 = lambda k: np.asarray(inputs[k], np.float32)
    T = lambda a: np.ascontiguousarray(np.asarray(a, np.float32).T)
    shared = {
        "wq": f32("L0_attn_a_Wq"), "wo": f32("L0_attn_a_Wo"),
        "gwt": T(inputs["L0_ffn_gate_w"]),
        "ws_in": f32("L0_ffn_ws_in"), "vs_in": f32("L0_ffn_vs_in"),
        "ws_out": f32("L0_ffn_ws_out"),
        "e_win": f32("L0_ffn_e_win"), "e_vin": f32("L0_ffn_e_vin"),
        "e_wout": f32("L0_ffn_e_wout"),
        "wkT": T(inputs["L0_attn_b_wk_w"]), "wvT": T(inputs["L0_attn_b_wv_w"]),
        "bk_t": np.ascontiguousarray(
            np.asarray(inputs["L0_attn_b_wk_b"], np.float32).reshape(8, 128).T),
        "bv_row": np.asarray(inputs["L0_attn_b_wv_b"],
                             np.float32).reshape(1, D),
    }
    shared.update({k: consts[k] for k in ("c_eye", "c_misc", "c_onesf",
                                          "c_erep", "c_hm2", "c_ones")})
    maps = []
    for c in range(8):
        bb = c // 4
        kvp_all = np.concatenate([kvps[bb * 4 + r] for r in range(4)], axis=0)
        m = dict(shared)
        m["x"] = xs[c]
        m["kvp_all"] = np.ascontiguousarray(kvp_all)
        maps.append(m)
    return maps


def run_launch_b(inputs, kvps, stage="full"):
    xs = _shard_inputs(inputs)
    nc = build_launch_b(stage)
    res = run_bass_kernel_spmd(nc, _b_inmaps(inputs, xs, kvps),
                               core_ids=list(range(8)))
    x2 = [np.asarray(r["x2"], np.float32) for r in res.results]
    if stage == "x1":
        return x2, None, None
    kt = [np.asarray(r["kt"]).view(bf16) if r["kt"].dtype != bf16
          else np.asarray(r["kt"]) for r in res.results]
    v65 = [np.asarray(r["v65"]) for r in res.results]
    return x2, kt, v65



def emit_attn_b(b, xt, x_sb, wb, bqt_sb, borow_sb, kt_full, v65_full, consts,
                xt_ready, wq_ready, wo_ready, kt_rows=1026, nshard=4):
    sp = b.sem("ab_p")
    sv = b.sem("ab_v")
    ss = b.sem("ab_s")
    sd = b.sem("ab_d")
    skv = b.sem("ab_kv")
    scd = b.sem("ab_cd")
    eye = consts["c_eye"]
    onesb = consts["c_ones"]
    onesf = consts["c_onesf"]
    with ExitStack() as ps:
        qt = b.sbuf("ab_qt", [128, 16384], BF16, stack=ps)
        at_qc = b.sbuf("ab_at", [128, 4096], BF16, stack=ps)
        pt = b.sbuf("ab_pt", [128, 2 * 1024], BF16, stack=ps)
        NKV = 6
        ktc = b.sbuf("ab_ktc", [128, NKV * 256], BF16, stack=ps)
        vc = b.sbuf("ab_vc", [128, NKV * 260], BF16, stack=ps)
        atst = b.sbuf("ab_atst", [64, 2 * 512], BF16, stack=ps)
        asm = b.sbuf("ab_asm", [128, 1024], F32, stack=ps)
        rrb = b.sbuf("ab_rrb", [1, 1024], BF16, stack=ps)
        qn = b.sbuf("ab_qn", [128, 48], F32, stack=ps)
        ncb = b.sbuf("ab_ncb", [16, 16], BF16, stack=ps)
        ncrow = b.sbuf("ab_ncrow", [1, 8], BF16, stack=ps)
        negc = b.sbuf("ab_negc", [128, 8], F32, stack=ps)

        with ExitStack() as qs:
            pq = [b.psum(f"ab_pq{i}", [128, 512], stack=qs) for i in range(2)]
            pn = b.psum("ab_pn", [2, 512], stack=qs)
            ptr8 = b.psum("ab_ptr8", [16, 128], BF16, stack=qs)
            pnc = b.psum("ab_pnc", [128, 8], stack=qs)

            b.wait("te", *xt_ready)
            b.wait("te", *wq_ready)
            tok_pq = [0, 0]
            tok_sq = 0
            tok_pn = 0
            for oc in range(ND):
                for qc in range(4):
                    par = (oc * 4 + qc) % 2
                    b.wait("te", ss, tok_pq[par])
                    for kic in range(ND):
                        b.op("te", lambda te, oc=oc, qc=qc, kic=kic, par=par:
                             te.matmul(
                                 pq[par][:, :],
                                 wb[:, kic * 1024 + oc * 128:kic * 1024 + (oc + 1) * 128],
                                 xt[:, kic * 2048 + qc * 512:kic * 2048 + (qc + 1) * 512],
                                 start=(kic == 0), stop=(kic == ND - 1)),
                             inc=(sp if kic == ND - 1 else None))
                    b.wait("sc", sp)
                    b.op("sc", lambda s, oc=oc, qc=qc, par=par: s.activation(
                        qt[:, oc * 2048 + qc * 512:oc * 2048 + (qc + 1) * 512],
                        pq[par][:, :], AF.Identity,
                        bias=bqt_sb[:, oc:oc + 1]), inc=ss)
                    tok_pq[par] = ss.n
                    b.wait("sc", sp, tok_sq)
                    b.opf("sc", lambda s, oc=oc, qc=qc: s.activation(
                        asm[:, 0:512],
                        qt[:, oc * 2048 + qc * 512:oc * 2048 + (qc + 1) * 512],
                        AF.Square))
                    b.wait("te", b.selfsem["sc"])
                    b.wait("te", b.selfsem["ve"], tok_pn)
                    b.op("te", lambda te: te.matmul(
                        pn[:, :], consts["c_hm2"][:, :], asm[:, 0:512],
                        start=True, stop=True), inc=sp)
                    tok_sq = sp.n
                    b.wait("ve", sp)
                    if qc == 0:
                        b.opf("ve", lambda v, oc=oc: v.reduce_max(
                            qn[0:2, 16 + oc:17 + oc], pn[:, :], AX.X))
                    else:
                        b.opf("ve", lambda v: v.reduce_max(
                            qn[0:2, 32:33], pn[:, :], AX.X))
                        b.opf("ve", lambda v, oc=oc: v.tensor_max(
                            qn[0:2, 16 + oc:17 + oc], qn[0:2, 16 + oc:17 + oc],
                            qn[0:2, 32:33]))
                    tok_pn = b.selfsem["ve"].n
            b.opf("ve", lambda v: v.tensor_copy(ncb[0:2, 0:8],
                                                qn[0:2, 16:24]))
            b.wait("te", b.selfsem["ve"])
            b.op("te", lambda te: te.transpose(ptr8[0:8, 0:2], ncb[0:2, 0:8],
                                               eye[0:2, 0:2]), inc=sp)
            b.wait("ve", sp)
            b.opf("ve", lambda v: v.tensor_copy(qn[0:8, 33:35],
                                                ptr8[0:8, 0:2]))
            b.opf("ve", lambda v: v.reduce_max(qn[0:8, 35:36],
                                               qn[0:8, 33:35], AX.X))
            if nshard == 4:
                knsrc = [kt_full[r * kt_rows + 1024:r * kt_rows + 1025, 0:8]
                         for r in range(4)]
            else:
                knsrc = [kt_full[s * kt_rows + 1024:s * kt_rows + 1025,
                                 g * 2048:g * 2048 + 8]
                         for s in range(2) for g in range(2)]
            for r in range(4):
                b.op("sy", lambda e, r=r: e.dma_start(
                    out=ncb[0:8, 8 + r:9 + r],
                    in_=knsrc[r].rearrange("a (c u) -> (a c) u", u=1)),
                    dinc=scd)
            b.wait("ve", scd)
            b.opf("ve", lambda v: v.tensor_copy(qn[0:8, 36:40],
                                                ncb[0:8, 8:12]))
            b.opf("ve", lambda v: v.reduce_max(qn[0:8, 40:41],
                                               qn[0:8, 36:40], AX.X))
            b.opf("ve", lambda v: v.tensor_mul(qn[0:8, 41:42],
                                               qn[0:8, 40:41],
                                               qn[0:8, 35:36]))
            b.wait("sc", b.selfsem["ve"])
            b.opf("sc", lambda s: s.activation(qn[0:8, 42:43],
                                               qn[0:8, 41:42], AF.Ln))
            b.opf("sc", lambda s: s.activation(
                qn[0:8, 43:44], qn[0:8, 42:43], AF.Exp, scale=0.5,
                bias=consts["c_misc"][0:8, 1:2]))
            b.wait("ve", b.selfsem["sc"])
            b.opf("ve", lambda v: v.tensor_single_scalar(
                qn[0:8, 44:45], qn[0:8, 43:44], -1.0, ALU.mult))
            b.opf("ve", lambda v: v.tensor_copy(ncb[0:8, 12:13],
                                                qn[0:8, 44:45]))
            b.wait("te", b.selfsem["ve"])
            b.op("te", lambda te: te.transpose(ptr8[0:1, 0:8],
                                               ncb[0:8, 12:13],
                                               eye[0:8, 0:8]), inc=sp)
            b.wait("ve", sp)
            b.opf("ve", lambda v: v.tensor_copy(ncrow[0:1, 0:8],
                                                ptr8[0:1, 0:8]))
            b.wait("te", b.selfsem["ve"])
            b.op("te", lambda te: te.matmul(
                pnc[:, :], onesb[0:1, 0:128], ncrow[0:1, 0:8],
                start=True, stop=True), inc=sp)
            b.wait("ve", sp)
            b.opf("ve", lambda v: v.tensor_copy(negc[:, :], pnc[:, :]))
            negc_done = b.selfsem["ve"].n

        pss = [b.psum(f"ab_ps{i}", [128, 1024], stack=ps) for i in range(2)]
        pvA = b.psum("ab_pvA", [65, 512], stack=ps)
        pvB = b.psum("ab_pvB", [65, 512], stack=ps)
        po = [b.psum(f"ab_po{i}", [128, 512], stack=ps) for i in range(2)]

        b.wait("sc", b.selfsem["ve"], negc_done)
        tok_kv = [0] * 6
        tok_ps = [0, 0]
        tok_pt = [0, 0]
        tok_atst = [0, 0]
        tok_at = 0
        tok_po = [0, 0]
        nhalf = 0
        for qc in range(4):
            for hp in range(8):
                halves = [(sc2, si) for sc2 in range(32) for si in range(2)]
                exp_tok = {}
                pend_pv = None
                for idx in range(len(halves) + 1):
                    if idx < len(halves):
                        sc2, si = halves[idx]
                        cpar = sc2 % NKV
                        if si == 0:
                            if nshard == 4:
                                rb = sc2 // 8
                                lc = (sc2 * 256) % 2048
                            else:
                                rb = sc2 // 16
                                lc = (sc2 * 256) % 4096
                            b.wait("sy", sp, tok_kv[cpar])
                            b.op("sy", lambda e, rb=rb, lc=lc, hp=hp,
                                 cpar=cpar: e.dma_start(
                                     out=ktc[:, cpar * 256:(cpar + 1) * 256],
                                     in_=kt_full[rb * kt_rows + hp * 128:
                                                 rb * kt_rows + (hp + 1) * 128,
                                                 lc:lc + 256]), dinc=skv)
                            b.wait("gp", sp, tok_kv[cpar])
                            b.op("gp", lambda e, sc2=sc2, hp=hp, cpar=cpar:
                                 e.dma_start(
                                     out=vc[:, cpar * 260:(cpar + 1) * 260]
                                     .rearrange("p (s c) -> p s c", s=2),
                                     in_=v65_full[sc2 * 256:(sc2 + 1) * 256,
                                                  hp * 130:(hp + 1) * 130]
                                     .rearrange("(s p) c -> p s c", p=128)),
                                 dinc=skv)
                        kv_ok = skv.n
                        spar = nhalf % 2
                        nhalf += 1
                        b.wait("te", skv, kv_ok)
                        b.wait("te", ss, tok_ps[spar])
                        for j in range(2):
                            b.op("te", lambda te, si=si, j=j, hp=hp, qc=qc,
                                 cpar=cpar, spar=spar: te.matmul(
                                     pss[spar][:, j * 512:(j + 1) * 512],
                                     ktc[j * 64:(j + 1) * 64,
                                         cpar * 256 + si * 128:
                                         cpar * 256 + (si + 1) * 128],
                                     qt[j * 64:(j + 1) * 64,
                                        hp * 2048 + qc * 512:
                                        hp * 2048 + (qc + 1) * 512],
                                     start=True, stop=True),
                                 inc=(sp if j == 1 else None))
                        s_tok = sp.n
                    if pend_pv is not None:
                        psi, pcpar, pspar, pfirst, plast = pend_pv
                        b.wait("te", ss, exp_tok[idx - 1])
                        b.op("te", lambda te, si=psi, cpar=pcpar, spar=pspar,
                             first=pfirst, last=plast: te.matmul(
                                 pvA[:, :],
                                 vc[:, cpar * 260 + si * 130:
                                    cpar * 260 + si * 130 + 65],
                                 pt[:, spar * 1024:spar * 1024 + 512],
                                 start=first, stop=last))
                        b.op("te", lambda te, si=psi, cpar=pcpar, spar=pspar,
                             first=pfirst, last=plast: te.matmul(
                                 pvB[:, :],
                                 vc[:, cpar * 260 + si * 130 + 65:
                                    cpar * 260 + si * 130 + 130],
                                 pt[:, spar * 1024 + 512:spar * 1024 + 1024],
                                 start=first, stop=plast), inc=sp)
                        tok_pt[pspar] = sp.n
                        tok_kv[pcpar] = sp.n
                    if idx < len(halves):
                        b.wait("sc", sp, s_tok)
                        b.wait("sc", sp, tok_pt[spar])
                        b.op("sc", lambda s, spar=spar, hp=hp: s.activation(
                            pt[:, spar * 1024:(spar + 1) * 1024],
                            pss[spar][:, :], AF.Exp, scale=0.125,
                            bias=negc[:, hp:hp + 1]), inc=ss)
                        tok_ps[spar] = ss.n
                        exp_tok[idx] = ss.n
                        pend_pv = (si, cpar, spar,
                                   idx == 0, idx == len(halves) - 1)
                pend_pv = None
                for j, pv in ((0, pvA), (1, pvB)):
                    b.wait("ve", sp)
                    b.opf("ve", lambda v, j=j, pv=pv: _recip16(b, v,
                        rrb[0:1, j * 512:(j + 1) * 512], pv[64:65, :]))
                    b.wait("te", b.selfsem["ve"])
                    b.wait("te", ss)
                    b.op("te", lambda te, j=j: te.matmul(
                        pss[0][0:64, j * 512:(j + 1) * 512],
                        onesb[0:1, 0:64],
                        rrb[0:1, j * 512:(j + 1) * 512],
                        start=True, stop=True), inc=sp)
                    b.wait("sc", sp)
                    b.op("sc", lambda s, j=j: s.copy(
                        asm[0:64, j * 512:(j + 1) * 512],
                        pss[0][0:64, j * 512:(j + 1) * 512]), inc=ss)
                    b.wait("ve", ss)
                    b.wait("ve", sd, tok_atst[j])
                    b.op("ve", lambda v, j=j, pv=pv: v.tensor_tensor(
                        atst[:, j * 512:(j + 1) * 512], pv[0:64, :],
                        asm[0:64, j * 512:(j + 1) * 512], ALU.mult), inc=sv)
                    b.wait("sy", sv)
                    b.wait("sy", sp, tok_at)
                    b.op("sy", lambda e, j=j, hp=hp: e.dma_start(
                        out=at_qc[j * 64:(j + 1) * 64,
                                  hp * 512:(hp + 1) * 512],
                        in_=atst[:, j * 512:(j + 1) * 512]), dinc=sd)
                    tok_atst[j] = sd.n
                tok_ps[0] = ss.n
            b.wait("te", sd)
            b.wait("te", *wo_ready)
            for u in range(4):
                for dn in range(2):
                    pr = (4 * qc + u * 2 + dn) % 2
                    b.wait("te", sv, tok_po[pr])
                    for kdc in range(ND):
                        b.op("te", lambda te, kdc=kdc, u=u, dn=dn, pr=pr:
                             te.matmul(
                                 po[pr][:, :],
                                 at_qc[:, kdc * 512 + u * 128:kdc * 512 + (u + 1) * 128],
                                 wb[:, 8192 + kdc * 1024 + dn * 512:
                                    8192 + kdc * 1024 + (dn + 1) * 512],
                                 start=(kdc == 0), stop=False))
                    b.op("te", lambda te, dn=dn, pr=pr: te.matmul(
                        po[pr][:, :], onesb[0:1, 0:128],
                        borow_sb[0:1, dn * 512:(dn + 1) * 512],
                        start=False, stop=True), inc=sp)
                    b.wait("ve", sp)
                    b.op("ve", lambda v, qc=qc, u=u, dn=dn, pr=pr:
                         v.tensor_tensor(
                             x_sb[:, (qc * 4 + u) * 1024 + dn * 512:
                                  (qc * 4 + u) * 1024 + (dn + 1) * 512],
                             po[pr][:, :],
                             x_sb[:, (qc * 4 + u) * 1024 + dn * 512:
                                  (qc * 4 + u) * 1024 + (dn + 1) * 512],
                             ALU.add), inc=sv)
                    tok_po[pr] = sv.n
            tok_at = sp.n
    return sv, sv.n


def build_launch_c():
    nc = bass.Bass("TRN2", target_bir_lowering=False)
    with ExitStack() as stack:
        b = B(nc, stack)
        dram = {}
        for nm, shape, dt in [
                ("x2", [TC, D], F32), ("wqT", [D, D], F32),
                ("woT", [D, D], F32), ("bq_t", [128, 8], F32),
                ("bo_row", [1, D], F32), ("gwt", [D, 16], F32),
                ("ws_in", [D, D], F32), ("vs_in", [D, D], F32),
                ("ws_out", [D, D], F32),
                ("e_win", [NE, D, EH], F32), ("e_vin", [NE, D, EH], F32),
                ("e_wout", [NE, EH, D], F32),
                ("kt_full", [4 * 1026, 2048], BF16),
                ("v65_full", [8192, 1040], BF16)]:
            dram[nm] = nc.dram_tensor(nm, shape, dt,
                                      kind="ExternalInput").ap()
        out_d = nc.dram_tensor("out", [TC, D], F32,
                               kind="ExternalOutput").ap()

        consts = decl_consts(b, ["c_eye", "c_misc", "c_onesf", "c_erep",
                                 "c_hm2", "c_ones"])
        x_sb = b.sbuf("x_sbuf", [128, NT * 1024], F32)
        xt = b.sbuf("xt_sbuf", [128, ND * 2048], BF16)
        wb = b.sbuf("wb_sbuf", [128, 16384], BF16)
        gwt_sb = b.sbuf("gwt_sb", [128, 128], BF16)
        bqt_sb = b.sbuf("bqt_sb", [128, 8], F32)
        borow_sb = b.sbuf("borow_sb", [1, 1024], BF16)

        sw = b.sem("s_w")
        load_weight(b, dram["wqT"], wb[:, 0:8192], 8, sw)
        load_weight(b, dram["woT"], wb[:, 8192:16384], 8, sw)
        b.op("gp", lambda e: e.dma_start(out=bqt_sb[:, :],
                                         in_=dram["bq_t"]), dinc=sw)
        b.op("gp", lambda e: e.dma_start(out=borow_sb[:, :],
                                         in_=dram["bo_row"]), dinc=sw)
        load_weight(b, dram["gwt"], gwt_sb, 8, sw)
        w_ld = sw.n
        load_x(b, dram["x2"], x_sb)
        sld = b.sem("s_ld")

        xt3_ready = emit_ln(b, "ln3", x_sb, xt, consts["c_eye"],
                            consts["c_misc"], (sld, sld.n))
        x3_done = emit_attn_b(b, xt, x_sb, wb, bqt_sb, borow_sb,
                              dram["kt_full"], dram["v65_full"], consts,
                              xt3_ready, (sw, w_ld), (sw, w_ld))
        xt4_ready = emit_ln(b, "ln4", x_sb, xt, consts["c_eye"],
                            consts["c_misc"], x3_done)
        dram_w = {"ws_in": dram["ws_in"], "vs_in": dram["vs_in"],
                  "ws_out": dram["ws_out"],
                  "e_win": [dram["e_win"][e] for e in range(NE)],
                  "e_vin": [dram["e_vin"][e] for e in range(NE)],
                  "e_wout": [dram["e_wout"][e] for e in range(NE)]}
        out_done = emit_ffn(b, "f2", xt, x_sb, wb, gwt_sb, dram_w, consts,
                            xt4_ready, (sw, w_ld), x3_done)
        sst = b.sem("s_out")
        b.wait("sy", *out_done)
        b.op("sy", lambda e: e.dma_start(
            out=out_d.rearrange("(t p) d -> p t d", p=128),
            in_=x_sb[:, :].rearrange("p (t d) -> p t d", t=NT)), dinc=sst)
        b.wait("sy", sst)
        b.wait("gp", sst)

        block = stack.enter_context(nc.Block())
        b.play(block)
    return nc


def _c_inmaps(inputs, x2s, kts, v65s):
    consts = _host_consts()
    f32 = lambda k: np.asarray(inputs[k], np.float32)
    T = lambda a: np.ascontiguousarray(np.asarray(a, np.float32).T)
    shared = {
        "wqT": T(inputs["L0_attn_b_wq_w"]), "woT": T(inputs["L0_attn_b_wo_w"]),
        "bq_t": np.ascontiguousarray(
            np.asarray(inputs["L0_attn_b_wq_b"], np.float32).reshape(8, 128).T),
        "bo_row": np.asarray(inputs["L0_attn_b_wo_b"],
                             np.float32).reshape(1, D),
        "gwt": T(inputs["L0_ffn_gate_w"]),
        "ws_in": f32("L0_ffn_ws_in"), "vs_in": f32("L0_ffn_vs_in"),
        "ws_out": f32("L0_ffn_ws_out"),
        "e_win": f32("L0_ffn_e_win"), "e_vin": f32("L0_ffn_e_vin"),
        "e_wout": f32("L0_ffn_e_wout"),
    }
    shared.update({k: consts[k] for k in ("c_eye", "c_misc", "c_onesf",
                                          "c_erep", "c_hm2", "c_ones")})
    maps = []
    for c in range(8):
        bb = c // 4
        m = dict(shared)
        m["x2"] = x2s[c]
        m["kt_full"] = np.ascontiguousarray(
            np.concatenate([kts[bb * 4 + r] for r in range(4)], axis=0))
        m["v65_full"] = np.ascontiguousarray(
            np.concatenate([v65s[bb * 4 + r] for r in range(4)], axis=0))
        maps.append(m)
    return maps


def run_launch_c(inputs, x2s, kts, v65s):
    nc = build_launch_c()
    res = run_bass_kernel_spmd(nc, _c_inmaps(inputs, x2s, kts, v65s),
                               core_ids=list(range(8)))
    return [np.asarray(r["out"], np.float32) for r in res.results]


def kernel_3launch(**inputs):
    kvps = run_launch_a(inputs)
    x2s, kts, v65s = run_launch_b(inputs, kvps)
    outs = run_launch_c(inputs, x2s, kts, v65s)
    out = np.concatenate(outs, axis=0).reshape(2, 8192, D)
    return out



KT_ROWS = 1026
KT_SH_OFF = KT_ROWS * 4096


def emit_passA(b, x_d, wk_sb, wv_sb, eye_sb, cb, w_ready, kv_sh, regs):
    sv = b.sem("pAv")
    ss = b.sem("pAs")
    sp = b.sem("pAp")
    skv = b.sem("pAkv")
    sx = b.sem("pAx")
    sw = b.sem("pAw")
    with ExitStack() as ps:
        xtile = b.sbuf("pA_x", [128, 2 * 1024], F32, stack=ps)
        stats = b.sbuf("pA_st", [128, 20], F32, stack=ps)
        lnst = b.sbuf("pA_ln", [128, 2 * 1024], BF16, stack=ps)
        xtt = b.sbuf("pA_xtt", [128, 2 * 1024], BF16, stack=ps)
        kf = b.sbuf("pA_kf", [128, 2 * 1024], BF16, stack=ps)
        v65 = b.sbuf("pA_v65", [128, 2 * 1040], BF16, stack=ps)
        phs = b.sbuf("pA_phs", [128, 2 * 1024], F32, stack=ps)
        kvs = b.sbuf("pA_kvs", [64, 1040], F32, stack=ps)
        ptr0 = b.psum("pA_ptr0", [128, 128], BF16, stack=ps)
        pk = [b.psum(f"pA_pk{i}", [128, 512], stack=ps) for i in range(2)]
        pv = [b.psum(f"pA_pv{i}", [128, 512], stack=ps) for i in range(2)]
        pkv = [b.psum("pA_pkv0", [64, 455], stack=ps),
               b.psum("pA_pkv1", [64, 455], stack=ps),
               b.psum("pA_pkv2", [64, 130], stack=ps)]

        for sl in range(2):
            b.op("gp", lambda e, sl=sl: e.memset(
                bass.AP(v65, sl * 1040 + 64, [[2 * 1040, 128], [65, 16]]),
                1.0), inc=b.sem("pA_ones"))
        so = b.sems["pA_ones"]
        b.wait("ve", so)
        b.wait("te", *w_ready)

        tok_x = [0, 0]
        tok_lnst = [0, 0]
        tok_ptr = [0]
        tok_xtt = [0, 0]
        tok_k = [0, 0]
        tok_v = [0, 0]
        for t in range(32):
            sl = t % 2
            b.wait("sy", ss, tok_x[sl])
            b.op("sy", lambda e, t=t, sl=sl: e.dma_start(
                out=xtile[:, sl * 1024:(sl + 1) * 1024],
                in_=x_d[t * 128:(t + 1) * 128, :]), dinc=sx)
            b.wait("ve", sx)
            b.op("ve", lambda v, sl=sl: v.bn_stats(
                stats[:, 0:6], xtile[:, sl * 1024:sl * 1024 + 512]))
            b.opf("ve", lambda v, sl=sl: v.bn_stats(
                stats[:, 6:12], xtile[:, sl * 1024 + 512:sl * 1024 + 1024]))
            b.op("ve", lambda v: v.bn_aggr(stats[:, 12:14], stats[:, 0:12]),
                 inc=sv)
            b.wait("sc", sv)
            b.opf("sc", lambda s: s.activation(
                stats[:, 14:15], stats[:, 13:14], AF.Ln, bias=cb[:, 0:1]))
            b.op("sc", lambda s: s.activation(
                stats[:, 15:16], stats[:, 14:15], AF.Exp, scale=-0.5),
                inc=ss)
            b.wait("ve", ss)
            b.op("ve", lambda v: v.scalar_tensor_tensor(
                stats[:, 16:17], stats[:, 12:13], -1.0, stats[:, 15:16],
                ALU.mult, ALU.mult), inc=sv)
            b.wait("sc", sv)
            b.wait("sc", sp, tok_lnst[sl])
            b.op("sc", lambda s, sl=sl: s.activation(
                lnst[:, sl * 1024:(sl + 1) * 1024],
                xtile[:, sl * 1024:(sl + 1) * 1024], AF.Identity,
                bias=stats[:, 16:17], scale=stats[:, 15:16]), inc=ss)
            tok_x[sl] = ss.n
            b.wait("te", ss)
            b.wait("te", sv, tok_xtt[sl])
            for dc in range(ND):
                b.wait("te", sv, tok_ptr[0])
                b.op("te", lambda te, sl=sl, dc=dc: te.transpose(
                    ptr0[:, :],
                    lnst[:, sl * 1024 + dc * 128:sl * 1024 + (dc + 1) * 128],
                    eye_sb[:, :]), inc=sp)
                b.wait("ve", sp)
                b.op("ve", lambda v, sl=sl, dc=dc: v.tensor_copy(
                    xtt[:, sl * 1024 + dc * 128:sl * 1024 + (dc + 1) * 128],
                    ptr0[:, :]), inc=sv)
                tok_ptr[0] = sv.n
            tok_lnst[sl] = sp.n
            xtt_ok = sv.n
            if t >= 2:
                b.wait("ve", skv, t - 1)
            for dn in range(2):
                b.wait("te", sv, tok_k[dn])
                b.wait("te", sv, xtt_ok)
                for kdc in range(ND):
                    b.op("te", lambda te, dn=dn, kdc=kdc, sl=sl: te.matmul(
                        pk[dn][:, :],
                        xtt[:, sl * 1024 + kdc * 128:sl * 1024 + (kdc + 1) * 128],
                        wk_sb[:, kdc * 1024 + dn * 512:kdc * 1024 + (dn + 1) * 512],
                        start=(kdc == 0), stop=(kdc == ND - 1)),
                        inc=(sp if kdc == ND - 1 else None))
                b.wait("ve", sp)
                b.op("ve", lambda v, dn=dn, sl=sl: v.tensor_single_scalar(
                    phs[:, sl * 1024 + dn * 512:sl * 1024 + (dn + 1) * 512],
                    pk[dn][:, :], 0.0, ALU.min), inc=sv)
                b.wait("sc", sv)
                b.op("sc", lambda s, dn=dn, sl=sl: s.activation(
                    phs[:, sl * 1024 + dn * 512:sl * 1024 + (dn + 1) * 512],
                    phs[:, sl * 1024 + dn * 512:sl * 1024 + (dn + 1) * 512],
                    AF.Exp), inc=ss)
                b.wait("ve", ss)
                b.op("ve", lambda v, dn=dn, sl=sl: v.scalar_tensor_tensor(
                    kf[:, sl * 1024 + dn * 512:sl * 1024 + (dn + 1) * 512],
                    pk[dn][:, :], 0.0,
                    phs[:, sl * 1024 + dn * 512:sl * 1024 + (dn + 1) * 512],
                    ALU.max, ALU.add), inc=sv)
                tok_k[dn] = sv.n
            for dn in range(2):
                b.wait("te", sv, tok_v[dn])
                for kdc in range(ND):
                    b.op("te", lambda te, dn=dn, kdc=kdc, sl=sl: te.matmul(
                        pv[dn][:, :],
                        xtt[:, sl * 1024 + kdc * 128:sl * 1024 + (kdc + 1) * 128],
                        wv_sb[:, kdc * 1024 + dn * 512:kdc * 1024 + (dn + 1) * 512],
                        start=(kdc == 0), stop=(kdc == ND - 1)),
                        inc=(sp if kdc == ND - 1 else None))
                b.wait("ve", sp)
                b.op("ve", lambda v, dn=dn, sl=sl: v.tensor_copy(
                    bass.AP(v65, sl * 1040 + dn * 8 * 65,
                            [[2 * 1040, 128], [65, 8], [1, 64]]),
                    bass.AP(pv[dn], 0, [[512, 128], [64, 8], [1, 64]])),
                    inc=sv)
                tok_v[dn] = sv.n
            b.wait("te", sv)
            for h in range(H):
                bank, slot2 = divmod(h, 7)
                b.op("te", lambda te, h=h, sl=sl, bank=bank, slot2=slot2, t=t:
                     te.matmul(
                         pkv[bank][:, slot2 * 65:(slot2 + 1) * 65],
                         kf[:, sl * 1024 + h * 64:sl * 1024 + (h + 1) * 64],
                         v65[:, sl * 1040 + h * 65:sl * 1040 + (h + 1) * 65],
                         start=(t == 0 and slot2 == 0), stop=(t == 31)),
                     inc=(skv if h == H - 1 else None))
            tok_xtt[sl] = sp.n
        b.wait("ve", skv)
        b.op("ve", lambda v: v.tensor_copy(kvs[:, 0:455], pkv[0][:, :]))
        b.op("ve", lambda v: v.tensor_copy(kvs[:, 455:910], pkv[1][:, :]))
        b.op("ve", lambda v: v.tensor_copy(kvs[:, 910:1040], pkv[2][:, :]),
             inc=sv)
        b.wait("gp", sv)
        b.op_branch("gp", regs["slot"],
                    lambda e: e.dma_start(out=kv_sh[0:64, :], in_=kvs[:, :]),
                    lambda e: e.dma_start(out=kv_sh[64:128, :], in_=kvs[:, :]),
                    dinc=sw)
        b.wait("gp", sw)
    return sw, sw.n


def emit_barrier(b, k, nonce_sb, shf, fbuf, regs, done):
    pol = b.sem(f"bar{k}")
    b.op_branch("gp", regs["slot"],
                lambda e, k=k: e.dma_start(
                    out=shf[0:1, :], in_=nonce_sb[0:1, k * 64:(k + 1) * 64]),
                lambda e, k=k: e.dma_start(
                    out=shf[1:2, :], in_=nonce_sb[0:1, k * 64:(k + 1) * 64]),
                dinc=pol)
    b.wait("gp", pol)
    rk = regs["rk"]
    rf0 = regs["rf0"]
    rf1 = regs["rf1"]
    rgo = regs["rgo"]

    def poll(gp, k=k, pol=pol.h):
        gp.reg_load(rk, nonce_sb[0:1, k * 64:k * 64 + 1])
        gp.reg_mov(rgo, 1)
        with gp.While(rgo):
            gp.dma_start(out=fbuf[0:1, 0:1], in_=shf[0:1, 0:1]).then_inc(
                pol, 16)
            gp.dma_start(out=fbuf[0:1, 1:2], in_=shf[1:2, 0:1]).then_inc(
                pol, 16)
            gp.wait_ge(pol, 48)
            gp.reg_load(rf0, fbuf[0:1, 0:1])
            gp.reg_load(rf1, fbuf[0:1, 1:2])
            gp.reg_alu(rf0, rf0, rk, ALU.not_equal)
            gp.reg_alu(rf1, rf1, rk, ALU.not_equal)
            gp.reg_alu(rgo, rf0, rf1, ALU.add)
    pol.n += 32 * 2
    b.q["gp"].append(poll)
    b.op("gp", lambda e: e.sem_inc(done.h, 1))
    done.n += 1
    return done, done.n


def emit_kv_merge(b, kv_sh, kvb2, barrier_done):
    sa = b.sem("kvm_a")
    sv = b.sem("kvm_v")
    sd = b.sem("kvm_d")
    kvs = b.sbuf("kvm_f", [64, 1040], F32)
    b.wait("gp", *barrier_done)
    b.op("gp", lambda e: e.dma_start(out=kvs[:, :], in_=kv_sh[0:64, :]),
         dinc=sa)
    b.wait("gp", sa)
    b.op("gp", lambda e: e.dma_start(out=kvs[:, :], in_=kv_sh[64:128, :],
                                     accum_op=ALU.add), dinc=sa)
    b.wait("ve", sa)
    b.op("ve", lambda v: v.tensor_copy(kvb2[0:64, :], kvs[:, :]), inc=sv)
    b.wait("sy", sv)
    b.op("sy", lambda e: e.dma_start(out=kvb2[64:128, :], in_=kvb2[0:64, :]),
         dinc=sd)
    return sd, sd.n


def emit_kvpayload_sh(b, tag, g, xt, wkT_sb, wvT_sb, bkt_sb, bvrow_sb,
                      mask_sb, consts, kt_sh, v65_sh, regs, xt_ready,
                      w_ready, mask_ready):
    sp = b.sem(tag + "p")
    sv = b.sem(tag + "v")
    ss = b.sem(tag + "s")
    sd = b.sem(tag + "d")
    eye = consts["c_eye"]
    with ExitStack() as ps:
        ktile = b.sbuf(tag + "_kt", [128, 2 * 512], BF16, stack=ps)
        ksq = b.sbuf(tag + "_ksq", [128, 512], F32, stack=ps)
        kn2 = b.sbuf(tag + "_kn2", [128, 32], F32, stack=ps)
        kn2b = b.sbuf(tag + "_kn2b", [16, 16], BF16, stack=ps)
        v65t = b.sbuf(tag + "_v65", [128, 2 * 1040], BF16, stack=ps)
        pkb = [b.psum(tag + f"_pk{i}", [128, 512], stack=ps)
               for i in range(2)]
        pn = b.psum(tag + "_pn", [2, 512], stack=ps)
        ptr8 = b.psum(tag + "_ptr8", [16, 128], BF16, stack=ps)

        b.wait("te", *xt_ready)
        b.wait("te", *w_ready)
        tok_pkb = [0, 0]
        tok_dma = [0, 0]
        tok_ksq = 0
        tok_pn = 0
        for oc in range(ND):
            for tc in range(4):
                par = (oc * 4 + tc) % 2
                b.wait("te", ss, tok_pkb[par])
                for kic in range(ND):
                    b.op("te", lambda te, oc=oc, tc=tc, kic=kic, par=par:
                         te.matmul(
                             pkb[par][:, :],
                             wkT_sb[:, kic * 1024 + oc * 128:kic * 1024 + (oc + 1) * 128],
                             xt[:, kic * 2048 + tc * 512:kic * 2048 + (tc + 1) * 512],
                             start=(kic == 0), stop=(kic == ND - 1)),
                         inc=(sp if kic == ND - 1 else None))
                b.wait("sc", sp)
                b.wait("sc", sd, tok_dma[par])
                b.op("sc", lambda s, oc=oc, par=par: s.activation(
                    ktile[:, par * 512:(par + 1) * 512], pkb[par][:, :],
                    AF.Identity, bias=bkt_sb[:, oc:oc + 1]), inc=ss)
                tok_pkb[par] = ss.n
                b.wait("gp", ss)
                b.op_branch(
                    "gp", regs["slot"],
                    lambda e, oc=oc, tc=tc, par=par: e.dma_start(
                        out=kt_sh[oc * 128:(oc + 1) * 128,
                                  g * 2048 + tc * 512:g * 2048 + (tc + 1) * 512],
                        in_=ktile[:, par * 512:(par + 1) * 512]),
                    lambda e, oc=oc, tc=tc, par=par: e.dma_start(
                        out=kt_sh[KT_ROWS + oc * 128:KT_ROWS + (oc + 1) * 128,
                                  g * 2048 + tc * 512:g * 2048 + (tc + 1) * 512],
                        in_=ktile[:, par * 512:(par + 1) * 512]), dinc=sd)
                tok_dma[par] = sd.n
                b.wait("sc", sp, tok_ksq)
                b.opf("sc", lambda s, par=par: s.activation(
                    ksq[:, :], ktile[:, par * 512:(par + 1) * 512],
                    AF.Square))
                b.wait("te", b.selfsem["sc"])
                b.wait("te", b.selfsem["ve"], tok_pn)
                b.op("te", lambda te: te.matmul(
                    pn[:, :], consts["c_hm2"][:, :], ksq[:, :],
                    start=True, stop=True), inc=sp)
                tok_ksq = sp.n
                b.wait("ve", sp)
                if tc == 0:
                    b.opf("ve", lambda v, oc=oc: v.reduce_max(
                        kn2[0:2, oc:oc + 1], pn[:, :], AX.X))
                else:
                    b.opf("ve", lambda v: v.reduce_max(kn2[0:2, 16:17],
                                                       pn[:, :], AX.X))
                    b.opf("ve", lambda v, oc=oc: v.tensor_max(
                        kn2[0:2, oc:oc + 1], kn2[0:2, oc:oc + 1],
                        kn2[0:2, 16:17]))
                tok_pn = b.selfsem["ve"].n
        b.opf("ve", lambda v: v.tensor_copy(kn2b[0:2, 0:8], kn2[0:2, 0:8]))
        b.wait("te", b.selfsem["ve"])
        b.op("te", lambda te: te.transpose(ptr8[0:8, 0:2], kn2b[0:2, 0:8],
                                           eye[0:2, 0:2]), inc=sp)
        b.wait("ve", sp)
        b.opf("ve", lambda v: v.tensor_copy(kn2[0:8, 20:22], ptr8[0:8, 0:2]))
        b.opf("ve", lambda v: v.reduce_max(kn2[0:8, 22:23], kn2[0:8, 20:22],
                                           AX.X))
        b.opf("ve", lambda v: v.tensor_copy(kn2b[0:8, 8:9], kn2[0:8, 22:23]))
        b.wait("gp", b.selfsem["ve"])
        b.op_branch(
            "gp", regs["slot"],
            lambda e: e.dma_start(
                out=kt_sh[1024:1025, g * 2048:g * 2048 + 8]
                .rearrange("a (c u) -> (a c) u", u=1),
                in_=kn2b[0:8, 8:9]),
            lambda e: e.dma_start(
                out=kt_sh[KT_ROWS + 1024:KT_ROWS + 1025, g * 2048:g * 2048 + 8]
                .rearrange("a (c u) -> (a c) u", u=1),
                in_=kn2b[0:8, 8:9]), dinc=sd)
        tok_pv = [0, 0]
        tok_v65dma = [0, 0]
        b.wait("te", ss)
        for t in range(NT):
            for dn in range(2):
                par = dn
                b.wait("te", sv, tok_pv[par])
                for kic in range(ND):
                    b.op("te", lambda te, t=t, dn=dn, kic=kic, par=par:
                         te.matmul(
                             pkb[par][:, :],
                             xt[:, kic * 2048 + t * 128:kic * 2048 + (t + 1) * 128],
                             wvT_sb[:, kic * 1024 + dn * 512:kic * 1024 + (dn + 1) * 512],
                             start=(kic == 0), stop=False))
                b.op("te", lambda te, dn=dn, par=par: te.matmul(
                    pkb[par][:, :], consts["c_ones"][0:1, 0:128],
                    bvrow_sb[0:1, dn * 512:(dn + 1) * 512],
                    start=False, stop=True), inc=sp)
                b.wait("ve", sp)
                sl = t % 2
                b.wait("ve", sd, tok_v65dma[sl])
                b.op("ve", lambda v, t=t, dn=dn, par=par, sl=sl:
                     v.tensor_single_scalar(
                         bass.AP(v65t, sl * 1040 + dn * 8 * 65,
                                 [[2 * 1040, 128], [65, 8], [1, 64]]),
                         bass.AP(pkb[par], 0, [[512, 128], [64, 8], [1, 64]]),
                         mask_sb[:, t:t + 1], ALU.mult), inc=sv)
                tok_pv[par] = sv.n
                b.op("ve", lambda v, t=t, dn=dn, sl=sl: v.tensor_copy(
                    bass.AP(v65t, sl * 1040 + dn * 8 * 65 + 64,
                            [[2 * 1040, 128], [65, 8]]),
                    mask_sb[:, t:t + 1].to_broadcast([128, 8])), inc=sv)
            b.wait("gp", sv)
            b.op_branch(
                "gp", regs["slot"],
                lambda e, t=t, sl=t % 2: e.dma_start(
                    out=v65_sh[g * 2048 + t * 128:g * 2048 + (t + 1) * 128, :],
                    in_=v65t[:, sl * 1040:sl * 1040 + 1040]),
                lambda e, t=t, sl=t % 2: e.dma_start(
                    out=v65_sh[4096 + g * 2048 + t * 128:
                               4096 + g * 2048 + (t + 1) * 128, :],
                    in_=v65t[:, sl * 1040:sl * 1040 + 1040]), dinc=sd)
            tok_v65dma[t % 2] = sd.n
        b.wait("gp", sd)
    return sd, sd.n


def build_fused():
    nc = bass.Bass("TRN2", target_bir_lowering=False)
    with ExitStack() as stack:
        b = B(nc, stack)
        dram = {}
        for nm, shape, dt in [
                ("x", [4096, D], F32), ("slotv", [1, 1], I32),
                ("nonces", [1, 128], I32),
                ("wk", [D, D], F32), ("wv", [D, D], F32),
                ("wq", [D, D], F32), ("wo", [D, D], F32),
                ("gwt", [D, 16], F32),
                ("ws_in", [D, D], F32), ("vs_in", [D, D], F32),
                ("ws_out", [D, D], F32),
                ("e_win", [NE, D, EH], F32), ("e_vin", [NE, D, EH], F32),
                ("e_wout", [NE, EH, D], F32),
                ("wkT", [D, D], F32), ("wvT", [D, D], F32),
                ("bk_t", [128, 8], F32), ("bv_row", [1, D], F32),
                ("wqT", [D, D], F32), ("woT", [D, D], F32),
                ("bq_t", [128, 8], F32), ("bo_row", [1, D], F32)]:
            dram[nm] = nc.dram_tensor(nm, shape, dt,
                                      kind="ExternalInput").ap()
        out_d = nc.dram_tensor("out", [TC, D], F32,
                               kind="ExternalOutput").ap()
        kt_sh = nc.dram_tensor("kt_sh", [2 * KT_ROWS, 4096], BF16,
                               kind="Internal", addr_space="Shared")
        v65_sh = nc.dram_tensor("v65_sh", [8192, 1040], BF16,
                                kind="Internal", addr_space="Shared")
        kv_sh = nc.dram_tensor("kv_sh", [128, 1040], F32,
                               kind="Internal", addr_space="Shared")
        shf = nc.dram_tensor("shf", [2, 64], I32,
                             kind="Internal", addr_space="Shared")
        lnH_d = nc.dram_tensor("lnH_d", [2 * 2048, 1024], F32,
                               kind="Internal")
        ybuf_d = nc.dram_tensor("ybuf_d", [2048, 1024], BF16,
                                kind="Internal")
        idxD_d = nc.dram_tensor("idxD_d", [32, CAP], I16, kind="Internal")
        cmbD_d = nc.dram_tensor("cmbD_d", [32, CAP], BF16, kind="Internal")

        regs = {nm: nc.gpsimd.alloc_register("r_" + nm)
                for nm in ("slot", "rb_kt", "rb_v", "rb_kv", "rb_f",
                           "rtmp", "rtmp2", "rk", "rf0", "rf1", "rgo",
                           "rca", "rcb")}
        consts = decl_consts(b, ["c_eye", "c_misc", "c_onesf", "c_erep",
                                 "c_hm2", "c_ones", "c_triu", "c_o128",
                                 "c_sloti", "c_hl3"])
        b.op("gp", lambda e: e.load_library(_libcfg.mlp))
        x_sb = b.sbuf("x_sbuf", [128, NT * 1024], F32)
        xt = b.sbuf("xt_sbuf", [128, ND * 2048], BF16)
        wb = b.sbuf("wb_sbuf", [128, 16384], BF16)
        kvb2 = b.sbuf("kvb2_sb", [128, 1040], BF16)
        mask_sb = b.sbuf("mask_sb", [128, 16], F32)
        gwt_sb = b.sbuf("gwt_sb", [128, 128], BF16)
        bkt_sb = b.sbuf("bkt_sb", [128, 8], F32)
        bvrow_sb = b.sbuf("bvrow_sb", [1, 1024], BF16)
        bqt_sb = b.sbuf("bqt_sb", [128, 8], F32)
        borow_sb = b.sbuf("borow_sb", [1, 1024], BF16)
        slot_sb = b.sbuf("slot_sb", [1, 1], I32)
        nonce_sb = b.sbuf("nonce_sb", [1, 128], I32)
        fbuf = b.sbuf("fbuf", [1, 64], I32)

        sld = b.sem("s_ld")
        sw = b.sem("s_w")
        b.op("sy", lambda e: e.dma_start(out=slot_sb[:, :],
                                         in_=dram["slotv"]), dinc=sld)
        b.op("sy", lambda e: e.dma_start(out=nonce_sb[:, :],
                                         in_=dram["nonces"]), dinc=sld)
        b.wait("gp", sld)
        b.op("gp", lambda e: e.reg_load(regs["slot"], slot_sb[0:1, 0:1]))
        load_weight(b, dram["wk"], wb[:, 0:8192], 8, sw)
        load_weight(b, dram["wv"], wb[:, 8192:16384], 8, sw)
        load_weight(b, dram["gwt"], gwt_sb, 8, sw)
        b.op("gp", lambda e: e.dma_start(out=bkt_sb[:, :],
                                         in_=dram["bk_t"]), dinc=sw)
        b.op("gp", lambda e: e.dma_start(out=bvrow_sb[:, :],
                                         in_=dram["bv_row"]), dinc=sw)
        b.op("gp", lambda e: e.dma_start(out=bqt_sb[:, :],
                                         in_=dram["bq_t"]), dinc=sw)
        b.op("gp", lambda e: e.dma_start(out=borow_sb[:, :],
                                         in_=dram["bo_row"]), dinc=sw)
        wkv_ld = sw.n

        pA_done = emit_passA(b, dram["x"], wb[:, 0:8192], wb[:, 8192:16384],
                             consts["c_eye"], consts["c_misc"], (sw, wkv_ld),
                             kv_sh, regs)
        bar1 = b.sem("bar_done")
        emit_barrier(b, 0, nonce_sb, shf, fbuf, regs, bar1)
        kv_ready = emit_kv_merge(b, kv_sh, kvb2, (bar1, bar1.n))

        dram_w = {"ws_in": dram["ws_in"], "vs_in": dram["vs_in"],
                  "ws_out": dram["ws_out"],
                  "e_win": [dram["e_win"][e] for e in range(NE)],
                  "e_vin": [dram["e_vin"][e] for e in range(NE)],
                  "e_wout": [dram["e_wout"][e] for e in range(NE)]}

        prev_phase_done = pA_done
        prev_x_free = (sld, 0)
        for g in (1, 0):
            tg = f"g{g}"
            b.wait("gp", *prev_phase_done)
            load_weight(b, dram["wq"], wb[:, 0:8192], 8, sw)
            load_weight(b, dram["wo"], wb[:, 8192:16384], 8, sw)
            wqwo_ld = sw.n
            b.wait("sy", *prev_x_free)
            b.op("sy", lambda e, g=g: e.dma_start(
                out=x_sb[:, :].rearrange("p (t d) -> p t d", t=NT),
                in_=dram["x"][g * 2048:(g + 1) * 2048, :]
                .rearrange("(t p) d -> p t d", p=128)), dinc=sld)
            x_ok = (sld, sld.n)
            b.wait("ve", *prev_phase_done)
            b.wait("te", *prev_phase_done)
            b.wait("sc", *prev_phase_done)
            b.wait("sy", *prev_phase_done)
            xt_ready = emit_ln(b, "l1" + tg, x_sb, xt, consts["c_eye"],
                               consts["c_misc"], x_ok)
            x1_done = emit_attn_a_main(b, xt, x_sb, wb[:, 0:8192],
                                       wb[:, 8192:16384], kvb2,
                                       consts["c_onesf"], consts["c_ones"],
                                       xt_ready, kv_ready,
                                       (sw, wqwo_ld), (sw, wqwo_ld),
                                       tag="a" + tg)
            pi = 0 if g == 1 else 1
            stg = []
            xt2_ready = emit_ln(b, "l2" + tg, x_sb, xt, consts["c_eye"],
                                consts["c_misc"], x1_done,
                                lnH=lnH_d.ap()[pi * 2048:(pi + 1) * 2048, :],
                                stage_tok=stg)
            if SPARSE_F1:
                x2_done = emit_ffn_sparse(
                    b, "f" + tg, xt, x_sb, wb, gwt_sb, dram_w, consts,
                    xt2_ready, (sw, sw.n), x1_done,
                    lnH_d.ap()[pi * 2048:(pi + 1) * 2048, :], stg[0],
                    ybuf_d.ap(), idxD_d.ap()[pi * 16:(pi + 1) * 16, :],
                    cmbD_d.ap()[pi * 16:(pi + 1) * 16, :], regs)
            else:
                x2_done = emit_ffn(b, "f" + tg, xt, x_sb, wb, gwt_sb,
                                   dram_w, consts, xt2_ready, (sw, sw.n),
                                   x1_done)
            b.wait("gp", *x2_done)
            load_weight(b, dram["wkT"], wb[:, 0:8192], 8, sw)
            load_weight(b, dram["wvT"], wb[:, 8192:16384], 8, sw)
            kw_ld = sw.n
            xt3_ready = emit_ln(b, "l3" + tg, x_sb, xt, consts["c_eye"],
                                consts["c_misc"], x2_done, mask_sb=mask_sb)
            mask_ready = (b.sems["l3" + tg + "v"],
                          b.sems["l3" + tg + "v"].n)
            kp_done = emit_kvpayload_sh(b, "kp" + tg, g, xt,
                                        wb[:, 0:8192], wb[:, 8192:16384],
                                        bkt_sb, bvrow_sb, mask_sb, consts,
                                        kt_sh, v65_sh, regs, xt3_ready,
                                        (sw, kw_ld), mask_ready)
            prev_phase_done = kp_done
            prev_x_free = (b.sems["l3" + tg + "s"],
                           b.sems["l3" + tg + "s"].n)

        bar2 = b.sem("bar2_done")
        emit_barrier(b, 1, nonce_sb, shf, fbuf, regs, bar2)
        b.wait("sy", bar2, bar2.n)
        b.wait("gp", bar2, bar2.n)
        b.wait("te", bar2, bar2.n)
        b.wait("gp", *prev_phase_done)
        b.wait("ve", *prev_phase_done)
        b.wait("te", *prev_phase_done)
        b.wait("sc", *prev_phase_done)
        load_weight(b, dram["wqT"], wb[:, 0:8192], 8, sw)
        load_weight(b, dram["woT"], wb[:, 8192:16384], 8, sw)
        wfl_ld = sw.n
        x3_done = emit_attn_b(b, xt, x_sb, wb, bqt_sb, borow_sb,
                              kt_sh, v65_sh, consts,
                              (b.sems["l3g0e"], b.sems["l3g0e"].n),
                              (sw, wfl_ld), (sw, wfl_ld),
                              kt_rows=KT_ROWS, nshard=2)
        stg4 = []
        xt4_ready = emit_ln(b, "l4", x_sb, xt, consts["c_eye"],
                            consts["c_misc"], x3_done,
                            lnH=lnH_d.ap()[0:2048, :], stage_tok=stg4)
        if SPARSE_F2:
            out_done = emit_ffn_sparse(
                b, "f2", xt, x_sb, wb, gwt_sb, dram_w, consts,
                xt4_ready, (sw, sw.n), x3_done,
                lnH_d.ap()[0:2048, :], stg4[0], ybuf_d.ap(),
                idxD_d.ap()[0:16, :], cmbD_d.ap()[0:16, :], regs)
        else:
            out_done = emit_ffn(b, "f2", xt, x_sb, wb, gwt_sb, dram_w,
                                consts, xt4_ready, (sw, sw.n), x3_done)
        sst = b.sem("s_out")
        b.wait("sy", *out_done)
        b.op("sy", lambda e: e.dma_start(
            out=out_d.rearrange("(t p) d -> p t d", p=128),
            in_=x_sb[:, :].rearrange("p (t d) -> p t d", t=NT)), dinc=sst)
        b.wait("sy", sst)
        b.wait("gp", sst)

        block = stack.enter_context(nc.Block())
        b.play(block)
    return nc


def _fused_inmaps(inputs):
    consts = _host_consts()
    f32 = lambda k: np.asarray(inputs[k], np.float32)
    T = lambda a: np.ascontiguousarray(np.asarray(a, np.float32).T)
    rng_nonce = np.random.default_rng()
    nonces = rng_nonce.integers(1, 2**31 - 1, size=(1, 2),
                                dtype=np.int32)
    nonrow = np.zeros((1, 128), np.int32)
    nonrow[0, 0:64] = nonces[0, 0]
    nonrow[0, 64:128] = nonces[0, 1]
    shared = {
        "wk": f32("L0_attn_a_Wk"), "wv": f32("L0_attn_a_Wv"),
        "wq": f32("L0_attn_a_Wq"), "wo": f32("L0_attn_a_Wo"),
        "gwt": T(inputs["L0_ffn_gate_w"]),
        "ws_in": f32("L0_ffn_ws_in"), "vs_in": f32("L0_ffn_vs_in"),
        "ws_out": f32("L0_ffn_ws_out"),
        "e_win": f32("L0_ffn_e_win"), "e_vin": f32("L0_ffn_e_vin"),
        "e_wout": f32("L0_ffn_e_wout"),
        "wkT": T(inputs["L0_attn_b_wk_w"]), "wvT": T(inputs["L0_attn_b_wv_w"]),
        "bk_t": np.ascontiguousarray(
            np.asarray(inputs["L0_attn_b_wk_b"], np.float32).reshape(8, 128).T),
        "bv_row": np.asarray(inputs["L0_attn_b_wv_b"],
                             np.float32).reshape(1, D),
        "wqT": T(inputs["L0_attn_b_wq_w"]), "woT": T(inputs["L0_attn_b_wo_w"]),
        "bq_t": np.ascontiguousarray(
            np.asarray(inputs["L0_attn_b_wq_b"], np.float32).reshape(8, 128).T),
        "bo_row": np.asarray(inputs["L0_attn_b_wo_b"],
                             np.float32).reshape(1, D),
        "nonces": nonrow,
    }
    shared.update({k: consts[k] for k in ("c_eye", "c_misc", "c_onesf",
                                          "c_erep", "c_hm2", "c_ones",
                                          "c_triu", "c_o128", "c_sloti",
                                          "c_hl3")})
    x = np.asarray(inputs["L0_attn_a_x"], np.float32).reshape(2 * 8192, D)
    maps = []
    for c in range(8):
        bb = c // 4
        s = c % 2
        wide = x[bb * 8192 + s * 4096: bb * 8192 + (s + 1) * 4096]
        if c % 4 >= 2:
            xw = np.concatenate([wide[2048:4096], wide[0:2048]], axis=0)
        else:
            xw = wide
        m = dict(shared)
        m["x"] = np.ascontiguousarray(xw)
        m["slotv"] = np.array([[s]], np.int32)
        maps.append(m)
    return maps


def _fused_out_ranges():
    out = []
    for c in range(8):
        bb = c // 4
        s = c % 2
        if c % 4 >= 2:
            lo = s * 4096 + 2048
        else:
            lo = s * 4096
        out.append((bb, lo))
    return out


def kernel(**inputs):
    nc = build_fused()
    _libov.lower_extended_insts(nc)
    res = run_bass_kernel_spmd(nc, _fused_inmaps(inputs),
                               core_ids=list(range(8)))
    out = np.zeros((2, 8192, D), np.float32)
    for c, (bb, lo) in enumerate(_fused_out_ranges()):
        out[bb, lo:lo + 2048] = np.asarray(res.results[c]["out"], np.float32)
    return out


# version 18 (final)
# speedup vs baseline: 1.7724x; 1.0007x over previous
import numpy as np
import ml_dtypes
from contextlib import ExitStack
import concourse.bass as bass
import concourse.mybir as mybir
from concourse.bass_utils import run_bass_kernel_spmd

F32 = mybir.dt.float32
BF16 = mybir.dt.bfloat16
AF = mybir.ActivationFunctionType
ALU = mybir.AluOpType
AX = mybir.AxisListType
I32 = mybir.dt.int32
bf16 = ml_dtypes.bfloat16

D = 1024
TC = 2048
NT = TC // 128
ND = D // 128
H = 16
DH = 64
NE = 16
EH = 512
EPS_LN = 1e-5
EPS_A = 1e-6
LN0125 = float(np.log(0.125))

ENGS = ("sy", "gp", "te", "ve", "sc")


class Sem:
    def __init__(self, stack, nc, name):
        self.h = stack.enter_context(nc.semaphore(name))
        self.n = 0


class B:

    def __init__(self, nc, stack):
        self.nc = nc
        self.stack = stack
        self.q = {e: [] for e in ENGS}
        self.sems = {}
        self.selfsem = {e: self.sem("fl_" + e) for e in ENGS}

    def sem(self, name):
        if name not in self.sems:
            self.sems[name] = Sem(self.stack, self.nc, name)
        return self.sems[name]

    def sbuf(self, name, shape, dt, stack=None):
        return (stack or self.stack).enter_context(
            self.nc.sbuf_tensor(name, shape, dt))

    def psum(self, name, shape, dt=F32, stack=None):
        return (stack or self.stack).enter_context(
            self.nc.psum_tensor(name, shape, dt))

    def op(self, eng, fn, inc=None, dinc=None):
        incs = []
        if inc is not None:
            s, k = inc if isinstance(inc, tuple) else (inc, 1)
            s.n += k
            incs.append((s.h, k))
        if dinc is not None:
            dinc.n += 16
            incs.append((dinc.h, 16))

        def play(e, fn=fn, incs=incs):
            ins = fn(e)
            for h, k in incs:
                ins.then_inc(h, k)
        self.q[eng].append(play)

    def wait(self, eng, sem, val=None):
        v = sem.n if val is None else val
        if v > 0:
            self.q[eng].append(lambda e, h=sem.h, v=v: e.wait_ge(h, v))

    def opf(self, eng, fn):
        s = self.selfsem[eng]
        self.op(eng, fn, inc=s)
        self.wait(eng, s)

    def op_branch(self, eng, reg, fn_slot0, fn_slot1, inc=None, dinc=None):
        incs = []
        if inc is not None:
            s, k = inc if isinstance(inc, tuple) else (inc, 1)
            s.n += k
            incs.append((s.h, k))
        if dinc is not None:
            dinc.n += 16
            incs.append((dinc.h, 16))

        def play(e, fn0=fn_slot0, fn1=fn_slot1, incs=incs, reg=reg):
            with e.If_eq(reg, 0):
                ins = fn0(e)
                for h, k in incs:
                    ins.then_inc(h, k)
            with e.Else():
                ins = fn1(e)
                for h, k in incs:
                    ins.then_inc(h, k)
        self.q[eng].append(play)

    def play(self, block):
        for eng, meth in (("sy", block.sync), ("gp", block.gpsimd),
                          ("te", block.tensor), ("ve", block.vector),
                          ("sc", block.scalar)):
            ops = self.q[eng]
            if ops:
                meth(lambda e, ops=ops: [f(e) for f in ops])



def _recip16(b, v, out, in_):
    with b.nc.allow_low_precision(reason="bf16 rden for fast replicate MM"):
        return v.reciprocal(out, in_)


def _host_consts():
    eye = np.eye(128, dtype=bf16)
    onesf = np.ones((128, 64), dtype=np.float32)
    ones_bf = np.ones((1, 512), dtype=bf16)
    erep = np.zeros((16, 16 * 128), dtype=bf16)
    for e in range(16):
        erep[e, e * 128:(e + 1) * 128] = 1
    hm2 = np.zeros((128, 2), dtype=np.float32)
    hm2[0:64, 0] = 1.0
    hm2[64:128, 1] = 1.0
    misc = np.zeros((128, 4), dtype=np.float32)
    misc[:, 0] = EPS_LN
    misc[:, 1] = LN0125
    misc[:, 2] = EPS_A
    return dict(c_eye=eye, c_onesf=onesf, c_ones=ones_bf, c_erep=erep,
                c_hm2=hm2, c_misc=misc)


CONST_SHAPES = {"c_eye": ([128, 128], BF16), "c_onesf": ([128, 64], F32),
                "c_ones": ([1, 512], BF16), "c_erep": ([16, 2048], BF16),
                "c_hm2": ([128, 2], F32), "c_misc": ([128, 4], F32)}


def decl_consts(b, names):
    out = {}
    for n in names:
        shape, dt = CONST_SHAPES[n]
        d = b.nc.dram_tensor(n, shape, dt, kind="ExternalInput").ap()
        sb = b.sbuf(n + "_sb", shape, dt)
        out[n] = sb
        b.op("sy", lambda e, sb=sb, d=d: e.dma_start(out=sb[:, :], in_=d),
             dinc=b.sem("s_ld"))
    return out


def load_x(b, x_d, x_sb):
    b.op("sy", lambda e: e.dma_start(
        out=x_sb[:, :].rearrange("p (t d) -> p t d", t=NT),
        in_=x_d.rearrange("(t p) d -> p t d", p=128)), dinc=b.sem("s_ld"))


def _ap(x):
    return x if isinstance(x, bass.AP) else x.ap()


def load_weight(b, d_ap, sb, nblk, sem):
    b.op("gp", lambda e: e.dma_start(
        out=_ap(sb).rearrange("p (k c) -> p k c", k=nblk),
        in_=d_ap.rearrange("(k p) c -> p k c", p=128)), dinc=sem)



def emit_ln(b, tag, x_sb, xt, eye_sb, cb, x_ready, mask_sb=None):
    sv = b.sem(tag + "v")
    ss = b.sem(tag + "s")
    sp = b.sem(tag + "p")
    se = b.sem(tag + "e")
    with ExitStack() as ps:
        stats = b.sbuf(tag + "_st", [128, 20], F32, stack=ps)
        lnst = b.sbuf(tag + "_ln", [128, 2 * 1024], BF16, stack=ps)
        ptrs = [b.psum(tag + f"_ptr{i}", [128, 128], BF16, stack=ps)
                for i in range(2)]

        b.wait("ve", *x_ready)
        b.wait("sc", *x_ready)
        b.wait("te", *x_ready)
        tr_done = {}
        ev_done = {}
        for t in range(NT):
            lo = (t % 2) * 1024
            xt_ap = x_sb[:, t * 1024:(t + 1) * 1024]
            b.op("ve", lambda v, a=xt_ap: v.bn_stats(stats[:, 0:6],
                                                     a[:, 0:512]))
            b.opf("ve", lambda v, a=xt_ap: v.bn_stats(stats[:, 6:12],
                                                      a[:, 512:1024]))
            b.op("ve", lambda v: v.bn_aggr(stats[:, 12:14], stats[:, 0:12]),
                 inc=sv)
            b.wait("sc", sv)
            b.opf("sc", lambda s: s.activation(
                stats[:, 14:15], stats[:, 13:14], AF.Ln, bias=cb[:, 0:1]))
            b.op("sc", lambda s: s.activation(
                stats[:, 15:16], stats[:, 14:15], AF.Exp, scale=-0.5), inc=ss)
            b.wait("ve", ss)
            b.op("ve", lambda v: v.scalar_tensor_tensor(
                stats[:, 16:17], stats[:, 12:13], -1.0, stats[:, 15:16],
                ALU.mult, ALU.mult), inc=sv)
            if (t % 2) in tr_done:
                b.wait("sc", sp, tr_done[t % 2])
            b.wait("sc", sv)
            b.op("sc", lambda s, a=xt_ap, lo=lo: s.activation(
                lnst[:, lo:lo + 1024], a, AF.Identity,
                bias=stats[:, 16:17], scale=stats[:, 15:16]), inc=ss)
            if mask_sb is not None:
                b.wait("ve", ss)
                b.op("ve", lambda v, t=t, lo=lo: v.tensor_single_scalar(
                    mask_sb[:, t:t + 1], lnst[:, lo:lo + 1], 0.0,
                    ALU.not_equal), inc=sv)
            b.wait("te", ss)
            for dc in range(ND):
                par = dc % 2
                if par in ev_done:
                    b.wait("te", se, ev_done[par])
                b.op("te", lambda te, par=par, lo=lo, dc=dc: te.transpose(
                    ptrs[par][:, :], lnst[:, lo + dc * 128:lo + (dc + 1) * 128],
                    eye_sb[:, :]), inc=sp)
                b.wait("ve", sp)
                b.op("ve", lambda v, par=par, dc=dc, t=t: v.tensor_copy(
                    xt[:, dc * 2048 + t * 128:dc * 2048 + (t + 1) * 128],
                    ptrs[par][:, :]), inc=se)
                ev_done[par] = se.n
            tr_done[t % 2] = sp.n
    return se, se.n



def emit_attn_a_kv(b, xt, wk_sb, wv_sb, xt_ready, w_ready, kvp_d):
    sp = b.sem("akv_p")
    sv = b.sem("akv_v")
    ss = b.sem("akv_s")
    skv = b.sem("akv_kv")
    sst = b.sem("akv_st")
    so = b.sem("akv_ones")

    kf = b.sbuf("kf", [128, 2 * 1024], BF16)
    v65 = b.sbuf("v65", [128, 2 * 1040], BF16)
    phs = b.sbuf("phs", [128, 2 * 1024], F32)
    kvs = b.sbuf("kvs", [64, 1040], F32)
    pk = [b.psum(f"pk{i}", [128, 512]) for i in range(2)]
    pv = [b.psum(f"pvv{i}", [128, 512]) for i in range(2)]
    pkv = [b.psum("pkv0", [64, 455]), b.psum("pkv1", [64, 455]),
           b.psum("pkv2", [64, 130])]

    for sl in range(2):
        b.op("gp", lambda e, sl=sl: e.memset(
            bass.AP(v65, sl * 1040 + 64, [[2 * 1040, 128], [65, 16]]), 1.0),
            inc=so)

    b.wait("te", *xt_ready)
    b.wait("te", *w_ready)
    b.wait("ve", so)
    last_k_use = [0, 0]
    last_v_use = [0, 0]
    last_kf_use = 0
    for t in range(NT):
        sl = t % 2
        if t >= 2:
            b.wait("ve", skv, t - 1)
        for dn in range(2):
            b.wait("te", sv, last_k_use[dn])
            for kdc in range(ND):
                b.op("te", lambda te, dn=dn, kdc=kdc, t=t: te.matmul(
                    pk[dn][:, :],
                    xt[:, kdc * 2048 + t * 128:kdc * 2048 + (t + 1) * 128],
                    wk_sb[:, kdc * 1024 + dn * 512:kdc * 1024 + (dn + 1) * 512],
                    start=(kdc == 0), stop=(kdc == ND - 1)),
                    inc=(sp if kdc == ND - 1 else None))
            b.wait("ve", sp)
            b.op("ve", lambda v, dn=dn, sl=sl: v.tensor_single_scalar(
                phs[:, sl * 1024 + dn * 512:sl * 1024 + (dn + 1) * 512],
                pk[dn][:, :], 0.0, ALU.min), inc=sv)
            b.wait("sc", sv)
            b.op("sc", lambda s, dn=dn, sl=sl: s.activation(
                phs[:, sl * 1024 + dn * 512:sl * 1024 + (dn + 1) * 512],
                phs[:, sl * 1024 + dn * 512:sl * 1024 + (dn + 1) * 512],
                AF.Exp), inc=ss)
            b.wait("ve", ss)
            b.op("ve", lambda v, dn=dn, sl=sl: v.scalar_tensor_tensor(
                kf[:, sl * 1024 + dn * 512:sl * 1024 + (dn + 1) * 512],
                pk[dn][:, :], 0.0,
                phs[:, sl * 1024 + dn * 512:sl * 1024 + (dn + 1) * 512],
                ALU.max, ALU.add), inc=sv)
            last_k_use[dn] = sv.n
        for dn in range(2):
            b.wait("te", sv, last_v_use[dn])
            for kdc in range(ND):
                b.op("te", lambda te, dn=dn, kdc=kdc, t=t: te.matmul(
                    pv[dn][:, :],
                    xt[:, kdc * 2048 + t * 128:kdc * 2048 + (t + 1) * 128],
                    wv_sb[:, kdc * 1024 + dn * 512:kdc * 1024 + (dn + 1) * 512],
                    start=(kdc == 0), stop=(kdc == ND - 1)),
                    inc=(sp if kdc == ND - 1 else None))
            b.wait("ve", sp)
            b.op("ve", lambda v, dn=dn, sl=sl: v.tensor_copy(
                bass.AP(v65, sl * 1040 + dn * 8 * 65,
                        [[2 * 1040, 128], [65, 8], [1, 64]]),
                bass.AP(pv[dn], 0, [[512, 128], [64, 8], [1, 64]])), inc=sv)
            last_v_use[dn] = sv.n
        b.wait("te", sv)
        for h in range(H):
            bank, slot = divmod(h, 7)
            b.op("te", lambda te, h=h, sl=sl, bank=bank, slot=slot, t=t:
                 te.matmul(
                     pkv[bank][:, slot * 65:(slot + 1) * 65],
                     kf[:, sl * 1024 + h * 64:sl * 1024 + (h + 1) * 64],
                     v65[:, sl * 1040 + h * 65:sl * 1040 + (h + 1) * 65],
                     start=(t == 0 and slot == 0), stop=(t == NT - 1)),
                 inc=(skv if h == H - 1 else None))
    b.wait("ve", skv)
    b.op("ve", lambda v: v.tensor_copy(kvs[:, 0:455], pkv[0][:, :]))
    b.op("ve", lambda v: v.tensor_copy(kvs[:, 455:910], pkv[1][:, :]))
    b.op("ve", lambda v: v.tensor_copy(kvs[:, 910:1040], pkv[2][:, :]),
         inc=sv)
    b.wait("sy", sv)
    b.op("sy", lambda e: e.dma_start(out=kvp_d, in_=kvs[:, :]), dinc=sst)
    b.wait("sy", sst)


def build_launch_a():
    nc = bass.Bass("TRN2", target_bir_lowering=False)
    with ExitStack() as stack:
        b = B(nc, stack)
        x_d = nc.dram_tensor("x", [TC, D], F32, kind="ExternalInput").ap()
        wk_d = nc.dram_tensor("wk", [D, D], F32, kind="ExternalInput").ap()
        wv_d = nc.dram_tensor("wv", [D, D], F32, kind="ExternalInput").ap()
        kvp_d = nc.dram_tensor("kvp", [64, 1040], F32,
                               kind="ExternalOutput").ap()

        consts = decl_consts(b, ["c_eye", "c_misc"])
        x_sb = b.sbuf("x_sbuf", [128, NT * 1024], F32)
        xt = b.sbuf("xt_sbuf", [128, ND * 2048], BF16)
        wk_sb = b.sbuf("wk_sb", [128, 8192], BF16)
        wv_sb = b.sbuf("wv_sb", [128, 8192], BF16)
        sw = b.sem("s_w")
        load_weight(b, wk_d, wk_sb, 8, sw)
        load_weight(b, wv_d, wv_sb, 8, sw)
        load_x(b, x_d, x_sb)
        sld = b.sem("s_ld")

        xt_ready = emit_ln(b, "ln1", x_sb, xt, consts["c_eye"],
                           consts["c_misc"], (sld, sld.n))
        emit_attn_a_kv(b, xt, wk_sb, wv_sb, xt_ready, (sw, sw.n), kvp_d)

        block = stack.enter_context(nc.Block())
        b.play(block)
    return nc



def emit_kv_sum(b, kvp_all_d, kvb2):
    sa = b.sem("kvs_a")
    sv = b.sem("kvs_v")
    sd = b.sem("kvs_d")
    kvs = b.sbuf("kvs_f", [64, 1040], F32)
    for r in range(4):
        if r > 0:
            b.wait("gp", sa)
        b.op("gp", lambda e, r=r: e.dma_start(
            out=kvs[:, :], in_=kvp_all_d[r * 64:(r + 1) * 64, :],
            accum_op=(ALU.add if r > 0 else ALU.bypass)), dinc=sa)
    b.wait("ve", sa)
    b.op("ve", lambda v: v.tensor_copy(kvb2[0:64, :], kvs[:, :]), inc=sv)
    b.wait("sy", sv)
    b.op("sy", lambda e: e.dma_start(out=kvb2[64:128, :], in_=kvb2[0:64, :]),
         dinc=sd)
    return sd, sd.n


def emit_attn_a_main(b, xt, x_sb, wq_sb, wo_sb, kvb2, onesf, onesb, xt_ready,
                     kv_ready, wq_ready, wo_ready, tag="am"):
    sp = b.sem(tag + "_p")
    sv = b.sem(tag + "_v")
    ss = b.sem(tag + "_s")
    sd = b.sem(tag + "_d")
    with ExitStack() as ps:
        at = b.sbuf(tag + "_at_sb", [128, 16384], BF16, stack=ps)
        qf = b.sbuf(tag + "_qf", [128, 2 * 512], BF16, stack=ps)
        phs = b.sbuf(tag + "_am_phs", [128, 2 * 512], F32, stack=ps)
        asm = b.sbuf(tag + "_am_asm", [128, 2048], F32, stack=ps)
        rrec = b.sbuf(tag + "_am_rrec", [1, 1024], BF16, stack=ps)
        atst = b.sbuf(tag + "_am_atst", [64, 2 * 512], BF16, stack=ps)
        pq = [b.psum(tag + f"_pq{i}", [128, 512], stack=ps) for i in range(2)]
        pnd = [b.psum(tag + f"_pnd{i}", [65, 512], stack=ps) for i in range(2)]
        prep = [b.psum(tag + f"_prep{i}", [64, 512], stack=ps) for i in range(2)]
        po = [b.psum(tag + f"_po{i}", [128, 512], stack=ps) for i in range(2)]

        b.wait("te", *xt_ready)
        b.wait("te", *kv_ready)
        b.wait("te", *wq_ready)
        tok_pq = [0, 0]
        tok_qf = [0, 0]
        tok_pnd = [0, 0]
        tok_prep = [0, 0]
        tok_atst = [0, 0]
        tok_po = [0, 0]
        tok_asm = [0, 0]
        hidx = 0
        for qc in range(4):
            for dc in range(ND):
                par = (qc * ND + dc) % 2
                b.wait("te", sv, tok_pq[par])
                for kdc in range(ND):
                    b.op("te", lambda te, kdc=kdc, dc=dc, qc=qc, par=par:
                         te.matmul(
                             pq[par][:, :],
                             wq_sb[:, kdc * 1024 + dc * 128:kdc * 1024 + (dc + 1) * 128],
                             xt[:, kdc * 2048 + qc * 512:kdc * 2048 + (qc + 1) * 512],
                             start=(kdc == 0), stop=(kdc == ND - 1)),
                         inc=(sp if kdc == ND - 1 else None))
                b.wait("ve", sp)
                b.wait("ve", sp, tok_qf[par])
                b.op("ve", lambda v, par=par: v.tensor_single_scalar(
                    phs[:, par * 512:(par + 1) * 512], pq[par][:, :], 0.0,
                    ALU.min), inc=sv)
                b.wait("sc", sv)
                b.op("sc", lambda s, par=par: s.activation(
                    phs[:, par * 512:(par + 1) * 512],
                    phs[:, par * 512:(par + 1) * 512], AF.Exp), inc=ss)
                b.wait("ve", ss)
                b.op("ve", lambda v, par=par: v.scalar_tensor_tensor(
                    qf[:, par * 512:(par + 1) * 512], pq[par][:, :], 0.0,
                    phs[:, par * 512:(par + 1) * 512], ALU.max, ALU.add),
                    inc=sv)
                tok_pq[par] = sv.n
                qf_sv = sv.n
                for j in range(2):
                    h = 2 * dc + j
                    hp = hidx % 2
                    hidx += 1
                    b.wait("te", sv, tok_pnd[hp])
                    b.wait("te", sv, qf_sv)
                    b.op("te", lambda te, h=h, j=j, par=par, hp=hp: te.matmul(
                        pnd[hp][:, :],
                        kvb2[j * 64:(j + 1) * 64, h * 65:(h + 1) * 65],
                        qf[j * 64:(j + 1) * 64, par * 512:(par + 1) * 512],
                        start=True, stop=True), inc=sp)
                    tok_qf[par] = sp.n
                    b.wait("ve", sp)
                    b.wait("ve", sp, tok_asm[hp])
                    b.opf("ve", lambda v, hp=hp: v.tensor_single_scalar(
                        asm[64:65, hp * 1024:hp * 1024 + 512],
                        pnd[hp][64:65, :], EPS_A, ALU.add))
                    b.op("ve", lambda v, hp=hp: _recip16(b, v,
                        rrec[0:1, hp * 512:(hp + 1) * 512],
                        asm[64:65, hp * 1024:hp * 1024 + 512]), inc=sv)
                    b.wait("te", sv)
                    b.wait("te", ss, tok_prep[hp])
                    b.op("te", lambda te, hp=hp: te.matmul(
                        prep[hp][:, :], onesb[0:1, 0:64],
                        rrec[0:1, hp * 512:(hp + 1) * 512],
                        start=True, stop=True), inc=sp)
                    tok_asm[hp] = sp.n
                    b.wait("sc", sp)
                    b.wait("sc", sv, tok_pnd[hp])
                    b.op("sc", lambda s, hp=hp: s.copy(
                        asm[0:64, hp * 1024 + 512:hp * 1024 + 1024],
                        prep[hp][:, :]), inc=ss)
                    tok_prep[hp] = ss.n
                    b.wait("ve", ss)
                    b.wait("ve", sd, tok_atst[hp])
                    b.op("ve", lambda v, hp=hp: v.tensor_tensor(
                        atst[:, hp * 512:(hp + 1) * 512], pnd[hp][0:64, :],
                        asm[0:64, hp * 1024 + 512:hp * 1024 + 1024],
                        ALU.mult), inc=sv)
                    tok_pnd[hp] = sv.n
                    b.wait("sy", sv)
                    b.op("sy", lambda e, hp=hp, j=j, dc=dc, qc=qc: e.dma_start(
                        out=at[j * 64:(j + 1) * 64,
                               dc * 2048 + qc * 512:dc * 2048 + (qc + 1) * 512],
                        in_=atst[:, hp * 512:(hp + 1) * 512]), dinc=sd)
                    tok_atst[hp] = sd.n
            b.wait("te", sd)
            b.wait("te", *wo_ready)
            for t in range(4 * qc, 4 * qc + 4):
                for dn in range(2):
                    pr = (2 * t + dn) % 2
                    b.wait("te", sv, tok_po[pr])
                    for kdc in range(ND):
                        b.op("te", lambda te, kdc=kdc, t=t, dn=dn, pr=pr:
                             te.matmul(
                                 po[pr][:, :],
                                 at[:, kdc * 2048 + t * 128:kdc * 2048 + (t + 1) * 128],
                                 wo_sb[:, kdc * 1024 + dn * 512:kdc * 1024 + (dn + 1) * 512],
                                 start=(kdc == 0), stop=(kdc == ND - 1)),
                             inc=(sp if kdc == ND - 1 else None))
                    b.wait("ve", sp)
                    b.op("ve", lambda v, t=t, dn=dn, pr=pr: v.tensor_tensor(
                        x_sb[:, t * 1024 + dn * 512:t * 1024 + (dn + 1) * 512],
                        po[pr][:, :],
                        x_sb[:, t * 1024 + dn * 512:t * 1024 + (dn + 1) * 512],
                        ALU.add), inc=sv)
                    tok_po[pr] = sv.n
    return sv, sv.n


def emit_ffn(b, tag, xt, x_sb, wb, gwt_sb, dram_w, consts, xt_ready,
             gwt_ready, wb_free):
    sp = b.sem(tag + "p")
    sv = b.sem(tag + "v")
    ss = b.sem(tag + "s")
    swl = b.sem(tag + "w")
    eye = consts["c_eye"]
    erep = consts["c_erep"]
    with ExitStack() as ps:
        cmb = b.sbuf(tag + "_cmb", [16, 2048], BF16, stack=ps)
        crep2 = b.sbuf(tag + "_crep", [128, 4096], BF16, stack=ps)
        zb = b.sbuf(tag + "_z", [128, 2048], BF16, stack=ps)
        vw = b.sbuf(tag + "_vw", [128, 2048], BF16, stack=ps)
        hsl = b.sbuf(tag + "_h", [128, 16384], BF16, stack=ps)
        gt = b.sbuf(tag + "_gt", [128, 64], F32, stack=ps)
        gcb = b.sbuf(tag + "_gcb", [128, 16], BF16, stack=ps)

        with ExitStack() as gs:
            pg = b.psum(tag + "_pg", [128, 16], stack=gs)
            ptc = b.psum(tag + "_ptc", [16, 128], BF16, stack=gs)
            b.wait("te", *xt_ready)
            b.wait("te", *gwt_ready)
            tok_pg = 0
            tok_ptc = 0
            tok_gcb = 0
            for t in range(NT):
                b.wait("te", ss, tok_pg)
                for kdc in range(ND):
                    b.op("te", lambda te, kdc=kdc, t=t: te.matmul(
                        pg[:, :],
                        xt[:, kdc * 2048 + t * 128:kdc * 2048 + (t + 1) * 128],
                        gwt_sb[:, kdc * 16:(kdc + 1) * 16],
                        start=(kdc == 0), stop=(kdc == ND - 1)),
                        inc=(sp if kdc == ND - 1 else None))
                b.wait("sc", sp)
                b.op("sc", lambda s: s.activation(gt[:, 0:16], pg[:, :],
                                                  AF.Sigmoid), inc=ss)
                tok_pg = ss.n
                b.wait("ve", ss)
                b.opf("ve", lambda v: v.max(out=gt[:, 16:24], in_=gt[:, 0:16]))
                b.opf("ve", lambda v: v.tensor_single_scalar(
                    gt[:, 24:40], gt[:, 0:16], gt[:, 19:20], ALU.is_ge))
                b.opf("ve", lambda v: v.tensor_mul(gt[:, 40:56], gt[:, 0:16],
                                                   gt[:, 24:40]))
                b.opf("ve", lambda v: v.reduce_sum(gt[:, 56:57], gt[:, 40:56],
                                                   AX.X))
                b.opf("ve", lambda v: v.reciprocal(gt[:, 57:58], gt[:, 56:57]))
                b.wait("ve", sp, tok_gcb)
                b.op("ve", lambda v: v.tensor_single_scalar(
                    gcb[:, :], gt[:, 40:56], gt[:, 57:58], ALU.mult), inc=sv)
                b.wait("te", sv)
                b.wait("te", sv, tok_ptc)
                b.op("te", lambda te: te.transpose(ptc[:, :], gcb[:, :],
                                                   eye[:, :]), inc=sp)
                tok_gcb = sp.n
                b.wait("ve", sp)
                b.op("ve", lambda v, t=t: v.tensor_copy(
                    cmb[:, t * 128:(t + 1) * 128], ptc[:, :]), inc=sv)
                tok_ptc = sv.n

        ph = b.psum(tag + "_ph", [128, 2048], stack=ps)
        pvf = b.psum(tag + "_pv", [128, 2048], stack=ps)
        W1 = wb[:, 0:4096]
        W2 = wb[:, 4096:8192]
        O_ = [wb[:, 8192:12288], wb[:, 12288:16384]]

        def loadw(dst, src, nblk):
            b.op("gp", lambda e, dst=dst, src=src, nblk=nblk: e.dma_start(
                out=dst.rearrange("p (k c) -> p k c", k=nblk),
                in_=src.rearrange("(k p) c -> p k c", p=128)), dinc=swl)

        tok_o = [0, 0]
        tok_h = [0, 0]
        tok_ph = 0
        tok_pvf = 0
        tok_zb = 0
        tok_po = [0, 0]
        crep_last = [0, 0]
        wl12 = {}
        wlo = {}
        b.wait("gp", *wb_free)
        loadw(W1, dram_w["e_win"][0], 8)
        loadw(W2, dram_w["e_vin"][0], 8)
        wl12[0] = swl.n
        loadw(O_[0], dram_w["e_wout"][0], 4)
        wlo[0] = swl.n
        loadw(O_[1], dram_w["e_wout"][1], 4)
        wlo[1] = swl.n

        def emit_crep(e):
            for tc in range(4):
                b.op("te", lambda te, e=e, tc=tc: te.matmul(
                    ph[:, tc * 512:(tc + 1) * 512],
                    erep[:, e * 128:(e + 1) * 128],
                    cmb[:, tc * 512:(tc + 1) * 512],
                    start=True, stop=True), inc=(sp if tc == 3 else None))
            b.wait("sc", sp)
            b.wait("sc", sv, crep_last[e % 2])
            b.op("sc", lambda s, e=e: s.copy(
                crep2[:, (e % 2) * 2048:(e % 2) * 2048 + 2048], ph[:, :]),
                inc=ss)
            return ss.n

        b.wait("te", sv)
        b.wait("te", ss, tok_ph)
        tok_ph = emit_crep(0)
        for e in range(NE):
            sl = e % 2
            crep = crep2[:, sl * 2048:sl * 2048 + 2048]
            b.wait("te", swl, wl12[e])
            for hc in range(4):
                b.wait("te", ss, tok_ph)
                b.wait("te", sv, tok_pvf)
                for tc in range(4):
                    for kdc in range(ND):
                        last = (tc == 3 and kdc == ND - 1)
                        b.op("te", lambda te, hc=hc, tc=tc, kdc=kdc:
                             te.matmul(
                                 ph[:, tc * 512:(tc + 1) * 512],
                                 W1[:, kdc * 512 + hc * 128:kdc * 512 + (hc + 1) * 128],
                                 xt[:, kdc * 2048 + tc * 512:kdc * 2048 + (tc + 1) * 512],
                                 start=(kdc == 0), stop=(kdc == ND - 1)),
                             inc=(sp if last else None))
                w1_tok = sp.n
                for tc in range(4):
                    for kdc in range(ND):
                        last = (tc == 3 and kdc == ND - 1)
                        b.op("te", lambda te, hc=hc, tc=tc, kdc=kdc:
                             te.matmul(
                                 pvf[:, tc * 512:(tc + 1) * 512],
                                 W2[:, kdc * 512 + hc * 128:kdc * 512 + (hc + 1) * 128],
                                 xt[:, kdc * 2048 + tc * 512:kdc * 2048 + (tc + 1) * 512],
                                 start=(kdc == 0), stop=(kdc == ND - 1)),
                             inc=(sp if last else None))
                w2_tok = sp.n
                b.wait("sc", sp, w1_tok)
                b.wait("sc", sv, tok_zb)
                b.op("sc", lambda s: s.activation(zb[:, :], ph[:, :],
                                                  AF.Silu), inc=ss)
                tok_ph = ss.n
                b.wait("ve", sp, w2_tok)
                b.wait("ve", ss)
                b.op("ve", lambda v, crep=crep: v.tensor_tensor(
                    vw[:, :], pvf[:, :], crep, ALU.mult), inc=sv)
                tok_pvf = sv.n
                b.wait("ve", sp, tok_h[sl])
                b.op("ve", lambda v, sl=sl, hc=hc: v.tensor_tensor(
                    hsl[:, sl * 8192 + hc * 2048:sl * 8192 + (hc + 1) * 2048],
                    zb[:, :], vw[:, :], ALU.mult), inc=sv)
                tok_zb = sv.n
            crep_last[sl] = sv.n
            tok_w12 = sp.n
            if e + 1 < NE:
                b.wait("te", ss, tok_ph)
                tok_ph = emit_crep(e + 1)
                b.wait("gp", sp, tok_w12)
                loadw(W1, dram_w["e_win"][e + 1], 8)
                loadw(W2, dram_w["e_vin"][e + 1], 8)
                wl12[e + 1] = swl.n
            if sl == 1:
                b.wait("te", sv)
                b.wait("te", swl, wlo[e])
                for t in range(NT):
                    for dn in range(2):
                        pr = (2 * t + dn) % 2
                        b.wait("te", sv, tok_po[pr])
                        n = 0
                        for s2 in range(2):
                            for hkc in range(4):
                                n += 1
                                b.op("te", lambda te, s2=s2, hkc=hkc, t=t,
                                     dn=dn, pr=pr, first=(n == 1),
                                     last=(n == 8): te.matmul(
                                         pvf[:, pr * 512:(pr + 1) * 512],
                                         hsl[:, s2 * 8192 + hkc * 2048 + t * 128:
                                             s2 * 8192 + hkc * 2048 + (t + 1) * 128],
                                         O_[s2][:, hkc * 1024 + dn * 512:
                                                hkc * 1024 + (dn + 1) * 512],
                                         start=first, stop=last),
                                     inc=(sp if last else None))
                        b.wait("ve", sp)
                        b.op("ve", lambda v, t=t, dn=dn, pr=pr:
                             v.tensor_tensor(
                                 x_sb[:, t * 1024 + dn * 512:t * 1024 + (dn + 1) * 512],
                                 pvf[:, pr * 512:(pr + 1) * 512],
                                 x_sb[:, t * 1024 + dn * 512:t * 1024 + (dn + 1) * 512],
                                 ALU.add), inc=sv)
                        tok_po[pr] = sv.n
                tok_h = [sp.n, sp.n]
                tok_o = [sp.n, sp.n]
                tok_pvf = sv.n
            if e + 1 < NE and e >= 1:
                slot_next = (e + 1) % 2
                b.wait("gp", sp, tok_o[slot_next])
                loadw(O_[slot_next], dram_w["e_wout"][e + 1], 4)
                wlo[e + 1] = swl.n

        b.wait("gp", sp)
        loadw(wb[:, 0:8192], dram_w["ws_in"], 8)
        loadw(wb[:, 8192:16384], dram_w["vs_in"], 8)
        wsl1 = swl.n
        b.wait("te", swl, wsl1)
        for hsc in range(ND):
            b.wait("te", ss, tok_ph)
            b.wait("te", sv, tok_pvf)
            b.wait("te", sp, tok_h[hsc // 4])
            for tc in range(4):
                for kdc in range(ND):
                    b.op("te", lambda te, hsc=hsc, tc=tc, kdc=kdc: te.matmul(
                        ph[:, tc * 512:(tc + 1) * 512],
                        wb[:, kdc * 1024 + hsc * 128:kdc * 1024 + (hsc + 1) * 128],
                        xt[:, kdc * 2048 + tc * 512:kdc * 2048 + (tc + 1) * 512],
                        start=(kdc == 0), stop=(kdc == ND - 1)))
            for tc in range(4):
                for kdc in range(ND):
                    last = (tc == 3 and kdc == ND - 1)
                    b.op("te", lambda te, hsc=hsc, tc=tc, kdc=kdc: te.matmul(
                        pvf[:, tc * 512:(tc + 1) * 512],
                        wb[:, 8192 + kdc * 1024 + hsc * 128:
                           8192 + kdc * 1024 + (hsc + 1) * 128],
                        xt[:, kdc * 2048 + tc * 512:kdc * 2048 + (tc + 1) * 512],
                        start=(kdc == 0), stop=(kdc == ND - 1)),
                        inc=(sp if last else None))
            b.wait("sc", sp)
            b.wait("sc", sv, tok_zb)
            b.op("sc", lambda s: s.activation(zb[:, :], ph[:, :], AF.Silu),
                 inc=ss)
            tok_ph = ss.n
            b.wait("ve", ss)
            b.op("ve", lambda v, hsc=hsc: v.scalar_tensor_tensor(
                hsl[:, hsc * 2048:(hsc + 1) * 2048], pvf[:, :], 0.0,
                zb[:, :], ALU.bypass, ALU.mult), inc=sv)
            tok_pvf = sv.n
            tok_zb = sv.n
        hs_done = sv.n
        b.wait("gp", sp)
        loadw(wb[:, 0:8192], dram_w["ws_out"], 8)
        wsl2 = swl.n
        b.wait("te", swl, wsl2)
        b.wait("te", sv, hs_done)
        for t in range(NT):
            for dn in range(2):
                pr = (2 * t + dn) % 2
                b.wait("te", sv, tok_po[pr])
                for hkc in range(ND):
                    b.op("te", lambda te, hkc=hkc, t=t, dn=dn, pr=pr:
                         te.matmul(
                             pvf[:, pr * 512:(pr + 1) * 512],
                             hsl[:, hkc * 2048 + t * 128:hkc * 2048 + (t + 1) * 128],
                             wb[:, hkc * 1024 + dn * 512:hkc * 1024 + (dn + 1) * 512],
                             start=(hkc == 0), stop=(hkc == ND - 1)),
                         inc=(sp if hkc == ND - 1 else None))
                b.wait("ve", sp)
                b.op("ve", lambda v, t=t, dn=dn, pr=pr: v.tensor_tensor(
                    x_sb[:, t * 1024 + dn * 512:t * 1024 + (dn + 1) * 512],
                    pvf[:, pr * 512:(pr + 1) * 512],
                    x_sb[:, t * 1024 + dn * 512:t * 1024 + (dn + 1) * 512],
                    ALU.add), inc=sv)
                tok_po[pr] = sv.n
    return sv, sv.n


def emit_kvpayload(b, xt, wkT_sb, wvT_sb, bkt_sb, bvrow_sb, mask_sb, consts,
                   kt_d, v65_d, xt_ready, w_ready, mask_ready):
    sp = b.sem("kp_p")
    sv = b.sem("kp_v")
    ss = b.sem("kp_s")
    sd = b.sem("kp_d")
    eye = consts["c_eye"]
    with ExitStack() as ps:
        ktile = b.sbuf("kp_kt", [128, 2 * 512], BF16, stack=ps)
        ksq = b.sbuf("kp_ksq", [128, 512], F32, stack=ps)
        kn2 = b.sbuf("kp_kn2", [128, 32], F32, stack=ps)
        v65t = b.sbuf("kp_v65", [128, 2 * 1040], BF16, stack=ps)
        pkb = [b.psum(f"kp_pk{i}", [128, 512], stack=ps) for i in range(2)]
        pn = b.psum("kp_pn", [2, 512], stack=ps)
        ptr8 = b.psum("kp_ptr8", [16, 128], BF16, stack=ps)

        b.wait("te", *xt_ready)
        b.wait("te", *w_ready)
        tok_pkb = [0, 0]
        tok_dma = [0, 0]
        tok_ksq = 0
        tok_pn = 0
        for oc in range(ND):
            for tc in range(4):
                par = (oc * 4 + tc) % 2
                b.wait("te", ss, tok_pkb[par])
                for kic in range(ND):
                    b.op("te", lambda te, oc=oc, tc=tc, kic=kic, par=par:
                         te.matmul(
                             pkb[par][:, :],
                             wkT_sb[:, kic * 1024 + oc * 128:kic * 1024 + (oc + 1) * 128],
                             xt[:, kic * 2048 + tc * 512:kic * 2048 + (tc + 1) * 512],
                             start=(kic == 0), stop=(kic == ND - 1)),
                         inc=(sp if kic == ND - 1 else None))
                b.wait("sc", sp)
                b.wait("sc", sd, tok_dma[par])
                b.op("sc", lambda s, oc=oc, par=par: s.activation(
                    ktile[:, par * 512:(par + 1) * 512], pkb[par][:, :],
                    AF.Identity, bias=bkt_sb[:, oc:oc + 1]), inc=ss)
                tok_pkb[par] = ss.n
                b.wait("sy", ss)
                b.op("sy", lambda e, oc=oc, tc=tc, par=par: e.dma_start(
                    out=kt_d[oc * 128:(oc + 1) * 128, tc * 512:(tc + 1) * 512],
                    in_=ktile[:, par * 512:(par + 1) * 512]), dinc=sd)
                tok_dma[par] = sd.n
                b.wait("sc", sp, tok_ksq)
                b.opf("sc", lambda s, par=par: s.activation(
                    ksq[:, :], ktile[:, par * 512:(par + 1) * 512],
                    AF.Square))
                b.wait("te", b.selfsem["sc"])
                b.wait("te", b.selfsem["ve"], tok_pn)
                b.op("te", lambda te: te.matmul(
                    pn[:, :], consts["c_hm2"][:, :], ksq[:, :],
                    start=True, stop=True), inc=sp)
                tok_ksq = sp.n
                b.wait("ve", sp)
                if tc == 0:
                    b.opf("ve", lambda v, oc=oc: v.reduce_max(
                        kn2[0:2, oc:oc + 1], pn[:, :], AX.X))
                else:
                    b.opf("ve", lambda v: v.reduce_max(kn2[0:2, 16:17],
                                                       pn[:, :], AX.X))
                    b.opf("ve", lambda v, oc=oc: v.tensor_max(
                        kn2[0:2, oc:oc + 1], kn2[0:2, oc:oc + 1],
                        kn2[0:2, 16:17]))
                tok_pn = b.selfsem["ve"].n
        kn2b = b.sbuf("kp_kn2b", [16, 16], BF16, stack=ps)
        b.opf("ve", lambda v: v.tensor_copy(kn2b[0:2, 0:8], kn2[0:2, 0:8]))
        b.wait("te", b.selfsem["ve"])
        b.op("te", lambda te: te.transpose(ptr8[0:8, 0:2], kn2b[0:2, 0:8],
                                           eye[0:2, 0:2]), inc=sp)
        b.wait("ve", sp)
        b.opf("ve", lambda v: v.tensor_copy(kn2[0:8, 20:22], ptr8[0:8, 0:2]))
        b.opf("ve", lambda v: v.reduce_max(kn2[0:8, 22:23], kn2[0:8, 20:22],
                                           AX.X))
        b.opf("ve", lambda v: v.tensor_copy(kn2b[0:8, 8:9],
                                            kn2[0:8, 22:23]))
        b.wait("sy", b.selfsem["ve"])
        b.op("sy", lambda e: e.dma_start(
            out=kt_d[1024:1025, 0:8].rearrange("a (c u) -> (a c) u", u=1),
            in_=kn2b[0:8, 8:9]), dinc=sd)
        mrow = b.sbuf("kp_mrow", [16, 128], BF16, stack=ps)
        mbf = b.sbuf("kp_mbf", [128, 16], BF16, stack=ps)
        b.wait("ve", *mask_ready)
        b.opf("ve", lambda v: v.tensor_copy(mbf[:, :], mask_sb[:, :]))
        b.wait("te", b.selfsem["ve"])
        b.op("te", lambda te: te.transpose(ptr8[:, :], mbf[:, :], eye[:, :]),
             inc=sp)
        b.wait("ve", sp)
        b.op("ve", lambda v: v.tensor_copy(mrow[:, :], ptr8[:, :]), inc=sv)
        b.wait("sy", sv)
        b.op("sy", lambda e: e.dma_start(
            out=kt_d[1025:1026, :].rearrange("a (t p) -> (a t) p", p=128),
            in_=mrow[:, :]), dinc=sd)
        tok_pv = [0, 0]
        tok_v65dma = [0, 0]
        b.wait("te", ss)
        for t in range(NT):
            for dn in range(2):
                par = dn
                b.wait("te", sv, tok_pv[par])
                for kic in range(ND):
                    b.op("te", lambda te, t=t, dn=dn, kic=kic, par=par:
                         te.matmul(
                             pkb[par][:, :],
                             xt[:, kic * 2048 + t * 128:kic * 2048 + (t + 1) * 128],
                             wvT_sb[:, kic * 1024 + dn * 512:kic * 1024 + (dn + 1) * 512],
                             start=(kic == 0), stop=False))
                b.op("te", lambda te, dn=dn, par=par: te.matmul(
                    pkb[par][:, :], consts["c_ones"][0:1, 0:128],
                    bvrow_sb[0:1, dn * 512:(dn + 1) * 512],
                    start=False, stop=True), inc=sp)
                b.wait("ve", sp)
                sl = t % 2
                b.wait("ve", sd, tok_v65dma[sl])
                b.op("ve", lambda v, t=t, dn=dn, par=par, sl=sl:
                     v.tensor_single_scalar(
                         bass.AP(v65t, sl * 1040 + dn * 8 * 65,
                                 [[2 * 1040, 128], [65, 8], [1, 64]]),
                         bass.AP(pkb[par], 0, [[512, 128], [64, 8], [1, 64]]),
                         mask_sb[:, t:t + 1], ALU.mult), inc=sv)
                tok_pv[par] = sv.n
                b.op("ve", lambda v, t=t, dn=dn, sl=sl: v.tensor_copy(
                    bass.AP(v65t, sl * 1040 + dn * 8 * 65 + 64,
                            [[2 * 1040, 128], [65, 8]]),
                    mask_sb[:, t:t + 1].to_broadcast([128, 8])), inc=sv)
            b.wait("sy", sv)
            b.op("sy", lambda e, t=t, sl=t % 2: e.dma_start(
                out=v65_d[t * 128:(t + 1) * 128, :],
                in_=v65t[:, sl * 1040:sl * 1040 + 1040]), dinc=sd)
            tok_v65dma[t % 2] = sd.n
        b.wait("sy", sd)
    return sd, sd.n


def build_launch_b(stage="full"):
    nc = bass.Bass("TRN2", target_bir_lowering=False)
    with ExitStack() as stack:
        b = B(nc, stack)
        dram = {}
        for nm, shape in [("x", [TC, D]), ("kvp_all", [256, 1040]),
                          ("wq", [D, D]), ("wo", [D, D]),
                          ("gwt", [D, 16]),
                          ("ws_in", [D, D]), ("vs_in", [D, D]),
                          ("ws_out", [D, D]),
                          ("e_win", [NE, D, EH]), ("e_vin", [NE, D, EH]),
                          ("e_wout", [NE, EH, D]),
                          ("wkT", [D, D]), ("wvT", [D, D]),
                          ("bk_t", [128, 8]), ("bv_row", [1, D])]:
            dram[nm] = nc.dram_tensor(nm, shape, F32,
                                      kind="ExternalInput").ap()
        x2_d = nc.dram_tensor("x2", [TC, D], F32, kind="ExternalOutput").ap()
        kt_d = nc.dram_tensor("kt", [1026, 2048], BF16,
                              kind="ExternalOutput").ap()
        v65_d = nc.dram_tensor("v65", [TC, 1040], BF16,
                               kind="ExternalOutput").ap()

        consts = decl_consts(b, ["c_eye", "c_misc", "c_onesf", "c_erep",
                                 "c_hm2", "c_ones"])
        x_sb = b.sbuf("x_sbuf", [128, NT * 1024], F32)
        xt = b.sbuf("xt_sbuf", [128, ND * 2048], BF16)
        wb = b.sbuf("wb_sbuf", [128, 16384], BF16)
        kvb2 = b.sbuf("kvb2_sb", [128, 1040], BF16)
        mask_sb = b.sbuf("mask_sb", [128, 16], F32)
        gwt_sb = b.sbuf("gwt_sb", [128, 128], BF16)
        bkt_sb = b.sbuf("bkt_sb", [128, 8], F32)
        bvrow_sb = b.sbuf("bvrow_sb", [1, 1024], BF16)

        sw = b.sem("s_w")
        load_weight(b, dram["wq"], wb[:, 0:8192], 8, sw)
        load_weight(b, dram["wo"], wb[:, 8192:16384], 8, sw)
        wqwo_ld = sw.n
        load_x(b, dram["x"], x_sb)
        sld = b.sem("s_ld")
        kv_ready = emit_kv_sum(b, dram["kvp_all"], kvb2)

        xt_ready = emit_ln(b, "ln1", x_sb, xt, consts["c_eye"],
                           consts["c_misc"], (sld, sld.n))
        x1_done = emit_attn_a_main(b, xt, x_sb, wb[:, 0:8192],
                                   wb[:, 8192:16384], kvb2,
                                   consts["c_onesf"], consts["c_ones"],
                                   xt_ready, kv_ready,
                                   (sw, wqwo_ld), (sw, wqwo_ld))
        load_weight(b, dram["gwt"], gwt_sb, 8, sw)
        gwt_ld = sw.n
        xt2_ready = emit_ln(b, "ln2", x_sb, xt, consts["c_eye"],
                            consts["c_misc"], x1_done)
        dram_w = {k: dram[k] for k in
                  ("e_win", "e_vin", "e_wout", "ws_in", "vs_in", "ws_out")}
        dram_w = dict(dram_w)
        dram_w["e_win"] = [dram["e_win"][e] for e in range(NE)]
        dram_w["e_vin"] = [dram["e_vin"][e] for e in range(NE)]
        dram_w["e_wout"] = [dram["e_wout"][e] for e in range(NE)]
        if stage == "x1":
            sst = b.sem("s_x2")
            b.wait("sy", *x1_done)
            b.op("sy", lambda e: e.dma_start(
                out=x2_d.rearrange("(t p) d -> p t d", p=128),
                in_=x_sb[:, :].rearrange("p (t d) -> p t d", t=NT)), dinc=sst)
            b.wait("sy", sst)
            block = stack.enter_context(nc.Block())
            b.play(block)
            return nc
        x2_done = emit_ffn(b, "f1", xt, x_sb, wb, gwt_sb, dram_w, consts,
                           xt2_ready, (sw, gwt_ld), x1_done)
        b.wait("gp", *x2_done)
        load_weight(b, dram["wkT"], wb[:, 0:8192], 8, sw)
        load_weight(b, dram["wvT"], wb[:, 8192:16384], 8, sw)
        b.op("gp", lambda e: e.dma_start(out=bkt_sb[:, :],
                                         in_=dram["bk_t"]), dinc=sw)
        b.op("gp", lambda e: e.dma_start(out=bvrow_sb[:, :],
                                         in_=dram["bv_row"]), dinc=sw)
        kw_ld = sw.n
        xt3_ready = emit_ln(b, "ln3", x_sb, xt, consts["c_eye"],
                            consts["c_misc"], x2_done, mask_sb=mask_sb)
        mask_ready = (b.sems["ln3v"], b.sems["ln3v"].n)
        kp_done = emit_kvpayload(b, xt, wb[:, 0:8192], wb[:, 8192:16384],
                                 bkt_sb, bvrow_sb, mask_sb, consts, kt_d,
                                 v65_d, xt3_ready, (sw, kw_ld), mask_ready)
        sst = b.sem("s_x2")
        b.wait("sy", *x2_done)
        b.op("sy", lambda e: e.dma_start(
            out=x2_d.rearrange("(t p) d -> p t d", p=128),
            in_=x_sb[:, :].rearrange("p (t d) -> p t d", t=NT)), dinc=sst)
        b.wait("sy", sst)
        b.wait("gp", sst)
        b.wait("gp", *kp_done)

        block = stack.enter_context(nc.Block())
        b.play(block)
    return nc



def _shard_inputs(inputs):
    x = np.ascontiguousarray(
        np.asarray(inputs["L0_attn_a_x"], np.float32).reshape(2 * 8192, D))
    return [np.ascontiguousarray(x[c * TC:(c + 1) * TC]) for c in range(8)]


def run_launch_a(inputs):
    xs = _shard_inputs(inputs)
    consts = _host_consts()
    nc = build_launch_a()
    in_maps = [{"x": xs[c],
                "wk": np.asarray(inputs["L0_attn_a_Wk"], np.float32),
                "wv": np.asarray(inputs["L0_attn_a_Wv"], np.float32),
                "c_eye": consts["c_eye"], "c_misc": consts["c_misc"]}
               for c in range(8)]
    res = run_bass_kernel_spmd(nc, in_maps, core_ids=list(range(8)))
    return [np.asarray(r["kvp"], np.float32) for r in res.results]


def _b_inmaps(inputs, xs, kvps):
    consts = _host_consts()
    f32 = lambda k: np.asarray(inputs[k], np.float32)
    T = lambda a: np.ascontiguousarray(np.asarray(a, np.float32).T)
    shared = {
        "wq": f32("L0_attn_a_Wq"), "wo": f32("L0_attn_a_Wo"),
        "gwt": T(inputs["L0_ffn_gate_w"]),
        "ws_in": f32("L0_ffn_ws_in"), "vs_in": f32("L0_ffn_vs_in"),
        "ws_out": f32("L0_ffn_ws_out"),
        "e_win": f32("L0_ffn_e_win"), "e_vin": f32("L0_ffn_e_vin"),
        "e_wout": f32("L0_ffn_e_wout"),
        "wkT": T(inputs["L0_attn_b_wk_w"]), "wvT": T(inputs["L0_attn_b_wv_w"]),
        "bk_t": np.ascontiguousarray(
            np.asarray(inputs["L0_attn_b_wk_b"], np.float32).reshape(8, 128).T),
        "bv_row": np.asarray(inputs["L0_attn_b_wv_b"],
                             np.float32).reshape(1, D),
    }
    shared.update({k: consts[k] for k in ("c_eye", "c_misc", "c_onesf",
                                          "c_erep", "c_hm2", "c_ones")})
    maps = []
    for c in range(8):
        bb = c // 4
        kvp_all = np.concatenate([kvps[bb * 4 + r] for r in range(4)], axis=0)
        m = dict(shared)
        m["x"] = xs[c]
        m["kvp_all"] = np.ascontiguousarray(kvp_all)
        maps.append(m)
    return maps


def run_launch_b(inputs, kvps, stage="full"):
    xs = _shard_inputs(inputs)
    nc = build_launch_b(stage)
    res = run_bass_kernel_spmd(nc, _b_inmaps(inputs, xs, kvps),
                               core_ids=list(range(8)))
    x2 = [np.asarray(r["x2"], np.float32) for r in res.results]
    if stage == "x1":
        return x2, None, None
    kt = [np.asarray(r["kt"]).view(bf16) if r["kt"].dtype != bf16
          else np.asarray(r["kt"]) for r in res.results]
    v65 = [np.asarray(r["v65"]) for r in res.results]
    return x2, kt, v65



def emit_attn_b(b, xt, x_sb, wb, bqt_sb, borow_sb, kt_full, v65_full, consts,
                xt_ready, wq_ready, wo_ready, kt_rows=1026, nshard=4):
    sp = b.sem("ab_p")
    sv = b.sem("ab_v")
    ss = b.sem("ab_s")
    sd = b.sem("ab_d")
    skv = b.sem("ab_kv")
    scd = b.sem("ab_cd")
    eye = consts["c_eye"]
    onesb = consts["c_ones"]
    onesf = consts["c_onesf"]
    with ExitStack() as ps:
        qt = b.sbuf("ab_qt", [128, 16384], BF16, stack=ps)
        at_qc = b.sbuf("ab_at", [128, 4096], BF16, stack=ps)
        pt = b.sbuf("ab_pt", [128, 3 * 1024], BF16, stack=ps)
        NKV = 6
        ktc = b.sbuf("ab_ktc", [128, NKV * 256], BF16, stack=ps)
        vc = b.sbuf("ab_vc", [128, NKV * 260], BF16, stack=ps)
        atst = b.sbuf("ab_atst", [64, 2 * 512], BF16, stack=ps)
        asm = b.sbuf("ab_asm", [128, 1024], F32, stack=ps)
        rrb = b.sbuf("ab_rrb", [1, 1024], BF16, stack=ps)
        qn = b.sbuf("ab_qn", [128, 48], F32, stack=ps)
        ncb = b.sbuf("ab_ncb", [16, 16], BF16, stack=ps)
        ncrow = b.sbuf("ab_ncrow", [1, 8], BF16, stack=ps)
        negc = b.sbuf("ab_negc", [128, 8], F32, stack=ps)

        with ExitStack() as qs:
            pq = [b.psum(f"ab_pq{i}", [128, 512], stack=qs) for i in range(2)]
            pn = b.psum("ab_pn", [2, 512], stack=qs)
            ptr8 = b.psum("ab_ptr8", [16, 128], BF16, stack=qs)
            pnc = b.psum("ab_pnc", [128, 8], stack=qs)

            b.wait("te", *xt_ready)
            b.wait("te", *wq_ready)
            tok_pq = [0, 0]
            tok_sq = 0
            tok_pn = 0
            for oc in range(ND):
                for qc in range(4):
                    par = (oc * 4 + qc) % 2
                    b.wait("te", ss, tok_pq[par])
                    for kic in range(ND):
                        b.op("te", lambda te, oc=oc, qc=qc, kic=kic, par=par:
                             te.matmul(
                                 pq[par][:, :],
                                 wb[:, kic * 1024 + oc * 128:kic * 1024 + (oc + 1) * 128],
                                 xt[:, kic * 2048 + qc * 512:kic * 2048 + (qc + 1) * 512],
                                 start=(kic == 0), stop=(kic == ND - 1)),
                             inc=(sp if kic == ND - 1 else None))
                    b.wait("sc", sp)
                    b.op("sc", lambda s, oc=oc, qc=qc, par=par: s.activation(
                        qt[:, oc * 2048 + qc * 512:oc * 2048 + (qc + 1) * 512],
                        pq[par][:, :], AF.Identity,
                        bias=bqt_sb[:, oc:oc + 1]), inc=ss)
                    tok_pq[par] = ss.n
                    b.wait("sc", sp, tok_sq)
                    b.opf("sc", lambda s, oc=oc, qc=qc: s.activation(
                        asm[:, 0:512],
                        qt[:, oc * 2048 + qc * 512:oc * 2048 + (qc + 1) * 512],
                        AF.Square))
                    b.wait("te", b.selfsem["sc"])
                    b.wait("te", b.selfsem["ve"], tok_pn)
                    b.op("te", lambda te: te.matmul(
                        pn[:, :], consts["c_hm2"][:, :], asm[:, 0:512],
                        start=True, stop=True), inc=sp)
                    tok_sq = sp.n
                    b.wait("ve", sp)
                    if qc == 0:
                        b.opf("ve", lambda v, oc=oc: v.reduce_max(
                            qn[0:2, 16 + oc:17 + oc], pn[:, :], AX.X))
                    else:
                        b.opf("ve", lambda v: v.reduce_max(
                            qn[0:2, 32:33], pn[:, :], AX.X))
                        b.opf("ve", lambda v, oc=oc: v.tensor_max(
                            qn[0:2, 16 + oc:17 + oc], qn[0:2, 16 + oc:17 + oc],
                            qn[0:2, 32:33]))
                    tok_pn = b.selfsem["ve"].n
            b.opf("ve", lambda v: v.tensor_copy(ncb[0:2, 0:8],
                                                qn[0:2, 16:24]))
            b.wait("te", b.selfsem["ve"])
            b.op("te", lambda te: te.transpose(ptr8[0:8, 0:2], ncb[0:2, 0:8],
                                               eye[0:2, 0:2]), inc=sp)
            b.wait("ve", sp)
            b.opf("ve", lambda v: v.tensor_copy(qn[0:8, 33:35],
                                                ptr8[0:8, 0:2]))
            b.opf("ve", lambda v: v.reduce_max(qn[0:8, 35:36],
                                               qn[0:8, 33:35], AX.X))
            if nshard == 4:
                knsrc = [kt_full[r * kt_rows + 1024:r * kt_rows + 1025, 0:8]
                         for r in range(4)]
            else:
                knsrc = [kt_full[s * kt_rows + 1024:s * kt_rows + 1025,
                                 g * 2048:g * 2048 + 8]
                         for s in range(2) for g in range(2)]
            for r in range(4):
                b.op("sy", lambda e, r=r: e.dma_start(
                    out=ncb[0:8, 8 + r:9 + r],
                    in_=knsrc[r].rearrange("a (c u) -> (a c) u", u=1)),
                    dinc=scd)
            b.wait("ve", scd)
            b.opf("ve", lambda v: v.tensor_copy(qn[0:8, 36:40],
                                                ncb[0:8, 8:12]))
            b.opf("ve", lambda v: v.reduce_max(qn[0:8, 40:41],
                                               qn[0:8, 36:40], AX.X))
            b.opf("ve", lambda v: v.tensor_mul(qn[0:8, 41:42],
                                               qn[0:8, 40:41],
                                               qn[0:8, 35:36]))
            b.wait("sc", b.selfsem["ve"])
            b.opf("sc", lambda s: s.activation(qn[0:8, 42:43],
                                               qn[0:8, 41:42], AF.Ln))
            b.opf("sc", lambda s: s.activation(
                qn[0:8, 43:44], qn[0:8, 42:43], AF.Exp, scale=0.5,
                bias=consts["c_misc"][0:8, 1:2]))
            b.wait("ve", b.selfsem["sc"])
            b.opf("ve", lambda v: v.tensor_single_scalar(
                qn[0:8, 44:45], qn[0:8, 43:44], -1.0, ALU.mult))
            b.opf("ve", lambda v: v.tensor_copy(ncb[0:8, 12:13],
                                                qn[0:8, 44:45]))
            b.wait("te", b.selfsem["ve"])
            b.op("te", lambda te: te.transpose(ptr8[0:1, 0:8],
                                               ncb[0:8, 12:13],
                                               eye[0:8, 0:8]), inc=sp)
            b.wait("ve", sp)
            b.opf("ve", lambda v: v.tensor_copy(ncrow[0:1, 0:8],
                                                ptr8[0:1, 0:8]))
            b.wait("te", b.selfsem["ve"])
            b.op("te", lambda te: te.matmul(
                pnc[:, :], onesb[0:1, 0:128], ncrow[0:1, 0:8],
                start=True, stop=True), inc=sp)
            b.wait("ve", sp)
            b.opf("ve", lambda v: v.tensor_copy(negc[:, :], pnc[:, :]))
            negc_done = b.selfsem["ve"].n

        pss = [b.psum(f"ab_ps{i}", [128, 1024], stack=ps) for i in range(2)]
        pvA = b.psum("ab_pvA", [65, 512], stack=ps)
        pvB = b.psum("ab_pvB", [65, 512], stack=ps)
        po = [b.psum(f"ab_po{i}", [128, 512], stack=ps) for i in range(2)]

        b.wait("sc", b.selfsem["ve"], negc_done)
        tok_kv = [0] * 6
        tok_ps = [0, 0]
        tok_pt = [0, 0, 0]
        tok_atst = [0, 0]
        tok_at = 0
        tok_po = [0, 0]
        nhalf = 0
        for qc in range(4):
            for hp in range(8):
                halves = [(sc2, si) for sc2 in range(32) for si in range(2)]
                exp_tok = {}
                pend_pv = None
                for idx in range(len(halves) + 1):
                    if idx < len(halves):
                        sc2, si = halves[idx]
                        cpar = sc2 % NKV
                        if si == 0:
                            if nshard == 4:
                                rb = sc2 // 8
                                lc = (sc2 * 256) % 2048
                            else:
                                rb = sc2 // 16
                                lc = (sc2 * 256) % 4096
                            b.wait("sy", sp, tok_kv[cpar])
                            b.op("sy", lambda e, rb=rb, lc=lc, hp=hp,
                                 cpar=cpar: e.dma_start(
                                     out=ktc[:, cpar * 256:(cpar + 1) * 256],
                                     in_=kt_full[rb * kt_rows + hp * 128:
                                                 rb * kt_rows + (hp + 1) * 128,
                                                 lc:lc + 256]), dinc=skv)
                            b.wait("gp", sp, tok_kv[cpar])
                            b.op("gp", lambda e, sc2=sc2, hp=hp, cpar=cpar:
                                 e.dma_start(
                                     out=vc[:, cpar * 260:(cpar + 1) * 260]
                                     .rearrange("p (s c) -> p s c", s=2),
                                     in_=v65_full[sc2 * 256:(sc2 + 1) * 256,
                                                  hp * 130:(hp + 1) * 130]
                                     .rearrange("(s p) c -> p s c", p=128)),
                                 dinc=skv)
                        kv_ok = skv.n
                        spar = nhalf % 2
                        tpar = nhalf % 3
                        nhalf += 1
                        b.wait("te", skv, kv_ok)
                        b.wait("te", ss, tok_ps[spar])
                        for j in range(2):
                            b.op("te", lambda te, si=si, j=j, hp=hp, qc=qc,
                                 cpar=cpar, spar=spar: te.matmul(
                                     pss[spar][:, j * 512:(j + 1) * 512],
                                     ktc[j * 64:(j + 1) * 64,
                                         cpar * 256 + si * 128:
                                         cpar * 256 + (si + 1) * 128],
                                     qt[j * 64:(j + 1) * 64,
                                        hp * 2048 + qc * 512:
                                        hp * 2048 + (qc + 1) * 512],
                                     start=True, stop=True),
                                 inc=(sp if j == 1 else None))
                        s_tok = sp.n
                    if pend_pv is not None:
                        psi, pcpar, ptpar, pfirst, plast = pend_pv
                        b.wait("te", ss, exp_tok[idx - 1])
                        b.op("te", lambda te, si=psi, cpar=pcpar, tpar=ptpar,
                             first=pfirst, last=plast: te.matmul(
                                 pvA[:, :],
                                 vc[:, cpar * 260 + si * 130:
                                    cpar * 260 + si * 130 + 65],
                                 pt[:, tpar * 1024:tpar * 1024 + 512],
                                 start=first, stop=last))
                        b.op("te", lambda te, si=psi, cpar=pcpar, tpar=ptpar,
                             first=pfirst, last=plast: te.matmul(
                                 pvB[:, :],
                                 vc[:, cpar * 260 + si * 130 + 65:
                                    cpar * 260 + si * 130 + 130],
                                 pt[:, tpar * 1024 + 512:tpar * 1024 + 1024],
                                 start=first, stop=plast), inc=sp)
                        tok_pt[ptpar] = sp.n
                        tok_kv[pcpar] = sp.n
                    if idx < len(halves):
                        b.wait("sc", sp, s_tok)
                        b.wait("sc", sp, tok_pt[tpar])
                        b.op("sc", lambda s, spar=spar, tpar=tpar, hp=hp:
                             s.activation(
                                 pt[:, tpar * 1024:(tpar + 1) * 1024],
                                 pss[spar][:, :], AF.Exp, scale=0.125,
                                 bias=negc[:, hp:hp + 1]), inc=ss)
                        tok_ps[spar] = ss.n
                        exp_tok[idx] = ss.n
                        pend_pv = (si, cpar, tpar,
                                   idx == 0, idx == len(halves) - 1)
                pend_pv = None
                for j, pv in ((0, pvA), (1, pvB)):
                    b.wait("ve", sp)
                    b.opf("ve", lambda v, j=j, pv=pv: _recip16(b, v,
                        rrb[0:1, j * 512:(j + 1) * 512], pv[64:65, :]))
                    b.wait("te", b.selfsem["ve"])
                    b.wait("te", ss)
                    b.op("te", lambda te, j=j: te.matmul(
                        pss[0][0:64, j * 512:(j + 1) * 512],
                        onesb[0:1, 0:64],
                        rrb[0:1, j * 512:(j + 1) * 512],
                        start=True, stop=True), inc=sp)
                    b.wait("sc", sp)
                    b.op("sc", lambda s, j=j: s.copy(
                        asm[0:64, j * 512:(j + 1) * 512],
                        pss[0][0:64, j * 512:(j + 1) * 512]), inc=ss)
                    b.wait("ve", ss)
                    b.wait("ve", sd, tok_atst[j])
                    b.op("ve", lambda v, j=j, pv=pv: v.tensor_tensor(
                        atst[:, j * 512:(j + 1) * 512], pv[0:64, :],
                        asm[0:64, j * 512:(j + 1) * 512], ALU.mult), inc=sv)
                    b.wait("sy", sv)
                    b.wait("sy", sp, tok_at)
                    b.op("sy", lambda e, j=j, hp=hp: e.dma_start(
                        out=at_qc[j * 64:(j + 1) * 64,
                                  hp * 512:(hp + 1) * 512],
                        in_=atst[:, j * 512:(j + 1) * 512]), dinc=sd)
                    tok_atst[j] = sd.n
                tok_ps[0] = ss.n
            b.wait("te", sd)
            b.wait("te", *wo_ready)
            for u in range(4):
                for dn in range(2):
                    pr = (4 * qc + u * 2 + dn) % 2
                    b.wait("te", sv, tok_po[pr])
                    for kdc in range(ND):
                        b.op("te", lambda te, kdc=kdc, u=u, dn=dn, pr=pr:
                             te.matmul(
                                 po[pr][:, :],
                                 at_qc[:, kdc * 512 + u * 128:kdc * 512 + (u + 1) * 128],
                                 wb[:, 8192 + kdc * 1024 + dn * 512:
                                    8192 + kdc * 1024 + (dn + 1) * 512],
                                 start=(kdc == 0), stop=False))
                    b.op("te", lambda te, dn=dn, pr=pr: te.matmul(
                        po[pr][:, :], onesb[0:1, 0:128],
                        borow_sb[0:1, dn * 512:(dn + 1) * 512],
                        start=False, stop=True), inc=sp)
                    b.wait("ve", sp)
                    b.op("ve", lambda v, qc=qc, u=u, dn=dn, pr=pr:
                         v.tensor_tensor(
                             x_sb[:, (qc * 4 + u) * 1024 + dn * 512:
                                  (qc * 4 + u) * 1024 + (dn + 1) * 512],
                             po[pr][:, :],
                             x_sb[:, (qc * 4 + u) * 1024 + dn * 512:
                                  (qc * 4 + u) * 1024 + (dn + 1) * 512],
                             ALU.add), inc=sv)
                    tok_po[pr] = sv.n
            tok_at = sp.n
    return sv, sv.n


def build_launch_c():
    nc = bass.Bass("TRN2", target_bir_lowering=False)
    with ExitStack() as stack:
        b = B(nc, stack)
        dram = {}
        for nm, shape, dt in [
                ("x2", [TC, D], F32), ("wqT", [D, D], F32),
                ("woT", [D, D], F32), ("bq_t", [128, 8], F32),
                ("bo_row", [1, D], F32), ("gwt", [D, 16], F32),
                ("ws_in", [D, D], F32), ("vs_in", [D, D], F32),
                ("ws_out", [D, D], F32),
                ("e_win", [NE, D, EH], F32), ("e_vin", [NE, D, EH], F32),
                ("e_wout", [NE, EH, D], F32),
                ("kt_full", [4 * 1026, 2048], BF16),
                ("v65_full", [8192, 1040], BF16)]:
            dram[nm] = nc.dram_tensor(nm, shape, dt,
                                      kind="ExternalInput").ap()
        out_d = nc.dram_tensor("out", [TC, D], F32,
                               kind="ExternalOutput").ap()

        consts = decl_consts(b, ["c_eye", "c_misc", "c_onesf", "c_erep",
                                 "c_hm2", "c_ones"])
        x_sb = b.sbuf("x_sbuf", [128, NT * 1024], F32)
        xt = b.sbuf("xt_sbuf", [128, ND * 2048], BF16)
        wb = b.sbuf("wb_sbuf", [128, 16384], BF16)
        gwt_sb = b.sbuf("gwt_sb", [128, 128], BF16)
        bqt_sb = b.sbuf("bqt_sb", [128, 8], F32)
        borow_sb = b.sbuf("borow_sb", [1, 1024], BF16)

        sw = b.sem("s_w")
        load_weight(b, dram["wqT"], wb[:, 0:8192], 8, sw)
        load_weight(b, dram["woT"], wb[:, 8192:16384], 8, sw)
        b.op("gp", lambda e: e.dma_start(out=bqt_sb[:, :],
                                         in_=dram["bq_t"]), dinc=sw)
        b.op("gp", lambda e: e.dma_start(out=borow_sb[:, :],
                                         in_=dram["bo_row"]), dinc=sw)
        load_weight(b, dram["gwt"], gwt_sb, 8, sw)
        w_ld = sw.n
        load_x(b, dram["x2"], x_sb)
        sld = b.sem("s_ld")

        xt3_ready = emit_ln(b, "ln3", x_sb, xt, consts["c_eye"],
                            consts["c_misc"], (sld, sld.n))
        x3_done = emit_attn_b(b, xt, x_sb, wb, bqt_sb, borow_sb,
                              dram["kt_full"], dram["v65_full"], consts,
                              xt3_ready, (sw, w_ld), (sw, w_ld))
        xt4_ready = emit_ln(b, "ln4", x_sb, xt, consts["c_eye"],
                            consts["c_misc"], x3_done)
        dram_w = {"ws_in": dram["ws_in"], "vs_in": dram["vs_in"],
                  "ws_out": dram["ws_out"],
                  "e_win": [dram["e_win"][e] for e in range(NE)],
                  "e_vin": [dram["e_vin"][e] for e in range(NE)],
                  "e_wout": [dram["e_wout"][e] for e in range(NE)]}
        out_done = emit_ffn(b, "f2", xt, x_sb, wb, gwt_sb, dram_w, consts,
                            xt4_ready, (sw, w_ld), x3_done)
        sst = b.sem("s_out")
        b.wait("sy", *out_done)
        b.op("sy", lambda e: e.dma_start(
            out=out_d.rearrange("(t p) d -> p t d", p=128),
            in_=x_sb[:, :].rearrange("p (t d) -> p t d", t=NT)), dinc=sst)
        b.wait("sy", sst)
        b.wait("gp", sst)

        block = stack.enter_context(nc.Block())
        b.play(block)
    return nc


def _c_inmaps(inputs, x2s, kts, v65s):
    consts = _host_consts()
    f32 = lambda k: np.asarray(inputs[k], np.float32)
    T = lambda a: np.ascontiguousarray(np.asarray(a, np.float32).T)
    shared = {
        "wqT": T(inputs["L0_attn_b_wq_w"]), "woT": T(inputs["L0_attn_b_wo_w"]),
        "bq_t": np.ascontiguousarray(
            np.asarray(inputs["L0_attn_b_wq_b"], np.float32).reshape(8, 128).T),
        "bo_row": np.asarray(inputs["L0_attn_b_wo_b"],
                             np.float32).reshape(1, D),
        "gwt": T(inputs["L0_ffn_gate_w"]),
        "ws_in": f32("L0_ffn_ws_in"), "vs_in": f32("L0_ffn_vs_in"),
        "ws_out": f32("L0_ffn_ws_out"),
        "e_win": f32("L0_ffn_e_win"), "e_vin": f32("L0_ffn_e_vin"),
        "e_wout": f32("L0_ffn_e_wout"),
    }
    shared.update({k: consts[k] for k in ("c_eye", "c_misc", "c_onesf",
                                          "c_erep", "c_hm2", "c_ones")})
    maps = []
    for c in range(8):
        bb = c // 4
        m = dict(shared)
        m["x2"] = x2s[c]
        m["kt_full"] = np.ascontiguousarray(
            np.concatenate([kts[bb * 4 + r] for r in range(4)], axis=0))
        m["v65_full"] = np.ascontiguousarray(
            np.concatenate([v65s[bb * 4 + r] for r in range(4)], axis=0))
        maps.append(m)
    return maps


def run_launch_c(inputs, x2s, kts, v65s):
    nc = build_launch_c()
    res = run_bass_kernel_spmd(nc, _c_inmaps(inputs, x2s, kts, v65s),
                               core_ids=list(range(8)))
    return [np.asarray(r["out"], np.float32) for r in res.results]


def kernel_3launch(**inputs):
    kvps = run_launch_a(inputs)
    x2s, kts, v65s = run_launch_b(inputs, kvps)
    outs = run_launch_c(inputs, x2s, kts, v65s)
    out = np.concatenate(outs, axis=0).reshape(2, 8192, D)
    return out



KT_ROWS = 1026
KT_SH_OFF = KT_ROWS * 4096


def emit_passA(b, x_d, wk_sb, wv_sb, eye_sb, cb, w_ready, kv_sh, regs):
    sv = b.sem("pAv")
    ss = b.sem("pAs")
    sp = b.sem("pAp")
    skv = b.sem("pAkv")
    sx = b.sem("pAx")
    sw = b.sem("pAw")
    with ExitStack() as ps:
        xtile = b.sbuf("pA_x", [128, 2 * 1024], F32, stack=ps)
        stats = b.sbuf("pA_st", [128, 20], F32, stack=ps)
        lnst = b.sbuf("pA_ln", [128, 2 * 1024], BF16, stack=ps)
        xtt = b.sbuf("pA_xtt", [128, 2 * 1024], BF16, stack=ps)
        kf = b.sbuf("pA_kf", [128, 2 * 1024], BF16, stack=ps)
        v65 = b.sbuf("pA_v65", [128, 2 * 1040], BF16, stack=ps)
        phs = b.sbuf("pA_phs", [128, 2 * 1024], F32, stack=ps)
        kvs = b.sbuf("pA_kvs", [64, 1040], F32, stack=ps)
        ptr0 = b.psum("pA_ptr0", [128, 128], BF16, stack=ps)
        pk = [b.psum(f"pA_pk{i}", [128, 512], stack=ps) for i in range(2)]
        pv = [b.psum(f"pA_pv{i}", [128, 512], stack=ps) for i in range(2)]
        pkv = [b.psum("pA_pkv0", [64, 455], stack=ps),
               b.psum("pA_pkv1", [64, 455], stack=ps),
               b.psum("pA_pkv2", [64, 130], stack=ps)]

        for sl in range(2):
            b.op("gp", lambda e, sl=sl: e.memset(
                bass.AP(v65, sl * 1040 + 64, [[2 * 1040, 128], [65, 16]]),
                1.0), inc=b.sem("pA_ones"))
        so = b.sems["pA_ones"]
        b.wait("ve", so)
        b.wait("te", *w_ready)

        tok_x = [0, 0]
        tok_lnst = [0, 0]
        tok_ptr = [0]
        tok_xtt = [0, 0]
        tok_k = [0, 0]
        tok_v = [0, 0]
        for t in range(32):
            sl = t % 2
            b.wait("sy", ss, tok_x[sl])
            b.op("sy", lambda e, t=t, sl=sl: e.dma_start(
                out=xtile[:, sl * 1024:(sl + 1) * 1024],
                in_=x_d[t * 128:(t + 1) * 128, :]), dinc=sx)
            b.wait("ve", sx)
            b.op("ve", lambda v, sl=sl: v.bn_stats(
                stats[:, 0:6], xtile[:, sl * 1024:sl * 1024 + 512]))
            b.opf("ve", lambda v, sl=sl: v.bn_stats(
                stats[:, 6:12], xtile[:, sl * 1024 + 512:sl * 1024 + 1024]))
            b.op("ve", lambda v: v.bn_aggr(stats[:, 12:14], stats[:, 0:12]),
                 inc=sv)
            b.wait("sc", sv)
            b.opf("sc", lambda s: s.activation(
                stats[:, 14:15], stats[:, 13:14], AF.Ln, bias=cb[:, 0:1]))
            b.op("sc", lambda s: s.activation(
                stats[:, 15:16], stats[:, 14:15], AF.Exp, scale=-0.5),
                inc=ss)
            b.wait("ve", ss)
            b.op("ve", lambda v: v.scalar_tensor_tensor(
                stats[:, 16:17], stats[:, 12:13], -1.0, stats[:, 15:16],
                ALU.mult, ALU.mult), inc=sv)
            b.wait("sc", sv)
            b.wait("sc", sp, tok_lnst[sl])
            b.op("sc", lambda s, sl=sl: s.activation(
                lnst[:, sl * 1024:(sl + 1) * 1024],
                xtile[:, sl * 1024:(sl + 1) * 1024], AF.Identity,
                bias=stats[:, 16:17], scale=stats[:, 15:16]), inc=ss)
            tok_x[sl] = ss.n
            b.wait("te", ss)
            b.wait("te", sv, tok_xtt[sl])
            for dc in range(ND):
                b.wait("te", sv, tok_ptr[0])
                b.op("te", lambda te, sl=sl, dc=dc: te.transpose(
                    ptr0[:, :],
                    lnst[:, sl * 1024 + dc * 128:sl * 1024 + (dc + 1) * 128],
                    eye_sb[:, :]), inc=sp)
                b.wait("ve", sp)
                b.op("ve", lambda v, sl=sl, dc=dc: v.tensor_copy(
                    xtt[:, sl * 1024 + dc * 128:sl * 1024 + (dc + 1) * 128],
                    ptr0[:, :]), inc=sv)
                tok_ptr[0] = sv.n
            tok_lnst[sl] = sp.n
            xtt_ok = sv.n
            if t >= 2:
                b.wait("ve", skv, t - 1)
            for dn in range(2):
                b.wait("te", sv, tok_k[dn])
                b.wait("te", sv, xtt_ok)
                for kdc in range(ND):
                    b.op("te", lambda te, dn=dn, kdc=kdc, sl=sl: te.matmul(
                        pk[dn][:, :],
                        xtt[:, sl * 1024 + kdc * 128:sl * 1024 + (kdc + 1) * 128],
                        wk_sb[:, kdc * 1024 + dn * 512:kdc * 1024 + (dn + 1) * 512],
                        start=(kdc == 0), stop=(kdc == ND - 1)),
                        inc=(sp if kdc == ND - 1 else None))
                b.wait("ve", sp)
                b.op("ve", lambda v, dn=dn, sl=sl: v.tensor_single_scalar(
                    phs[:, sl * 1024 + dn * 512:sl * 1024 + (dn + 1) * 512],
                    pk[dn][:, :], 0.0, ALU.min), inc=sv)
                b.wait("sc", sv)
                b.op("sc", lambda s, dn=dn, sl=sl: s.activation(
                    phs[:, sl * 1024 + dn * 512:sl * 1024 + (dn + 1) * 512],
                    phs[:, sl * 1024 + dn * 512:sl * 1024 + (dn + 1) * 512],
                    AF.Exp), inc=ss)
                b.wait("ve", ss)
                b.op("ve", lambda v, dn=dn, sl=sl: v.scalar_tensor_tensor(
                    kf[:, sl * 1024 + dn * 512:sl * 1024 + (dn + 1) * 512],
                    pk[dn][:, :], 0.0,
                    phs[:, sl * 1024 + dn * 512:sl * 1024 + (dn + 1) * 512],
                    ALU.max, ALU.add), inc=sv)
                tok_k[dn] = sv.n
            for dn in range(2):
                b.wait("te", sv, tok_v[dn])
                for kdc in range(ND):
                    b.op("te", lambda te, dn=dn, kdc=kdc, sl=sl: te.matmul(
                        pv[dn][:, :],
                        xtt[:, sl * 1024 + kdc * 128:sl * 1024 + (kdc + 1) * 128],
                        wv_sb[:, kdc * 1024 + dn * 512:kdc * 1024 + (dn + 1) * 512],
                        start=(kdc == 0), stop=(kdc == ND - 1)),
                        inc=(sp if kdc == ND - 1 else None))
                b.wait("ve", sp)
                b.op("ve", lambda v, dn=dn, sl=sl: v.tensor_copy(
                    bass.AP(v65, sl * 1040 + dn * 8 * 65,
                            [[2 * 1040, 128], [65, 8], [1, 64]]),
                    bass.AP(pv[dn], 0, [[512, 128], [64, 8], [1, 64]])),
                    inc=sv)
                tok_v[dn] = sv.n
            b.wait("te", sv)
            for h in range(H):
                bank, slot2 = divmod(h, 7)
                b.op("te", lambda te, h=h, sl=sl, bank=bank, slot2=slot2, t=t:
                     te.matmul(
                         pkv[bank][:, slot2 * 65:(slot2 + 1) * 65],
                         kf[:, sl * 1024 + h * 64:sl * 1024 + (h + 1) * 64],
                         v65[:, sl * 1040 + h * 65:sl * 1040 + (h + 1) * 65],
                         start=(t == 0 and slot2 == 0), stop=(t == 31)),
                     inc=(skv if h == H - 1 else None))
            tok_xtt[sl] = sp.n
        b.wait("ve", skv)
        b.op("ve", lambda v: v.tensor_copy(kvs[:, 0:455], pkv[0][:, :]))
        b.op("ve", lambda v: v.tensor_copy(kvs[:, 455:910], pkv[1][:, :]))
        b.op("ve", lambda v: v.tensor_copy(kvs[:, 910:1040], pkv[2][:, :]),
             inc=sv)
        b.wait("gp", sv)
        b.op_branch("gp", regs["slot"],
                    lambda e: e.dma_start(out=kv_sh[0:64, :], in_=kvs[:, :]),
                    lambda e: e.dma_start(out=kv_sh[64:128, :], in_=kvs[:, :]),
                    dinc=sw)
        b.wait("gp", sw)
    return sw, sw.n


def emit_barrier(b, k, nonce_sb, shf, fbuf, regs, done):
    pol = b.sem(f"bar{k}")
    b.op_branch("gp", regs["slot"],
                lambda e, k=k: e.dma_start(
                    out=shf[0:1, :], in_=nonce_sb[0:1, k * 64:(k + 1) * 64]),
                lambda e, k=k: e.dma_start(
                    out=shf[1:2, :], in_=nonce_sb[0:1, k * 64:(k + 1) * 64]),
                dinc=pol)
    b.wait("gp", pol)
    rk = regs["rk"]
    rf0 = regs["rf0"]
    rf1 = regs["rf1"]
    rgo = regs["rgo"]

    def poll(gp, k=k, pol=pol.h):
        gp.reg_load(rk, nonce_sb[0:1, k * 64:k * 64 + 1])
        gp.reg_mov(rgo, 1)
        with gp.While(rgo):
            gp.dma_start(out=fbuf[0:1, 0:1], in_=shf[0:1, 0:1]).then_inc(
                pol, 16)
            gp.dma_start(out=fbuf[0:1, 1:2], in_=shf[1:2, 0:1]).then_inc(
                pol, 16)
            gp.wait_ge(pol, 48)
            gp.reg_load(rf0, fbuf[0:1, 0:1])
            gp.reg_load(rf1, fbuf[0:1, 1:2])
            gp.reg_alu(rf0, rf0, rk, ALU.not_equal)
            gp.reg_alu(rf1, rf1, rk, ALU.not_equal)
            gp.reg_alu(rgo, rf0, rf1, ALU.add)
    pol.n += 32 * 2
    b.q["gp"].append(poll)
    b.op("gp", lambda e: e.sem_inc(done.h, 1))
    done.n += 1
    return done, done.n


def emit_kv_merge(b, kv_sh, kvb2, barrier_done):
    sa = b.sem("kvm_a")
    sv = b.sem("kvm_v")
    sd = b.sem("kvm_d")
    kvs = b.sbuf("kvm_f", [64, 1040], F32)
    b.wait("gp", *barrier_done)
    b.op("gp", lambda e: e.dma_start(out=kvs[:, :], in_=kv_sh[0:64, :]),
         dinc=sa)
    b.wait("gp", sa)
    b.op("gp", lambda e: e.dma_start(out=kvs[:, :], in_=kv_sh[64:128, :],
                                     accum_op=ALU.add), dinc=sa)
    b.wait("ve", sa)
    b.op("ve", lambda v: v.tensor_copy(kvb2[0:64, :], kvs[:, :]), inc=sv)
    b.wait("sy", sv)
    b.op("sy", lambda e: e.dma_start(out=kvb2[64:128, :], in_=kvb2[0:64, :]),
         dinc=sd)
    return sd, sd.n


def emit_kvpayload_sh(b, tag, g, xt, wkT_sb, wvT_sb, bkt_sb, bvrow_sb,
                      mask_sb, consts, kt_sh, v65_sh, regs, xt_ready,
                      w_ready, mask_ready):
    sp = b.sem(tag + "p")
    sv = b.sem(tag + "v")
    ss = b.sem(tag + "s")
    sd = b.sem(tag + "d")
    eye = consts["c_eye"]
    with ExitStack() as ps:
        ktile = b.sbuf(tag + "_kt", [128, 2 * 512], BF16, stack=ps)
        ksq = b.sbuf(tag + "_ksq", [128, 512], F32, stack=ps)
        kn2 = b.sbuf(tag + "_kn2", [128, 32], F32, stack=ps)
        kn2b = b.sbuf(tag + "_kn2b", [16, 16], BF16, stack=ps)
        v65t = b.sbuf(tag + "_v65", [128, 2 * 1040], BF16, stack=ps)
        pkb = [b.psum(tag + f"_pk{i}", [128, 512], stack=ps)
               for i in range(2)]
        pn = b.psum(tag + "_pn", [2, 512], stack=ps)
        ptr8 = b.psum(tag + "_ptr8", [16, 128], BF16, stack=ps)

        b.wait("te", *xt_ready)
        b.wait("te", *w_ready)
        tok_pkb = [0, 0]
        tok_dma = [0, 0]
        tok_ksq = 0
        tok_pn = 0
        for oc in range(ND):
            for tc in range(4):
                par = (oc * 4 + tc) % 2
                b.wait("te", ss, tok_pkb[par])
                for kic in range(ND):
                    b.op("te", lambda te, oc=oc, tc=tc, kic=kic, par=par:
                         te.matmul(
                             pkb[par][:, :],
                             wkT_sb[:, kic * 1024 + oc * 128:kic * 1024 + (oc + 1) * 128],
                             xt[:, kic * 2048 + tc * 512:kic * 2048 + (tc + 1) * 512],
                             start=(kic == 0), stop=(kic == ND - 1)),
                         inc=(sp if kic == ND - 1 else None))
                b.wait("sc", sp)
                b.wait("sc", sd, tok_dma[par])
                b.op("sc", lambda s, oc=oc, par=par: s.activation(
                    ktile[:, par * 512:(par + 1) * 512], pkb[par][:, :],
                    AF.Identity, bias=bkt_sb[:, oc:oc + 1]), inc=ss)
                tok_pkb[par] = ss.n
                b.wait("gp", ss)
                b.op_branch(
                    "gp", regs["slot"],
                    lambda e, oc=oc, tc=tc, par=par: e.dma_start(
                        out=kt_sh[oc * 128:(oc + 1) * 128,
                                  g * 2048 + tc * 512:g * 2048 + (tc + 1) * 512],
                        in_=ktile[:, par * 512:(par + 1) * 512]),
                    lambda e, oc=oc, tc=tc, par=par: e.dma_start(
                        out=kt_sh[KT_ROWS + oc * 128:KT_ROWS + (oc + 1) * 128,
                                  g * 2048 + tc * 512:g * 2048 + (tc + 1) * 512],
                        in_=ktile[:, par * 512:(par + 1) * 512]), dinc=sd)
                tok_dma[par] = sd.n
                b.wait("sc", sp, tok_ksq)
                b.opf("sc", lambda s, par=par: s.activation(
                    ksq[:, :], ktile[:, par * 512:(par + 1) * 512],
                    AF.Square))
                b.wait("te", b.selfsem["sc"])
                b.wait("te", b.selfsem["ve"], tok_pn)
                b.op("te", lambda te: te.matmul(
                    pn[:, :], consts["c_hm2"][:, :], ksq[:, :],
                    start=True, stop=True), inc=sp)
                tok_ksq = sp.n
                b.wait("ve", sp)
                if tc == 0:
                    b.opf("ve", lambda v, oc=oc: v.reduce_max(
                        kn2[0:2, oc:oc + 1], pn[:, :], AX.X))
                else:
                    b.opf("ve", lambda v: v.reduce_max(kn2[0:2, 16:17],
                                                       pn[:, :], AX.X))
                    b.opf("ve", lambda v, oc=oc: v.tensor_max(
                        kn2[0:2, oc:oc + 1], kn2[0:2, oc:oc + 1],
                        kn2[0:2, 16:17]))
                tok_pn = b.selfsem["ve"].n
        b.opf("ve", lambda v: v.tensor_copy(kn2b[0:2, 0:8], kn2[0:2, 0:8]))
        b.wait("te", b.selfsem["ve"])
        b.op("te", lambda te: te.transpose(ptr8[0:8, 0:2], kn2b[0:2, 0:8],
                                           eye[0:2, 0:2]), inc=sp)
        b.wait("ve", sp)
        b.opf("ve", lambda v: v.tensor_copy(kn2[0:8, 20:22], ptr8[0:8, 0:2]))
        b.opf("ve", lambda v: v.reduce_max(kn2[0:8, 22:23], kn2[0:8, 20:22],
                                           AX.X))
        b.opf("ve", lambda v: v.tensor_copy(kn2b[0:8, 8:9], kn2[0:8, 22:23]))
        b.wait("gp", b.selfsem["ve"])
        b.op_branch(
            "gp", regs["slot"],
            lambda e: e.dma_start(
                out=kt_sh[1024:1025, g * 2048:g * 2048 + 8]
                .rearrange("a (c u) -> (a c) u", u=1),
                in_=kn2b[0:8, 8:9]),
            lambda e: e.dma_start(
                out=kt_sh[KT_ROWS + 1024:KT_ROWS + 1025, g * 2048:g * 2048 + 8]
                .rearrange("a (c u) -> (a c) u", u=1),
                in_=kn2b[0:8, 8:9]), dinc=sd)
        tok_pv = [0, 0]
        tok_v65dma = [0, 0]
        b.wait("te", ss)
        for t in range(NT):
            for dn in range(2):
                par = dn
                b.wait("te", sv, tok_pv[par])
                for kic in range(ND):
                    b.op("te", lambda te, t=t, dn=dn, kic=kic, par=par:
                         te.matmul(
                             pkb[par][:, :],
                             xt[:, kic * 2048 + t * 128:kic * 2048 + (t + 1) * 128],
                             wvT_sb[:, kic * 1024 + dn * 512:kic * 1024 + (dn + 1) * 512],
                             start=(kic == 0), stop=False))
                b.op("te", lambda te, dn=dn, par=par: te.matmul(
                    pkb[par][:, :], consts["c_ones"][0:1, 0:128],
                    bvrow_sb[0:1, dn * 512:(dn + 1) * 512],
                    start=False, stop=True), inc=sp)
                b.wait("ve", sp)
                sl = t % 2
                b.wait("ve", sd, tok_v65dma[sl])
                b.op("ve", lambda v, t=t, dn=dn, par=par, sl=sl:
                     v.tensor_single_scalar(
                         bass.AP(v65t, sl * 1040 + dn * 8 * 65,
                                 [[2 * 1040, 128], [65, 8], [1, 64]]),
                         bass.AP(pkb[par], 0, [[512, 128], [64, 8], [1, 64]]),
                         mask_sb[:, t:t + 1], ALU.mult), inc=sv)
                tok_pv[par] = sv.n
                b.op("ve", lambda v, t=t, dn=dn, sl=sl: v.tensor_copy(
                    bass.AP(v65t, sl * 1040 + dn * 8 * 65 + 64,
                            [[2 * 1040, 128], [65, 8]]),
                    mask_sb[:, t:t + 1].to_broadcast([128, 8])), inc=sv)
            b.wait("gp", sv)
            b.op_branch(
                "gp", regs["slot"],
                lambda e, t=t, sl=t % 2: e.dma_start(
                    out=v65_sh[g * 2048 + t * 128:g * 2048 + (t + 1) * 128, :],
                    in_=v65t[:, sl * 1040:sl * 1040 + 1040]),
                lambda e, t=t, sl=t % 2: e.dma_start(
                    out=v65_sh[4096 + g * 2048 + t * 128:
                               4096 + g * 2048 + (t + 1) * 128, :],
                    in_=v65t[:, sl * 1040:sl * 1040 + 1040]), dinc=sd)
            tok_v65dma[t % 2] = sd.n
        b.wait("gp", sd)
    return sd, sd.n


def build_fused():
    nc = bass.Bass("TRN2", target_bir_lowering=False)
    with ExitStack() as stack:
        b = B(nc, stack)
        dram = {}
        for nm, shape, dt in [
                ("x", [4096, D], F32), ("slotv", [1, 1], I32),
                ("nonces", [1, 128], I32),
                ("wk", [D, D], F32), ("wv", [D, D], F32),
                ("wq", [D, D], F32), ("wo", [D, D], F32),
                ("gwt", [D, 16], F32),
                ("ws_in", [D, D], F32), ("vs_in", [D, D], F32),
                ("ws_out", [D, D], F32),
                ("e_win", [NE, D, EH], F32), ("e_vin", [NE, D, EH], F32),
                ("e_wout", [NE, EH, D], F32),
                ("wkT", [D, D], F32), ("wvT", [D, D], F32),
                ("bk_t", [128, 8], F32), ("bv_row", [1, D], F32),
                ("wqT", [D, D], F32), ("woT", [D, D], F32),
                ("bq_t", [128, 8], F32), ("bo_row", [1, D], F32)]:
            dram[nm] = nc.dram_tensor(nm, shape, dt,
                                      kind="ExternalInput").ap()
        out_d = nc.dram_tensor("out", [TC, D], F32,
                               kind="ExternalOutput").ap()
        kt_sh = nc.dram_tensor("kt_sh", [2 * KT_ROWS, 4096], BF16,
                               kind="Internal", addr_space="Shared")
        v65_sh = nc.dram_tensor("v65_sh", [8192, 1040], BF16,
                                kind="Internal", addr_space="Shared")
        kv_sh = nc.dram_tensor("kv_sh", [128, 1040], F32,
                               kind="Internal", addr_space="Shared")
        shf = nc.dram_tensor("shf", [2, 64], I32,
                             kind="Internal", addr_space="Shared")

        regs = {nm: nc.gpsimd.alloc_register("r_" + nm)
                for nm in ("slot", "rb_kt", "rb_v", "rb_kv", "rb_f",
                           "rtmp", "rtmp2", "rk", "rf0", "rf1", "rgo")}
        consts = decl_consts(b, ["c_eye", "c_misc", "c_onesf", "c_erep",
                                 "c_hm2", "c_ones"])
        x_sb = b.sbuf("x_sbuf", [128, NT * 1024], F32)
        xt = b.sbuf("xt_sbuf", [128, ND * 2048], BF16)
        wb = b.sbuf("wb_sbuf", [128, 16384], BF16)
        kvb2 = b.sbuf("kvb2_sb", [128, 1040], BF16)
        mask_sb = b.sbuf("mask_sb", [128, 16], F32)
        gwt_sb = b.sbuf("gwt_sb", [128, 128], BF16)
        bkt_sb = b.sbuf("bkt_sb", [128, 8], F32)
        bvrow_sb = b.sbuf("bvrow_sb", [1, 1024], BF16)
        bqt_sb = b.sbuf("bqt_sb", [128, 8], F32)
        borow_sb = b.sbuf("borow_sb", [1, 1024], BF16)
        slot_sb = b.sbuf("slot_sb", [1, 1], I32)
        nonce_sb = b.sbuf("nonce_sb", [1, 128], I32)
        fbuf = b.sbuf("fbuf", [1, 64], I32)

        sld = b.sem("s_ld")
        sw = b.sem("s_w")
        b.op("sy", lambda e: e.dma_start(out=slot_sb[:, :],
                                         in_=dram["slotv"]), dinc=sld)
        b.op("sy", lambda e: e.dma_start(out=nonce_sb[:, :],
                                         in_=dram["nonces"]), dinc=sld)
        b.wait("gp", sld)
        b.op("gp", lambda e: e.reg_load(regs["slot"], slot_sb[0:1, 0:1]))
        load_weight(b, dram["wk"], wb[:, 0:8192], 8, sw)
        load_weight(b, dram["wv"], wb[:, 8192:16384], 8, sw)
        load_weight(b, dram["gwt"], gwt_sb, 8, sw)
        b.op("gp", lambda e: e.dma_start(out=bkt_sb[:, :],
                                         in_=dram["bk_t"]), dinc=sw)
        b.op("gp", lambda e: e.dma_start(out=bvrow_sb[:, :],
                                         in_=dram["bv_row"]), dinc=sw)
        b.op("gp", lambda e: e.dma_start(out=bqt_sb[:, :],
                                         in_=dram["bq_t"]), dinc=sw)
        b.op("gp", lambda e: e.dma_start(out=borow_sb[:, :],
                                         in_=dram["bo_row"]), dinc=sw)
        wkv_ld = sw.n

        pA_done = emit_passA(b, dram["x"], wb[:, 0:8192], wb[:, 8192:16384],
                             consts["c_eye"], consts["c_misc"], (sw, wkv_ld),
                             kv_sh, regs)
        bar1 = b.sem("bar_done")
        emit_barrier(b, 0, nonce_sb, shf, fbuf, regs, bar1)
        kv_ready = emit_kv_merge(b, kv_sh, kvb2, (bar1, bar1.n))

        dram_w = {"ws_in": dram["ws_in"], "vs_in": dram["vs_in"],
                  "ws_out": dram["ws_out"],
                  "e_win": [dram["e_win"][e] for e in range(NE)],
                  "e_vin": [dram["e_vin"][e] for e in range(NE)],
                  "e_wout": [dram["e_wout"][e] for e in range(NE)]}

        prev_phase_done = pA_done
        prev_x_free = (sld, 0)
        for g in (1, 0):
            tg = f"g{g}"
            b.wait("gp", *prev_phase_done)
            load_weight(b, dram["wq"], wb[:, 0:8192], 8, sw)
            load_weight(b, dram["wo"], wb[:, 8192:16384], 8, sw)
            wqwo_ld = sw.n
            b.wait("sy", *prev_x_free)
            b.op("sy", lambda e, g=g: e.dma_start(
                out=x_sb[:, :].rearrange("p (t d) -> p t d", t=NT),
                in_=dram["x"][g * 2048:(g + 1) * 2048, :]
                .rearrange("(t p) d -> p t d", p=128)), dinc=sld)
            x_ok = (sld, sld.n)
            b.wait("ve", *prev_phase_done)
            b.wait("te", *prev_phase_done)
            b.wait("sc", *prev_phase_done)
            b.wait("sy", *prev_phase_done)
            xt_ready = emit_ln(b, "l1" + tg, x_sb, xt, consts["c_eye"],
                               consts["c_misc"], x_ok)
            x1_done = emit_attn_a_main(b, xt, x_sb, wb[:, 0:8192],
                                       wb[:, 8192:16384], kvb2,
                                       consts["c_onesf"], consts["c_ones"],
                                       xt_ready, kv_ready,
                                       (sw, wqwo_ld), (sw, wqwo_ld),
                                       tag="a" + tg)
            xt2_ready = emit_ln(b, "l2" + tg, x_sb, xt, consts["c_eye"],
                                consts["c_misc"], x1_done)
            x2_done = emit_ffn(b, "f" + tg, xt, x_sb, wb, gwt_sb, dram_w,
                               consts, xt2_ready, (sw, sw.n), x1_done)
            b.wait("gp", *x2_done)
            load_weight(b, dram["wkT"], wb[:, 0:8192], 8, sw)
            load_weight(b, dram["wvT"], wb[:, 8192:16384], 8, sw)
            kw_ld = sw.n
            xt3_ready = emit_ln(b, "l3" + tg, x_sb, xt, consts["c_eye"],
                                consts["c_misc"], x2_done, mask_sb=mask_sb)
            mask_ready = (b.sems["l3" + tg + "v"],
                          b.sems["l3" + tg + "v"].n)
            kp_done = emit_kvpayload_sh(b, "kp" + tg, g, xt,
                                        wb[:, 0:8192], wb[:, 8192:16384],
                                        bkt_sb, bvrow_sb, mask_sb, consts,
                                        kt_sh, v65_sh, regs, xt3_ready,
                                        (sw, kw_ld), mask_ready)
            prev_phase_done = kp_done
            prev_x_free = (b.sems["l3" + tg + "s"],
                           b.sems["l3" + tg + "s"].n)

        bar2 = b.sem("bar2_done")
        emit_barrier(b, 1, nonce_sb, shf, fbuf, regs, bar2)
        b.wait("sy", bar2, bar2.n)
        b.wait("gp", bar2, bar2.n)
        b.wait("te", bar2, bar2.n)
        b.wait("gp", *prev_phase_done)
        b.wait("ve", *prev_phase_done)
        b.wait("te", *prev_phase_done)
        b.wait("sc", *prev_phase_done)
        load_weight(b, dram["wqT"], wb[:, 0:8192], 8, sw)
        load_weight(b, dram["woT"], wb[:, 8192:16384], 8, sw)
        wfl_ld = sw.n
        x3_done = emit_attn_b(b, xt, x_sb, wb, bqt_sb, borow_sb,
                              kt_sh, v65_sh, consts,
                              (b.sems["l3g0e"], b.sems["l3g0e"].n),
                              (sw, wfl_ld), (sw, wfl_ld),
                              kt_rows=KT_ROWS, nshard=2)
        xt4_ready = emit_ln(b, "l4", x_sb, xt, consts["c_eye"],
                            consts["c_misc"], x3_done)
        out_done = emit_ffn(b, "f2", xt, x_sb, wb, gwt_sb, dram_w, consts,
                            xt4_ready, (sw, sw.n), x3_done)
        sst = b.sem("s_out")
        b.wait("sy", *out_done)
        b.op("sy", lambda e: e.dma_start(
            out=out_d.rearrange("(t p) d -> p t d", p=128),
            in_=x_sb[:, :].rearrange("p (t d) -> p t d", t=NT)), dinc=sst)
        b.wait("sy", sst)
        b.wait("gp", sst)

        block = stack.enter_context(nc.Block())
        b.play(block)
    return nc


def _fused_inmaps(inputs):
    consts = _host_consts()
    f32 = lambda k: np.asarray(inputs[k], np.float32)
    T = lambda a: np.ascontiguousarray(np.asarray(a, np.float32).T)
    rng_nonce = np.random.default_rng()
    nonces = rng_nonce.integers(1, 2**31 - 1, size=(1, 2),
                                dtype=np.int32)
    nonrow = np.zeros((1, 128), np.int32)
    nonrow[0, 0:64] = nonces[0, 0]
    nonrow[0, 64:128] = nonces[0, 1]
    shared = {
        "wk": f32("L0_attn_a_Wk"), "wv": f32("L0_attn_a_Wv"),
        "wq": f32("L0_attn_a_Wq"), "wo": f32("L0_attn_a_Wo"),
        "gwt": T(inputs["L0_ffn_gate_w"]),
        "ws_in": f32("L0_ffn_ws_in"), "vs_in": f32("L0_ffn_vs_in"),
        "ws_out": f32("L0_ffn_ws_out"),
        "e_win": f32("L0_ffn_e_win"), "e_vin": f32("L0_ffn_e_vin"),
        "e_wout": f32("L0_ffn_e_wout"),
        "wkT": T(inputs["L0_attn_b_wk_w"]), "wvT": T(inputs["L0_attn_b_wv_w"]),
        "bk_t": np.ascontiguousarray(
            np.asarray(inputs["L0_attn_b_wk_b"], np.float32).reshape(8, 128).T),
        "bv_row": np.asarray(inputs["L0_attn_b_wv_b"],
                             np.float32).reshape(1, D),
        "wqT": T(inputs["L0_attn_b_wq_w"]), "woT": T(inputs["L0_attn_b_wo_w"]),
        "bq_t": np.ascontiguousarray(
            np.asarray(inputs["L0_attn_b_wq_b"], np.float32).reshape(8, 128).T),
        "bo_row": np.asarray(inputs["L0_attn_b_wo_b"],
                             np.float32).reshape(1, D),
        "nonces": nonrow,
    }
    shared.update({k: consts[k] for k in ("c_eye", "c_misc", "c_onesf",
                                          "c_erep", "c_hm2", "c_ones")})
    x = np.asarray(inputs["L0_attn_a_x"], np.float32).reshape(2 * 8192, D)
    maps = []
    for c in range(8):
        bb = c // 4
        s = c % 2
        wide = x[bb * 8192 + s * 4096: bb * 8192 + (s + 1) * 4096]
        if c % 4 >= 2:
            xw = np.concatenate([wide[2048:4096], wide[0:2048]], axis=0)
        else:
            xw = wide
        m = dict(shared)
        m["x"] = np.ascontiguousarray(xw)
        m["slotv"] = np.array([[s]], np.int32)
        maps.append(m)
    return maps


def _fused_out_ranges():
    out = []
    for c in range(8):
        bb = c // 4
        s = c % 2
        if c % 4 >= 2:
            lo = s * 4096 + 2048
        else:
            lo = s * 4096
        out.append((bb, lo))
    return out


def kernel(**inputs):
    nc = build_fused()
    res = run_bass_kernel_spmd(nc, _fused_inmaps(inputs),
                               core_ids=list(range(8)))
    out = np.zeros((2, 8192, D), np.float32)
    for c, (bb, lo) in enumerate(_fused_out_ranges()):
        out[bb, lo:lo + 2048] = np.asarray(res.results[c]["out"], np.float32)
    return out
